# version 07
# baseline (speedup 1.0000x reference)
import numpy as np
import ml_dtypes
from contextlib import ExitStack

import concourse.bass as bass
import concourse.mybir as mybir
from concourse.bass_utils import run_bass_kernel_spmd

F32 = mybir.dt.float32
BF16 = mybir.dt.bfloat16
I16 = mybir.dt.int16
I32 = mybir.dt.int32
ALU = mybir.AluOpType
ACT = mybir.ActivationFunctionType
AX = mybir.AxisListType

MAGIC = 12582912.0
T = 2048
NT = 16
D = 1024
ND = 8
HF = 3072
NH = 24
HEADS = 16
DH = 64
K0 = 256
EPS = 1e-5


SEM_ALIAS = {
    "f1z2": "f1z", "f1z3": "f1z",
    "f1d2": "f1d", "f1d3": "f1d", "f1d4": "f1d",
    "f1sg": "f1s", "f1sh": "f1s", "f1p": "f1s",
    "f1q1": "f1q", "f1q2": "f1q",
    "ro0": "riu0", "ro1": "riu1",
    "a5sq": "lnsq", "a5nm": "l0nm", "a5nr": "l0nx", "a5ns": "l0bs",
    "a5nb": "l0bv", "a5bb": "l0bb", "a5bc": "l0bc", "a5sc": "l0s",
    "a5e": "l0e", "a5pv": "l0pv", "a5ve": "l0m", "a5mm": "lrmm",
    "a5ev": "l0r", "a5w": "dcl", "a5s": "xn", "a5v": "xn", "a5r": "xn",
    "a5q": "lnsr", "a5o": "gpi",
}


class Ctx:
    def __init__(self, nc, es):
        self.nc = nc
        self.es = es
        self._sems = {}
        self.cnt = {}
        self.block = es.enter_context(nc.Block())

    def sem(self, name):
        name = SEM_ALIAS.get(name, name)
        if name not in self._sems:
            self._sems[name] = self.es.enter_context(self.nc.semaphore(name))
            self.cnt[name] = 0
        return self._sems[name]

    def inc(self, instr, name, n=1):
        name = SEM_ALIAS.get(name, name)
        instr.then_inc(self.sem(name), n)
        self.cnt[name] += n
        return instr

    def c(self, name):
        name = SEM_ALIAS.get(name, name)
        self.sem(name)
        return self.cnt[name]

    def wait(self, eng, name, val=None):
        name = SEM_ALIAS.get(name, name)
        s = self.sem(name)
        v = self.cnt[name] if val is None else val
        if v > 0:
            eng.wait_ge(s, v)


_COMPUTE_OPS = (
    "tensor_copy", "tensor_tensor", "tensor_add", "tensor_sub", "tensor_mul",
    "tensor_max", "tensor_scalar", "tensor_single_scalar", "tensor_scalar_add",
    "tensor_scalar_sub", "tensor_scalar_mul", "tensor_scalar_max",
    "tensor_scalar_min", "tensor_relu", "tensor_reduce", "reduce_max",
    "reduce_sum", "scalar_tensor_tensor", "tensor_tensor_scan", "reciprocal",
    "select", "copy_predicated", "memset", "memzero", "activation", "add",
    "mul", "copy", "sqrt", "square", "sign", "affine_select", "iota",
    "bn_stats", "bn_aggr", "transpose", "stream_shuffle", "pool", "pool_avg",
    "pool_max", "tensor_tensor_reduce", "affine_mul_reduce",
)


class _NoDrain:

    def __init__(self, proxy):
        self._p = proxy

    def __getattr__(self, name):
        attr = getattr(self._p._eng, name)
        if name in _COMPUTE_OPS:
            def wrapped(*a, **kw):
                self._p._dirty = True
                return attr(*a, **kw)
            return wrapped
        return attr


class EngProxy:

    def __init__(self, eng):
        self._eng = eng
        self._dirty = False
        self.nd = _NoDrain(self)

    def __getattr__(self, name):
        attr = getattr(self._eng, name)
        if name in _COMPUTE_OPS:
            def wrapped(*a, **kw):
                if self._dirty:
                    self._eng.drain()
                self._dirty = True
                return attr(*a, **kw)
            return wrapped
        return attr


def build_nc(debug_taps=(), cc_world=8):
    nc = bass.Bass("TRN2", target_bir_lowering=False, debug=False)

    d_x = nc.dram_tensor("x", [T, D], F32, kind="ExternalInput").ap()
    d_w0g = nc.dram_tensor("w0g", [128, NH, 8, 128], I32, kind="ExternalInput").ap()
    d_w0u = nc.dram_tensor("w0u", [128, NH, 8, 128], I32, kind="ExternalInput").ap()
    d_w0d = nc.dram_tensor("w0d", [128, 8, NH, 128], I32, kind="ExternalInput").ap()
    d_cl = nc.dram_tensor("cl", [128, 8, K0], F32, kind="ExternalInput").ap()
    d_cr = nc.dram_tensor("cr", [128, 8, K0], F32, kind="ExternalInput").ap()
    d_thr = nc.dram_tensor("thr", [1], F32, kind="ExternalInput").ap()
    d_gb = nc.dram_tensor("gb", [1], F32, kind="ExternalInput").ap()
    d_w1q = nc.dram_tensor("w1q", [128, 8, 8, 128], F32, kind="ExternalInput").ap()
    d_w1k = nc.dram_tensor("w1k", [128, 8, 8, 128], F32, kind="ExternalInput").ap()
    d_w1v = nc.dram_tensor("w1v", [128, 8, 8, 128], F32, kind="ExternalInput").ap()
    d_bq = nc.dram_tensor("bq", [128, 8], F32, kind="ExternalInput").ap()
    d_bk = nc.dram_tensor("bk", [128, 8], F32, kind="ExternalInput").ap()
    d_bv = nc.dram_tensor("bv", [128, 8], F32, kind="ExternalInput").ap()
    d_l1g = nc.dram_tensor("l1g", [128, 8], F32, kind="ExternalInput").ap()
    d_l1b = nc.dram_tensor("l1b", [128, 8], F32, kind="ExternalInput").ap()
    d_w1fg = nc.dram_tensor("w1fg", [HF, D], F32, kind="ExternalInput").ap()
    d_w1fu = nc.dram_tensor("w1fu", [HF, D], F32, kind="ExternalInput").ap()
    d_w1fd = nc.dram_tensor("w1fd", [D, HF], F32, kind="ExternalInput").ap()
    d_out = nc.dram_tensor("out", [T, D], F32, kind="ExternalOutput").ap()
    dbg_outs = {t: nc.dram_tensor(f"dbg_{t}", [T, D], F32, kind="ExternalOutput").ap()
                for t in debug_taps}

    s_w0g = nc.dram_tensor("s_w0g", [128, NH, 8, 128], BF16).ap()
    s_w0u = nc.dram_tensor("s_w0u", [128, NH, 8, 128], BF16).ap()
    s_w0d = nc.dram_tensor("s_w0d", [128, 8, NH, 128], BF16).ap()
    s_w1g = nc.dram_tensor("s_w1g", [128, NH, 8, 128], BF16).ap()
    s_w1u = nc.dram_tensor("s_w1u", [128, NH, 8, 128], BF16).ap()
    s_w1d = nc.dram_tensor("s_w1d", [128, 8, NH, 128], BF16).ap()
    s_up = nc.dram_tensor("s_up", [NH, 128, T], F32).ap()
    s_gate = nc.dram_tensor("s_gate", [NH, 128, T], F32).ap()
    s_ui = nc.dram_tensor("s_ui", [NH, 128, T], I16).ap()
    s_gi = nc.dram_tensor("s_gi", [NH, 128, T], I16).ap()
    s_o32 = nc.dram_tensor("s_o32", [NH, 128, T], I32).ap()
    cc_in = nc.dram_tensor("cc_in", [8], F32).ap()
    cc_out = nc.dram_tensor("cc_out", [8], F32).ap()
    d_sc = nc.dram_tensor("d_sc", [16], F32).ap()

    with ExitStack() as es:
        cx = Ctx(nc, es)
        rawblock = cx.block
        _prox = {}

        class _BlockWrap:
            def _mk(self, name):
                dec = getattr(rawblock, name)
                def deco(fn):
                    def inner(eng):
                        key = name
                        if key not in _prox:
                            _prox[key] = EngProxy(eng) if name in ("vector", "scalar", "gpsimd") else eng
                        return fn(_prox[key])
                    return dec(inner)
                return deco

            @property
            def vector(self):
                return self._mk("vector")

            @property
            def scalar(self):
                return self._mk("scalar")

            @property
            def gpsimd(self):
                return self._mk("gpsimd")

            @property
            def tensor(self):
                return rawblock.tensor

            @property
            def sync(self):
                return rawblock.sync

        block = _BlockWrap()

        def sbuf(name, shape, dt=F32):
            return es.enter_context(nc.sbuf_tensor(name, shape, dt))

        def salloc(name, shape, dt=F32):
            cm = nc.sbuf_tensor(name, shape, dt)
            return cm, cm.__enter__()

        def sfree(*cms):
            for cm in reversed(cms):
                cm.__exit__(None, None, None)

        def psum(name, shape, dt=F32):
            return es.enter_context(nc.psum_tensor(name, shape, dt))

        x_tm = sbuf("x_tm", [128, NT, D])
        ident = sbuf("ident", [128, 128])
        ones_bf = sbuf("ones_bf", [128, 1], BF16)
        cb = sbuf("cb", [128, 16])
        onesrow = sbuf("onesrow", [1, 128])
        stats = sbuf("stats", [128, 6 * NT])
        scal = sbuf("scal", [128, 24])
        vsml = sbuf("vsml", [128, 16])
        hsel = sbuf("hsel", [128, 8, 16], BF16)
        bqkv = sbuf("bqkv", [128, 24])
        gbet = sbuf("gbet", [128, 16])
        thr_sb = sbuf("thr_sb", [128, 2])
        xn_f = sbuf("xn_f", [128, 2, D])
        A1 = sbuf("A1", [128, 8, T], BF16)

        CB_HALF, CB_MAGIC, CB_I127, CB_ID, CB_EPS, CB_I7, CB_E12, CB_65536, \
            CB_65535, CB_I65536, CB_QM, CB_M05127, CB_I32R, CB_I8R = range(14)
        CB_VALS = [0.5, MAGIC, 1.0 / 127.0, 1.0 / 1024.0, EPS, 1.0 / 7.0, 1e-12,
                   65536.0, 65535.0, 1.0 / 65536.0, 1.0 / 32767.0, 0.5 / 127.0,
                   1.0 / 32.0, 0.125]

        def cbv(i):
            return cb[:, i:i + 1]

        pall = psum("pall", [128, 8, 512])
        pbank = [pall[:, i, :] for i in range(8)]

        @block.gpsimd
        def _(gp):
            for i, v in enumerate(CB_VALS):
                gp.memset(cb[:, i:i + 1], v)
            cx.inc(gp.memset(ident[:], 0.0), "gpi")
            gp.memset(ones_bf[:], 1.0)
            gp.memset(onesrow[:], 1.0)
            cx.inc(gp.memset(hsel[:], 0.0), "gpi")
            cx.wait(gp, "gpi")
            gp.affine_select(out=ident[:], in_=ident[:], compare_op=ALU.not_equal,
                             fill=1.0, base=0, pattern=[[-1, 128]], channel_multiplier=1)
            for oc in range(8):
                gp.memset(hsel[0:64, oc, 2 * oc:2 * oc + 1], 1.0)
                last = gp.memset(hsel[64:128, oc, 2 * oc + 1:2 * oc + 2], 1.0)
            cx.inc(last, "gp")

        @block.sync
        def _(sy):
            cx.inc(sy.dma_start(
                out=x_tm[:], in_=d_x.rearrange("(t p) d -> p t d", p=128)), "dmx", 16)
            cx.inc(sy.dma_start(out=bqkv[:, 0:8], in_=d_bq), "dsm", 16)
            cx.inc(sy.dma_start(out=bqkv[:, 8:16], in_=d_bk), "dsm", 16)
            cx.inc(sy.dma_start(out=bqkv[:, 16:24], in_=d_bv), "dsm", 16)
            cx.inc(sy.dma_start(out=gbet[:, 0:8], in_=d_l1g), "dsm", 16)
            cx.inc(sy.dma_start(out=gbet[:, 8:16], in_=d_l1b), "dsm", 16)
            cx.inc(sy.dma_start(out=thr_sb[:, 0:1],
                                in_=d_thr[None, :].partition_broadcast(128)), "dsm", 16)
            cx.inc(sy.dma_start(out=thr_sb[:, 1:2],
                                in_=d_gb[None, :].partition_broadcast(128)), "dsm", 16)

        @block.vector
        def _(ve):
            cx.wait(ve, "dsm")
            cx.inc(ve.tensor_scalar(thr_sb[:, 0:1], thr_sb[:, 0:1], 32.0, None,
                                    ALU.mult), "ve")

        def emit_transposes(jobs, te_pre_wait=None, ve_pre_wait=None):
            tb = cx.c("tpT")
            shp = [(j[0].shape[0], j[0].shape[1]) for j in jobs]

            @block.tensor
            def _(te):
                if te_pre_wait is not None:
                    te_pre_wait(te)
                for i in range(len(jobs)):
                    pc, fc = shp[i]
                    cx.wait(te, "tpE", tb + i - 1)
                    cx.inc(te.transpose(
                        pbank[6 + (tb + i) % 2][0:fc, 0:pc],
                        jobs[i][0], ident[0:pc, 0:pc]), "tpT")

            @block.vector
            def _(ve):
                if ve_pre_wait is not None:
                    ve_pre_wait(ve)
                for i in range(len(jobs)):
                    pc, fc = shp[i]
                    cx.wait(ve, "tpT", tb + i + 1)
                    cx.inc(jobs[i][1](ve, pbank[6 + (tb + i) % 2][0:fc, 0:pc]),
                           "tpE")

        def fence_entry():
            snap = {k: v for k, v in cx.cnt.items() if v > 0}

            @block.sync
            def _(sy):
                for k, v in snap.items():
                    cx.wait(sy, k, v)

            @block.gpsimd
            def _(gp):
                for k, v in snap.items():
                    cx.wait(gp, k, v)

        with nc.sbuf_tensor("w0buf", [128, NH * 8 * 128], BF16) as w0buf:
            @block.gpsimd
            def _(gp):
                for (src, dst) in [(d_w0g, s_w0g), (d_w0u, s_w0u), (d_w0d, s_w0d)]:
                    cx.inc(gp.dma_start(out=w0buf[:],
                                        in_=src.rearrange("p a b c -> p (a b c)")),
                           "dwc", 16)
                    cx.wait(gp, "dwc")
                    cx.inc(gp.dma_start(out=dst.rearrange("p a b c -> p (a b c)"),
                                        in_=w0buf[:]), "dwc", 16)
                    cx.wait(gp, "dwc")

        cm_dq = [salloc("wnat", [128, 2, 1024], F32),
                 salloc("wdq", [128, 2, 1024], F32),
                 salloc("wblk", [128, 64], F32),
                 salloc("wstg2", [128, 64], F32),
                 salloc("wstg", [128, 8, 128], BF16)]
        wnat, wdq, wblk, wstg2, wstg = [t for _, t in cm_dq]

        dq_units = []
        dq_slab = [0]
        dq_done = {}
        dq_tpt = {}

        def dequant_units(src, dst, nslab, icols):
            nch = icols // 1024
            for sl in range(nslab):
                for cc in range(nch):
                    g = dq_slab[0]
                    dq_slab[0] += 1
                    bg = g % 2
                    use_gp = (g % 4 == 3)

                    def unit(src=src, dst=dst, sl=sl, cc=cc, g=g, bg=bg,
                             use_gp=use_gp, nch=nch):
                        nblk = 32

                        @block.sync
                        def _(sy):
                            cx.wait(sy, "dwc")
                            if g - 2 in dq_done:
                                cx.wait(sy, *dq_done[g - 2])
                            cx.inc(sy.dma_start(
                                out=wnat[:, bg, :],
                                in_=src[sl * 128:(sl + 1) * 128,
                                        cc * 1024:(cc + 1) * 1024]),
                                f"dqL{bg}", 16)

                        @block.vector
                        def _(ve):
                            cx.wait(ve, "gp", 1)
                            cx.wait(ve, f"dqL{bg}")
                            if g - 2 in dq_done:
                                cx.wait(ve, *dq_done[g - 2])
                            w3 = wnat[:, bg, :].rearrange(
                                "p (b k) -> p b k", k=32)
                            bm = wblk[:, 32 * bg:32 * bg + nblk]
                            ve.nd.tensor_reduce(bm[:, :, None], w3, AX.X,
                                                ALU.max,
                                                apply_absolute_value=True)
                            ve.tensor_scalar(bm, bm, cbv(CB_I7), cbv(CB_E12),
                                             ALU.mult, ALU.max)
                            cx.inc(ve.reciprocal(
                                wstg2[:, 32 * bg:32 * bg + nblk], bm), "dqm")

                        engsec = block.gpsimd if use_gp else block.vector
                        dqm_tgt = cx.c("dqm")
                        dq_sem = "dqw" if use_gp else "dqv"

                        @engsec
                        def _(eng, dqm_tgt=dqm_tgt, dq_sem=dq_sem):
                            cx.wait(eng, "dqm", dqm_tgt)
                            if g - 2 in dq_tpt:
                                cx.wait(eng, "tpT", dq_tpt[g - 2])
                            w3 = wnat[:, bg, :].rearrange(
                                "p (b k) -> p b k", k=32)
                            bm = wblk[:, 32 * bg:32 * bg + nblk]
                            w3d = wdq[:, bg, :].rearrange("p (b k) -> p b k",
                                                          k=32)
                            eng.tensor_tensor(
                                w3d, w3,
                                wstg2[:, 32 * bg:32 * bg + nblk, None]
                                .broadcast_to([128, nblk, 32]), ALU.mult)
                            eng.tensor_scalar(wdq[:, bg, :], wdq[:, bg, :],
                                              cbv(CB_MAGIC), cbv(CB_MAGIC),
                                              ALU.add, ALU.subtract)
                            cx.inc(eng.tensor_tensor(
                                w3d, w3d,
                                bm[:, :, None].broadcast_to([128, nblk, 32]),
                                ALU.mult), dq_sem)
                        dq_done[g] = (dq_sem, cx.c(dq_sem))

                        def mk_evac(c):
                            def f(ve, bank):
                                return ve.nd.tensor_copy(wstg[:, c, :], bank)
                            return f

                        dqv_target = cx.c(dq_sem)

                        vres_now = cx.c("vres")
                        vh_now = cx.c("vh")

                        def prew(te, tgt=dqv_target, dq_sem=dq_sem,
                                 vres_now=vres_now, vh_now=vh_now):
                            cx.wait(te, dq_sem, tgt)
                            cx.wait(te, "vres", vres_now)
                            cx.wait(te, "vh", vh_now)

                        dqo_tgt = cx.c("dqo")

                        def vprew(ve, tgt=dqo_tgt):
                            cx.wait(ve, "dqo", tgt)

                        emit_transposes(
                            [(wdq[:, bg, c * 128:(c + 1) * 128], mk_evac(c))
                             for c in range(8)], te_pre_wait=prew,
                            ve_pre_wait=vprew)
                        dq_tpt[g] = cx.c("tpT")

                        @block.sync
                        def _(sy):
                            cx.wait(sy, "tpE")
                            cx.inc(sy.dma_start(
                                out=dst[:, sl, 8 * cc:8 * cc + 8, :],
                                in_=wstg[:, 0:8, :]), "dqo", 16)

                    dq_units.append(unit)

        dequant_units(d_w1fg, s_w1g, NH, D)
        dequant_units(d_w1fu, s_w1u, NH, D)
        dequant_units(d_w1fd, s_w1d, 8, HF)

        def pump_dq(k=1):
            for _ in range(k):
                if dq_units:
                    dq_units.pop(0)()

        def layernorm_stats():
            sums = stats[:, 0:NT]
            sumsq = stats[:, NT:2 * NT]

            @block.vector
            def _(ve):
                cx.wait(ve, "gp", 1)
                cx.wait(ve, "vres")
                cx.wait(ve, "dmx")
                cx.wait(ve, "tpT")
                for t in range(NT):
                    ve.nd.reduce_sum(sums[:, t:t + 1], x_tm[:, t, :], AX.X)
                for t in range(NT):
                    cx.inc(ve.scalar_tensor_tensor(
                        xn_f[:, t % 2, :], x_tm[:, t, :], 1.0, x_tm[:, t, :],
                        ALU.mult, ALU.mult, accum_out=sumsq[:, t:t + 1]), "lnsq")
                mu = stats[:, 2 * NT:3 * NT]
                var = stats[:, 3 * NT:4 * NT]
                ve.tensor_scalar(mu, sums, cbv(CB_ID), None, ALU.mult)
                ve.tensor_scalar(var, sumsq, cbv(CB_ID), None, ALU.mult)
                ve.tensor_tensor(sumsq, mu, mu, ALU.mult)
                ve.tensor_tensor(var, var, sumsq, ALU.subtract)
                cx.inc(ve.tensor_scalar(var, var, cbv(CB_EPS), None, ALU.add), "lnv")

            @block.scalar
            def _(se):
                cx.wait(se, "lnv")
                cx.inc(se.activation(stats[:, 5 * NT:6 * NT],
                                     stats[:, 3 * NT:4 * NT], ACT.Sqrt), "lnsr")

            @block.vector
            def _(ve):
                cx.wait(ve, "lnsr")
                var = stats[:, 3 * NT:4 * NT]
                r = stats[:, 4 * NT:5 * NT]
                sq = stats[:, 5 * NT:6 * NT]
                t1 = stats[:, 0:NT]
                ve.reciprocal(r, sq)
                ve.tensor_tensor(t1, r, r, ALU.mult)
                ve.tensor_tensor(t1, t1, var, ALU.mult)
                ve.tensor_scalar(t1, t1, -0.5, 1.5, ALU.mult, ALU.add)
                cx.inc(ve.tensor_tensor(r, r, t1, ALU.mult), "lnr")

        def MU(t):
            return stats[:, 2 * NT + t:2 * NT + t + 1]

        def RS(t):
            return stats[:, 4 * NT + t:4 * NT + t + 1]

        def normalize_to_fm(dst, dst_tm=None):
            for t in range(NT):
                @block.vector
                def _(ve, t=t):
                    cx.wait(ve, "lnr")
                    cx.wait(ve, "tpT", cx.c("tpT") - 8)
                    cx.inc(ve.nd.scalar_tensor_tensor(
                        xn_f[:, t % 2, :], x_tm[:, t, :], MU(t),
                        RS(t).broadcast_to([128, D]), ALU.subtract, ALU.mult), "xn")
                    if dst_tm is not None:
                        cx.inc(ve.tensor_copy(dst_tm[:, t, :], xn_f[:, t % 2, :]),
                               "xntm")

                def mk_evac(t, c):
                    def f(ve, bank):
                        return ve.tensor_copy(dst[:, c, t * 128:(t + 1) * 128], bank)
                    return f

                xn_target = cx.c("xn")

                def prew(te, tgt=xn_target):
                    cx.wait(te, "xn", tgt)

                emit_transposes(
                    [(xn_f[:, t % 2, c * 128:(c + 1) * 128], mk_evac(t, c))
                     for c in range(8)], te_pre_wait=prew)

        def down_half(HIDb, wsrc, wd_sb, res_scale, th):
            for dh in range(2):
                vh_all = cx.c("vh")
                vres_prev = cx.c("vres")
                for c in range(NH):
                    di = cx.c("tdn")
                    par = di % 2

                    @block.sync
                    def _(sy, c=c, dh=dh, par=par, di=di):
                        cx.wait(sy, "dwc")
                        cx.wait(sy, "so0")
                        cx.wait(sy, "so1")
                        cx.wait(sy, "tdn", di - 1)
                        cx.inc(sy.dma_start(
                            out=wd_sb[:, par],
                            in_=wsrc[:, 4 * dh:4 * (dh + 1), c, :]),
                            f"wd{par}", 16)

                    @block.tensor
                    def _(te, c=c, dh=dh, par=par, vh_all=vh_all,
                          vres_prev=vres_prev):
                        cx.wait(te, f"wd{par}")
                        if c == 0:
                            cx.wait(te, "vh", vh_all)
                            cx.wait(te, "vres", vres_prev)
                        last = None
                        for t in range(8):
                            last = te.matmul(
                                pbank[t][:],
                                HIDb[:, c, t * 128:(t + 1) * 128],
                                wd_sb[:, par].rearrange("p a b -> p (a b)"),
                                start=(c == 0), stop=(c == NH - 1))
                        cx.inc(last, "tdn")

                @block.vector
                def _(ve, dh=dh, th=th, res_scale=res_scale):
                    cx.wait(ve, "tdn")
                    for t in range(8):
                        tt = th * 8 + t
                        cx.inc(ve.nd.scalar_tensor_tensor(
                            x_tm[:, tt, dh * 512:(dh + 1) * 512],
                            pbank[t][:], res_scale,
                            x_tm[:, tt, dh * 512:(dh + 1) * 512],
                            ALU.mult, ALU.add), "vres")

        def l0_ffn(HIDb, wg_sb, wu_sb, wd_sb, sil, pump=False):
            fence_entry()
            layernorm_stats()
            normalize_to_fm(A1)
            xfE_all = cx.c("tpE")

            for th in range(2):
                tc0 = th * 1024
                vh_base = cx.c("vh")
                for ht in range(NH):
                    par = ht % 2

                    @block.sync
                    def _(sy, ht=ht, par=par):
                        cx.wait(sy, "dwc")
                        cx.wait(sy, "gumm", cx.c("gumm") - 2)
                        cx.inc(sy.dma_start(out=wg_sb[:, par], in_=s_w0g[:, ht]),
                               f"wg{par}", 16)
                        cx.inc(sy.dma_start(out=wu_sb[:, par], in_=s_w0u[:, ht]),
                               f"wu{par}", 16)

                    for t5 in range(2):
                        u = 2 * ht + t5
                        gumm_now = cx.c("gumm")
                        sil_now = cx.c("sil")

                        @block.tensor
                        def _(te, ht=ht, par=par, t5=t5, u=u, tc0=tc0,
                              vh_base=vh_base):
                            if t5 == 0:
                                cx.wait(te, f"wg{par}")
                                cx.wait(te, f"wu{par}")
                                cx.wait(te, "tpE", xfE_all)
                            cx.wait(te, "vh", vh_base + u - 2)
                            cx.wait(te, "vres")
                            b = 2 * (u % 3)
                            pg, pu = pall[:, b, :], pall[:, b + 1, :]
                            last = None
                            for k in range(8):
                                te.matmul(pg, wg_sb[:, par, k, :],
                                          A1[:, k, tc0 + t5 * 512:tc0 + (t5 + 1) * 512],
                                          start=(k == 0), stop=(k == 7))
                            for k in range(8):
                                last = te.matmul(
                                    pu, wu_sb[:, par, k, :],
                                    A1[:, k, tc0 + t5 * 512:tc0 + (t5 + 1) * 512],
                                    start=(k == 0), stop=(k == 7))
                            cx.inc(last, "gumm")

                        @block.scalar
                        def _(se, u=u, vh_base=vh_base, gumm_now=gumm_now):
                            cx.wait(se, "gumm", gumm_now + 1)
                            cx.wait(se, "vh", vh_base + u - 2)
                            cx.inc(se.nd.activation(
                                sil[:, u % 3, :], pall[:, 2 * (u % 3), :],
                                ACT.Silu, scale=cbv(CB_I127)), "sil")

                        @block.vector
                        def _(ve, u=u, sil_now=sil_now):
                            cx.wait(ve, "sil", sil_now + 1)
                            ht_, t5_ = u // 2, u % 2
                            cx.inc(ve.nd.scalar_tensor_tensor(
                                HIDb[:, ht_, t5_ * 512:(t5_ + 1) * 512],
                                pall[:, 2 * (u % 3) + 1, :], cbv(CB_I127),
                                sil[:, u % 3, :],
                                ALU.mult, ALU.mult), "vh")

                    if pump:
                        pump_dq(2)

                down_half(HIDb, s_w0d, wd_sb, cbv(CB_M05127), th)

        def l0_attn(A2, lrb, clb, crb, sql, lrow, rl0):
            fence_entry()
            layernorm_stats()
            normalize_to_fm(A1, dst_tm=A2)

            @block.gpsimd
            def _(gp):
                cx.inc(gp.dma_start(out=clb[:], in_=d_cl), "dcl", 16)
                cx.inc(gp.dma_start(out=crb[:], in_=d_cr), "dcl", 16)

            xn_all = cx.c("tpE")
            lr_jobs = []
            for m, wsb in ((0, clb), (1, crb)):
                for kc in range(2):
                    for t5 in range(4):
                        lr_jobs.append((m, wsb, kc, t5))
            for idx, (m, wsb, kc, t5) in enumerate(lr_jobs):
                @block.tensor
                def _(te, idx=idx, m=m, wsb=wsb, kc=kc, t5=t5):
                    if idx == 0:
                        cx.wait(te, "dcl")
                        cx.wait(te, "tpE", xn_all)
                    cx.wait(te, "lre", idx - 1)
                    last = None
                    for c in range(8):
                        last = te.matmul(pbank[idx % 2][:],
                                         wsb[:, c, kc * 128:(kc + 1) * 128],
                                         A1[:, c, t5 * 512:(t5 + 1) * 512],
                                         start=(c == 0), stop=(c == 7))
                    cx.inc(last, "lrmm")

                @block.scalar
                def _(se, idx=idx, m=m, kc=kc, t5=t5):
                    cx.wait(se, "lrmm", idx + 1)
                    cx.inc(se.nd.activation(
                        lrb[:, 2 * m + kc, t5 * 512:(t5 + 1) * 512],
                        pbank[idx % 2][:], ACT.Copy), "lre")

            for m in range(2):
                for t5 in range(4):
                    for kc in range(2):
                        @block.vector
                        def _(ve, m=m, kc=kc, t5=t5):
                            cx.wait(ve, "lre")
                            cx.wait(ve, "l0nm", cx.c("l0nm"))
                            src_ap = lrb[:, 2 * m + kc, t5 * 512:(t5 + 1) * 512]
                            cx.inc(ve.scalar_tensor_tensor(
                                sql[:, kc, :], src_ap, 1.0, src_ap,
                                ALU.mult, ALU.mult), "l0sq")

                    @block.tensor
                    def _(te, m=m, t5=t5):
                        cx.wait(te, "l0sq")
                        cx.wait(te, "l0ne", cx.c("l0ne"))
                        last = None
                        for kc in range(2):
                            last = te.matmul(pbank[2][0:1, :], ones_bf[:],
                                             sql[:, kc, :], start=(kc == 0),
                                             stop=(kc == 1))
                        cx.inc(last, "l0nm")

                    @block.vector
                    def _(ve, m=m, t5=t5):
                        cx.wait(ve, "l0nm")
                        cx.inc(ve.tensor_copy(
                            lrow[0:1, (4 * m + t5) * 512:
                                 (4 * m + t5 + 1) * 512],
                            pbank[2][0:1, :]), "l0ne")

            @block.vector
            def _(ve):
                cx.wait(ve, "l0ne")
                ve.reduce_max(rl0[0:1, 0:1], lrow[0:1, 0:T], AX.X)
                cx.inc(ve.reduce_max(rl0[0:1, 1:2], lrow[0:1, T:2 * T], AX.X),
                       "l0nx")

            @block.scalar
            def _(se):
                cx.wait(se, "l0nx")
                cx.inc(se.activation(rl0[0:1, 2:4], rl0[0:1, 0:2], ACT.Sqrt),
                       "l0bs")

            @block.vector
            def _(ve):
                cx.wait(ve, "l0bs")
                ve.tensor_tensor(rl0[0:1, 4:5], rl0[0:1, 2:3], rl0[0:1, 3:4],
                                 ALU.mult)
                cx.inc(ve.tensor_scalar(rl0[0:1, 4:5], rl0[0:1, 4:5],
                                        -1.03 / 32.0, None, ALU.mult), "l0bv")

            @block.tensor
            def _(te):
                cx.wait(te, "l0bv")
                cx.inc(te.matmul(pbank[2][:, 0:1], onesrow[:], rl0[0:1, 4:5],
                                 start=True, stop=True), "l0bb")

            @block.vector
            def _(ve):
                cx.wait(ve, "l0bb")
                cx.inc(ve.tensor_copy(scal[:, 0:1], pbank[2][:, 0:1]), "l0bc")

            ETb = A1[:, 0:4, :].rearrange("p a b -> p (a b)").rearrange(
                "p (j i) -> p j i", i=512)
            METb = A1[:, 4:8, :].rearrange("p a b -> p (a b)").rearrange(
                "p (j i) -> p j i", i=512)
            for isl in range(4):
                e_base = cx.c("l0e")
                m_base = cx.c("l0m")
                l_prev = cx.c("l0l")
                pv_prev = cx.c("l0pv")
                s_base = cx.c("l0s")
                for j in range(NT):
                    @block.tensor
                    def _(te, isl=isl, j=j):
                        if j == 0 and isl == 0:
                            cx.wait(te, "l0bc")
                        cx.wait(te, "l0e", e_base + j // 2 - 1)
                        cx.wait(te, "l0m", m_base + j // 2 - 1)
                        last = None
                        for kc in range(2):
                            last = te.matmul(pall[:, j % 4, :],
                                             lrb[:, 2 + kc, j * 128:(j + 1) * 128],
                                             lrb[:, kc, isl * 512:(isl + 1) * 512],
                                             start=(kc == 0), stop=(kc == 1))
                        cx.inc(last, "l0s")

                @block.scalar
                def _(se, isl=isl, s_base=s_base, l_prev=l_prev, e_base=e_base):
                    cx.wait(se, "l0l", l_prev)
                    for p in range(8):
                        cx.wait(se, "l0s", s_base + 2 * p + 2)
                        cx.wait(se, "l0m", m_base + p - 1)
                        cx.inc(se.nd.activation(
                            ETb[:, 2 * p:2 * p + 2, :],
                            pall[:, (2 * p) % 4:(2 * p) % 4 + 2, :],
                            ACT.Exp, bias=scal[:, 0:1],
                            scale=cbv(CB_I32R)), "l0e")

                @block.vector
                def _(ve, isl=isl, e_base=e_base):
                    cx.wait(ve, "l0pv", pv_prev)
                    for p in range(8):
                        cx.wait(ve, "l0e", e_base + p + 1)
                        cx.inc(ve.nd.scalar_tensor_tensor(
                            METb[:, 2 * p:2 * p + 2, :],
                            pall[:, (2 * p) % 4:(2 * p) % 4 + 2, :],
                            thr_sb[:, 0:1],
                            ETb[:, 2 * p:2 * p + 2, :], ALU.is_gt, ALU.mult),
                            "l0m")

                @block.tensor
                def _(te, isl=isl):
                    cx.wait(te, "l0e")
                    cx.wait(te, "l0r", cx.c("l0r"))
                    cx.wait(te, "l0bc", cx.c("l0bc"))
                    for ic in range(4):
                        last = None
                        for j in range(NT):
                            last = te.matmul(pall[:, 4, ic:ic + 1],
                                             ETb[:, j, ic * 128:(ic + 1) * 128],
                                             ones_bf[:], start=(j == 0),
                                             stop=(j == NT - 1))
                        cx.inc(last, "l0l")

                @block.vector
                def _(ve, isl=isl):
                    cx.wait(ve, "l0l")
                    cx.inc(ve.reciprocal(rl0[:, 8 + isl * 4:8 + isl * 4 + 4],
                                         pall[:, 4, 0:4]), "l0r")

                for ic in range(4):
                    @block.tensor
                    def _(te, isl=isl, ic=ic):
                        cx.wait(te, "l0m")
                        cx.wait(te, "l0r", cx.c("l0r"))
                        cx.wait(te, "vres")
                        last = None
                        for dhh in range(2):
                            for j in range(NT):
                                last = te.matmul(
                                    pall[:, 5 + dhh, :],
                                    METb[:, j, ic * 128:(ic + 1) * 128],
                                    A2[:, j, dhh * 512:(dhh + 1) * 512],
                                    start=(j == 0), stop=(j == NT - 1))
                        cx.inc(last, "l0pv")

                    @block.vector
                    def _(ve, isl=isl, ic=ic):
                        cx.wait(ve, "l0pv")
                        tt = isl * 4 + ic
                        for dhh in range(2):
                            cx.inc(ve.nd.scalar_tensor_tensor(
                                x_tm[:, tt, dhh * 512:(dhh + 1) * 512],
                                pall[:, 5 + dhh, :],
                                rl0[:, 8 + tt:8 + tt + 1],
                                x_tm[:, tt, dhh * 512:(dhh + 1) * 512],
                                ALU.mult, ALU.add), "vres")


        ccn = [0]

        def barrier(brow_k, dst_slots, prep_sem):
            k = brow_k
            n = ccn[0]
            ccn[0] += 1
            cci = nc.dram_tensor(f"cci{n}", [k], F32).ap()
            cco = nc.dram_tensor(f"cco{n}", [k], F32).ap()

            @block.sync
            def _(sy):
                cx.wait(sy, prep_sem[0], prep_sem[1])
                cx.inc(sy.dma_start(out=cci, in_=vsml[0:1, 8:8 + k]), "cds", 16)

            @block.gpsimd
            def _(gp):
                cx.wait(gp, "cds")
                cx.inc(gp.collective_compute(
                    "AllReduce", ALU.max, replica_groups=[list(range(cc_world))],
                    ins=[cci], outs=[cco]), "ccl")

            @block.sync
            def _(sy):
                cx.wait(sy, "ccl")
                for i, sl in enumerate(dst_slots):
                    cx.inc(sy.dma_start(
                        out=scal[:, sl:sl + 1],
                        in_=cco[i:i + 1][None, :].partition_broadcast(128)),
                        "cbb", 16)

        def part_max(srcs, dst_cols, src_sem):
            def mk(i):
                def f(ve, bank):
                    return ve.reduce_max(vsml[0:1, dst_cols[i]:dst_cols[i] + 1],
                                         bank[0:1, 0:128], AX.X)
                return f

            def prew(te, ss=src_sem):
                cx.wait(te, ss[0], ss[1])
            emit_transposes([(s, mk(i)) for i, s in enumerate(srcs)],
                            te_pre_wait=prew)

        def l1_ffn(tag):
            fence_entry()
            layernorm_stats()

            @block.vector
            def _(ve):
                cx.wait(ve, "lnr")
                xmx = stats[:, 0:NT]
                xmn = stats[:, NT:2 * NT]
                for t in range(NT):
                    ve.reduce_max(xmx[:, t:t + 1], x_tm[:, t, :], AX.X)
                    ve.tensor_reduce(xmn[:, t:t + 1], x_tm[:, t, :], AX.X, ALU.min)
                mu = stats[:, 2 * NT:3 * NT]
                r = stats[:, 4 * NT:5 * NT]
                sq = stats[:, 5 * NT:6 * NT]
                ve.tensor_tensor(xmx, xmx, mu, ALU.subtract)
                ve.tensor_tensor(xmn, mu, xmn, ALU.subtract)
                ve.tensor_tensor(xmx, xmx, xmn, ALU.max)
                ve.tensor_tensor(xmx, xmx, r, ALU.mult)
                cx.inc(ve.reduce_max(vsml[:, 0:1], xmx, AX.X), "f1x")

            prew_tgt = cx.c("f1x")

            def prew(te, tgt=prew_tgt):
                cx.wait(te, "f1x", tgt)
            emit_transposes([(vsml[:, 0:1],
                              lambda ve, bank: ve.reduce_max(
                                  vsml[0:1, 8:9], bank[0:1, 0:128], AX.X))],
                            te_pre_wait=prew)
            barrier(1, [1], ("tpE", cx.c("tpE")))

            @block.vector
            def _(ve):
                cx.wait(ve, "cbb")
                ve.tensor_scalar(scal[:, 2:3], scal[:, 1:2], cbv(CB_QM),
                                 cbv(CB_E12), ALU.mult, ALU.max)
                ve.reciprocal(scal[:, 3:4], scal[:, 2:3])
                A = stats[:, 0:NT]
                B = stats[:, NT:2 * NT]
                ve.tensor_scalar(A, stats[:, 4 * NT:5 * NT], scal[:, 3:4], None,
                                 ALU.mult)
                ve.tensor_tensor(B, stats[:, 2 * NT:3 * NT], A, ALU.mult)
                cx.inc(ve.tensor_scalar(B, B, -1.0, None, ALU.mult), "f1d")

            for t in range(NT):
                @block.vector
                def _(ve, t=t):
                    cx.wait(ve, "f1d")
                    cx.wait(ve, "tpT", cx.c("tpT") - 8)
                    ve.tensor_scalar(xn_f[:, t % 2, :], x_tm[:, t, :],
                                     stats[:, t:t + 1],
                                     stats[:, NT + t:NT + t + 1],
                                     ALU.mult, ALU.add)
                    cx.inc(ve.tensor_scalar(xn_f[:, t % 2, :], xn_f[:, t % 2, :],
                                            cbv(CB_MAGIC), cbv(CB_MAGIC),
                                            ALU.add, ALU.subtract), "xn")

                def mk_evac(t, c):
                    def f(ve, bank):
                        return ve.tensor_copy(A1[:, c, t * 128:(t + 1) * 128], bank)
                    return f

                xn_target = cx.c("xn")

                def prew2(te, tgt=xn_target):
                    cx.wait(te, "xn", tgt)

                emit_transposes(
                    [(xn_f[:, t % 2, c * 128:(c + 1) * 128], mk_evac(t, c))
                     for c in range(8)], te_pre_wait=prew2)

            cmA = [salloc(f"{tag}stgu", [128, 2, T], F32),
                   salloc(f"{tag}stgg", [128, 2, T], F32),
                   salloc(f"{tag}wg", [128, 2, 8, 128], BF16),
                   salloc(f"{tag}wu", [128, 2, 8, 128], BF16)]
            stgu, stgg, wg_sb, wu_sb = [t for _, t in cmA]

            @block.gpsimd
            def _(gp):
                gp.memset(vsml[:, 3:4], 0.0)
                cx.inc(gp.memset(vsml[:, 5:6], 0.0), "f1z")

            xfE_all = cx.c("tpE")
            f1g5_base = cx.c("f1g5")
            sue_base = cx.c("sue")
            spu_base = {0: cx.c("spu0"), 1: cx.c("spu1")}
            spg_base = {0: cx.c("spg0"), 1: cx.c("spg1")}
            sum_base = cx.c("sum")
            for ht in range(NH):
                par = ht % 2
                g_base = cx.c("f1g5")

                @block.sync
                def _(sy, ht=ht, par=par, g_base=g_base):
                    cx.wait(sy, "dqo")
                    cx.wait(sy, "f1g5", g_base - 4)
                    cx.inc(sy.dma_start(out=wg_sb[:, par], in_=s_w1g[:, ht]),
                           f"wg{par}", 16)
                    cx.inc(sy.dma_start(out=wu_sb[:, par], in_=s_w1u[:, ht]),
                           f"wu{par}", 16)

                for t5 in range(4):
                    g5 = cx.c("f1g5")

                    @block.tensor
                    def _(te, ht=ht, par=par, t5=t5, g5=g5):
                        if t5 == 0:
                            cx.wait(te, f"wg{par}")
                            cx.wait(te, f"wu{par}")
                            cx.wait(te, "tpE", xfE_all)
                        cx.wait(te, "sue", 2 * (g5 - 3))
                        bk = 2 * (g5 % 4)
                        last = None
                        for k in range(8):
                            te.matmul(pall[:, bk, :], wg_sb[:, par, k, :],
                                      A1[:, k, t5 * 512:(t5 + 1) * 512],
                                      start=(k == 0), stop=(k == 7))
                        for k in range(8):
                            last = te.matmul(pall[:, bk + 1, :], wu_sb[:, par, k, :],
                                             A1[:, k, t5 * 512:(t5 + 1) * 512],
                                             start=(k == 0), stop=(k == 7))
                        cx.inc(last, "f1g5")

                @block.sync
                def _(sy, ht=ht, par=par, sue_base=sue_base):
                    cx.wait(sy, "sue", sue_base + 8 * (ht + 1))
                    cx.inc(sy.dma_start(out=s_up[ht], in_=stgu[:, par, :]),
                           f"spu{par}", 16)
                    cx.inc(sy.dma_start(out=s_gate[ht], in_=stgg[:, par, :]),
                           f"spg{par}", 16)

            @block.scalar
            def _(se, f1g5_base=f1g5_base, spu_base=spu_base, spg_base=spg_base,
                  sum_base=sum_base):
                for ht in range(NH):
                    par = ht % 2
                    for t5 in range(4):
                        g5 = f1g5_base + 4 * ht + t5
                        cx.wait(se, "f1g5", g5 + 1)
                        if t5 == 0:
                            cx.wait(se, f"spu{par}",
                                    spu_base[par] + 16 * (ht // 2))
                            cx.wait(se, f"spg{par}",
                                    spg_base[par] + 16 * (ht // 2))
                            cx.wait(se, "sum", sum_base + ht - 1)
                        bk = 2 * (g5 % 4)
                        cx.inc(se.nd.activation(
                            stgg[:, par, t5 * 512:(t5 + 1) * 512],
                            pall[:, bk, :], ACT.Copy), "sue")
                        cx.inc(se.nd.activation(
                            stgu[:, par, t5 * 512:(t5 + 1) * 512],
                            pall[:, bk + 1, :], ACT.Copy), "sue")

            @block.vector
            def _(ve, sue_base=sue_base):
                cx.wait(ve, "f1z", cx.c("f1z"))
                for ht in range(NH):
                    par = ht % 2
                    cx.wait(ve, "sue", sue_base + 8 * (ht + 1))
                    ve.nd.tensor_reduce(vsml[:, 2:3], stgu[:, par, :], AX.X,
                                        ALU.max, apply_absolute_value=True)
                    ve.tensor_tensor(vsml[:, 3:4], vsml[:, 3:4], vsml[:, 2:3],
                                     ALU.max)
                    ve.nd.tensor_reduce(vsml[:, 4:5], stgg[:, par, :], AX.X,
                                        ALU.max, apply_absolute_value=True)
                    cx.inc(ve.tensor_tensor(vsml[:, 5:6], vsml[:, 5:6],
                                            vsml[:, 4:5], ALU.max), "sum")


            part_max([vsml[:, 3:4], vsml[:, 5:6]], [8, 9], ("sum", cx.c("sum")))
            barrier(2, [4, 5], ("tpE", cx.c("tpE")))

            @block.vector
            def _(ve):
                cx.wait(ve, "cbb")
                ve.tensor_tensor(vsml[:, 6:7], scal[:, 4:5], scal[:, 2:3], ALU.mult)
                ve.tensor_scalar(scal[:, 6:7], vsml[:, 6:7], cbv(CB_QM),
                                 cbv(CB_E12), ALU.mult, ALU.max)
                ve.reciprocal(vsml[:, 7:8], scal[:, 6:7])
                ve.tensor_tensor(scal[:, 8:9], scal[:, 2:3], vsml[:, 7:8], ALU.mult)
                ve.tensor_tensor(vsml[:, 6:7], scal[:, 5:6], scal[:, 2:3], ALU.mult)
                ve.tensor_scalar(scal[:, 7:8], vsml[:, 6:7], cbv(CB_QM),
                                 cbv(CB_E12), ALU.mult, ALU.max)
                ve.reciprocal(vsml[:, 7:8], scal[:, 7:8])
                cx.inc(ve.tensor_tensor(scal[:, 9:10], scal[:, 2:3], vsml[:, 7:8],
                                        ALU.mult), "f1d2")

            sfree(*[cm for cm, _ in cmA])
            cmT = [salloc(f"{tag}ts", [128, 4, T], F32),
                   salloc(f"{tag}iu", [128, 2, T], I16),
                   salloc(f"{tag}ig", [128, 2, T], I16)]
            tsc, iub, igb = [t for _, t in cmT]
            cmB = [salloc(f"{tag}ru", [128, 2, T], F32),
                   salloc(f"{tag}rg", [128, 2, T], F32)]
            ru, rg = [t for _, t in cmB]

            @block.gpsimd
            def _(gp):
                cx.inc(gp.memset(vsml[:, 11:12], 0.0), "f1z2")

            for ht in range(NH):
                par = ht % 2
                a_base = cx.c("f1a")

                @block.sync
                def _(sy, ht=ht, par=par, a_base=a_base):
                    cx.wait(sy, f"spu{par}")
                    cx.wait(sy, f"spg{par}")
                    cx.wait(sy, "f1a", a_base - 1)
                    cx.inc(sy.dma_start(out=ru[:, par, :], in_=s_up[ht]),
                           f"ru{par}", 16)
                    cx.inc(sy.dma_start(out=rg[:, par, :], in_=s_gate[ht]),
                           f"rg{par}", 16)

                @block.scalar
                def _(se, ht=ht, par=par):
                    cx.wait(se, f"rg{par}")
                    cx.wait(se, "f1d2", cx.c("f1d2"))
                    cx.wait(se, "f1a", cx.c("f1a"))
                    if ht < 2:
                        for s in ("spu0", "spu1", "spg0", "spg1"):
                            cx.wait(se, s)
                    cx.inc(se.activation(tsc[:, 0, :], rg[:, par, :], ACT.Identity,
                                         bias=cbv(CB_MAGIC), scale=scal[:, 9:10]),
                           "f1p")

                @block.vector
                def _(ve, ht=ht, par=par):
                    cx.wait(ve, f"ru{par}")
                    cx.wait(ve, "f1p", cx.c("f1p"))
                    cx.wait(ve, "f1z2", cx.c("f1z2"))
                    if ht < 2:
                        for s in ("spu0", "spu1", "spg0", "spg1"):
                            cx.wait(ve, s)
                    if ht >= 2:
                        cx.wait(ve, f"siu{par}")
                        cx.wait(ve, f"sig{par}")
                    ve.tensor_scalar(tsc[:, 1, :], ru[:, par, :], scal[:, 8:9],
                                     cbv(CB_MAGIC), ALU.mult, ALU.add)
                    ve.tensor_scalar(iub[:, par, :], tsc[:, 1, :], cbv(CB_MAGIC),
                                     None, ALU.subtract)
                    ve.tensor_scalar(tsc[:, 2, :], tsc[:, 0, :], cbv(CB_MAGIC),
                                     None, ALU.subtract)
                    cx.inc(ve.tensor_scalar(igb[:, par, :], tsc[:, 0, :],
                                            cbv(CB_MAGIC), None, ALU.subtract),
                           "f1q1")

                @block.scalar
                def _(se, ht=ht, par=par):
                    cx.wait(se, "f1q1", cx.c("f1q1"))
                    cx.inc(se.activation(tsc[:, 1, :], tsc[:, 2, :], ACT.Sigmoid,
                                         scale=scal[:, 7:8]), "f1sg")

                @block.vector
                def _(ve, ht=ht, par=par):
                    cx.wait(ve, "f1sg", cx.c("f1sg"))
                    ve.tensor_scalar(tsc[:, 3, :], tsc[:, 1, :], cbv(CB_65536),
                                     cbv(CB_MAGIC), ALU.mult, ALU.add)
                    ve.tensor_scalar(tsc[:, 3, :], tsc[:, 3, :], cbv(CB_MAGIC),
                                     cbv(CB_65535), ALU.subtract, ALU.min)
                    ve.tensor_tensor(tsc[:, 3, :], tsc[:, 3, :], tsc[:, 2, :],
                                     ALU.mult)
                    ve.tensor_reduce(vsml[:, 10:11], tsc[:, 3, :], AX.X, ALU.max,
                                     apply_absolute_value=True)
                    cx.inc(ve.tensor_tensor(vsml[:, 11:12], vsml[:, 11:12],
                                            vsml[:, 10:11], ALU.max), "f1a")

                @block.sync
                def _(sy, ht=ht, par=par):
                    cx.wait(sy, "f1a")
                    cx.inc(sy.dma_start(out=s_ui[ht], in_=iub[:, par, :]),
                           f"siu{par}", 16)
                    cx.inc(sy.dma_start(out=s_gi[ht], in_=igb[:, par, :]),
                           f"sig{par}", 16)

            part_max([vsml[:, 11:12]], [8], ("f1a", cx.c("f1a")))
            barrier(1, [11], ("tpE", cx.c("tpE")))

            @block.vector
            def _(ve):
                cx.wait(ve, "cbb")
                ve.tensor_tensor(vsml[:, 6:7], scal[:, 11:12], scal[:, 7:8],
                                 ALU.mult)
                ve.tensor_scalar(vsml[:, 6:7], vsml[:, 6:7], cbv(CB_I65536), None,
                                 ALU.mult)
                ve.tensor_scalar(scal[:, 12:13], vsml[:, 6:7], cbv(CB_QM),
                                 cbv(CB_E12), ALU.mult, ALU.max)
                ve.reciprocal(vsml[:, 7:8], scal[:, 12:13])
                ve.tensor_tensor(vsml[:, 7:8], vsml[:, 7:8], scal[:, 7:8], ALU.mult)
                cx.inc(ve.tensor_scalar(scal[:, 13:14], vsml[:, 7:8],
                                        cbv(CB_I65536), None, ALU.mult), "f1d3")

            sfree(*[cm for cm, _ in cmB])
            cmC = [salloc(f"{tag}o32", [128, 2, T], I32)]
            o32b = cmC[0][1]

            @block.gpsimd
            def _(gp):
                cx.inc(gp.memset(vsml[:, 13:14], 0.0), "f1z3")

            for ht in range(NH):
                par = ht % 2
                b_base = cx.c("f1b")

                @block.gpsimd
                def _(gp, ht=ht, par=par, b_base=b_base):
                    cx.wait(gp, f"siu{par}")
                    cx.wait(gp, f"sig{par}")
                    cx.wait(gp, "f1b", b_base)
                    cx.inc(gp.dma_start(out=tsc[:, 1, :], in_=s_ui[ht]),
                           f"riu{par}", 16)
                    cx.inc(gp.dma_start(out=tsc[:, 2, :], in_=s_gi[ht]),
                           f"rig{par}", 16)

                @block.vector
                def _(ve, ht=ht, par=par):
                    cx.wait(ve, f"riu{par}")
                    cx.wait(ve, f"rig{par}")
                    cx.wait(ve, "f1d3", cx.c("f1d3"))
                    cx.wait(ve, "f1z3", cx.c("f1z3"))
                    if ht >= 2:
                        cx.wait(ve, f"so{par}")
                    cx.inc(ve.nd.memset(vsml[:, 9:10], 0.0), "f1q2")

                @block.scalar
                def _(se, ht=ht, par=par):
                    cx.wait(se, "f1q2", cx.c("f1q2"))
                    cx.inc(se.activation(tsc[:, 0, :], tsc[:, 2, :], ACT.Sigmoid,
                                         scale=scal[:, 7:8]), "f1sh")

                @block.vector
                def _(ve, ht=ht, par=par):
                    cx.wait(ve, "f1sh", cx.c("f1sh"))
                    ve.tensor_scalar(tsc[:, 3, :], tsc[:, 0, :], cbv(CB_65536),
                                     cbv(CB_MAGIC), ALU.mult, ALU.add)
                    ve.tensor_scalar(tsc[:, 3, :], tsc[:, 3, :], cbv(CB_MAGIC),
                                     cbv(CB_65535), ALU.subtract, ALU.min)
                    ve.tensor_tensor(tsc[:, 3, :], tsc[:, 3, :], tsc[:, 2, :],
                                     ALU.mult)
                    ve.tensor_scalar(tsc[:, 3, :], tsc[:, 3, :], scal[:, 13:14],
                                     cbv(CB_MAGIC), ALU.mult, ALU.add)
                    ve.tensor_scalar(tsc[:, 3, :], tsc[:, 3, :], cbv(CB_MAGIC),
                                     None, ALU.subtract)
                    ve.tensor_tensor(tsc[:, 3, :], tsc[:, 3, :], tsc[:, 1, :],
                                     ALU.mult)
                    ve.tensor_reduce(vsml[:, 10:11], tsc[:, 3, :], AX.X, ALU.max,
                                     apply_absolute_value=True)
                    ve.tensor_tensor(vsml[:, 13:14], vsml[:, 13:14],
                                     vsml[:, 10:11], ALU.max)
                    cx.inc(ve.tensor_copy(o32b[:, par, :], tsc[:, 3, :]), "f1b")

                @block.sync
                def _(sy, ht=ht, par=par):
                    cx.wait(sy, "f1b")
                    cx.inc(sy.dma_start(out=s_o32[ht], in_=o32b[:, par, :]),
                           f"so{par}", 16)

            part_max([vsml[:, 13:14]], [8], ("f1b", cx.c("f1b")))
            barrier(1, [14], ("tpE", cx.c("tpE")))

            @block.vector
            def _(ve):
                cx.wait(ve, "cbb")
                ve.tensor_tensor(vsml[:, 6:7], scal[:, 14:15], scal[:, 12:13],
                                 ALU.mult)
                ve.tensor_tensor(vsml[:, 6:7], vsml[:, 6:7], scal[:, 6:7], ALU.mult)
                ve.tensor_scalar(scal[:, 15:16], vsml[:, 6:7], cbv(CB_QM),
                                 cbv(CB_E12), ALU.mult, ALU.max)
                ve.reciprocal(vsml[:, 7:8], scal[:, 15:16])
                ve.tensor_tensor(vsml[:, 7:8], vsml[:, 7:8], scal[:, 12:13],
                                 ALU.mult)
                ve.tensor_tensor(scal[:, 16:17], vsml[:, 7:8], scal[:, 6:7],
                                 ALU.mult)
                cx.inc(ve.tensor_scalar(scal[:, 17:18], scal[:, 15:16],
                                        cbv(CB_HALF), None, ALU.mult), "f1d4")

            sfree(*[cm for cm, _ in cmT], *[cm for cm, _ in cmC])
            cmD = [salloc(f"{tag}t3", [128, 1024], F32),
                   salloc(f"{tag}hid", [128, NH, D], BF16),
                   salloc(f"{tag}wd", [128, 2, 4, 128], BF16)]
            t3b, HIDb, wd_sb = [t for _, t in cmD]
            for th in range(2):
                for ht in range(NH):
                    par = ht % 2
                    h_base = cx.c("f1h")

                    @block.gpsimd
                    def _(gp, ht=ht, th=th, par=par, h_base=h_base):
                        cx.wait(gp, f"so{par}")
                        cx.wait(gp, "f1h", h_base)
                        cx.inc(gp.dma_start(
                            out=t3b[:],
                            in_=s_o32[ht, :, th * 1024:(th + 1) * 1024]),
                            f"ro{par}", 16)

                    @block.vector
                    def _(ve, ht=ht, th=th, par=par):
                        cx.wait(ve, f"ro{par}")
                        if ht < 2 and th == 0:
                            cx.wait(ve, "so0")
                            cx.wait(ve, "so1")
                        cx.wait(ve, "f1d4", cx.c("f1d4"))
                        cx.wait(ve, "tdn")
                        ve.nd.tensor_scalar(t3b[:], t3b[:],
                                            scal[:, 16:17], cbv(CB_MAGIC),
                                            ALU.mult, ALU.add)
                        cx.inc(ve.tensor_scalar(HIDb[:, ht, :], t3b[:],
                                                cbv(CB_MAGIC), None, ALU.subtract),
                               "f1h")
                        cx.inc(ve.nd.memset(vsml[:, 15:16], 0.0), "vh")

                down_half(HIDb, s_w1d, wd_sb, scal[:, 17:18], th)
            sfree(*[cm for cm, _ in cmD])


        def l1_attn(tag):
            fence_entry()
            layernorm_stats()

            sums2 = stats[:, 0:NT]
            sumsq2 = stats[:, NT:2 * NT]
            for t in range(NT):
                @block.vector
                def _(ve, t=t):
                    cx.wait(ve, "lnr")
                    ve.scalar_tensor_tensor(
                        xn_f[:, t % 2, :], x_tm[:, t, :], MU(t),
                        RS(t).broadcast_to([128, D]), ALU.subtract, ALU.mult)
                    ve.reduce_sum(sums2[:, t:t + 1], xn_f[:, t % 2, :], AX.X)
                    cx.inc(ve.scalar_tensor_tensor(
                        xn_f[:, t % 2, :], xn_f[:, t % 2, :], 1.0,
                        xn_f[:, t % 2, :], ALU.mult, ALU.mult,
                        accum_out=sumsq2[:, t:t + 1]), "a5s")

            @block.vector
            def _(ve):
                cx.wait(ve, "a5s")
                mu2 = stats[:, 3 * NT:4 * NT]
                var2 = stats[:, 5 * NT:6 * NT]
                ve.tensor_scalar(mu2, sums2, cbv(CB_ID), None, ALU.mult)
                ve.tensor_scalar(var2, sumsq2, cbv(CB_ID), None, ALU.mult)
                ve.tensor_tensor(sums2, mu2, mu2, ALU.mult)
                ve.tensor_tensor(var2, var2, sums2, ALU.subtract)
                cx.inc(ve.tensor_scalar(var2, var2, cbv(CB_EPS), None, ALU.add),
                       "a5v")

            @block.scalar
            def _(se):
                cx.wait(se, "a5v")
                cx.inc(se.activation(stats[:, 0:NT], stats[:, 5 * NT:6 * NT],
                                     ACT.Sqrt), "a5q")

            @block.vector
            def _(ve):
                cx.wait(ve, "a5q")
                mu2 = stats[:, 3 * NT:4 * NT]
                var2 = stats[:, 5 * NT:6 * NT]
                r1 = stats[:, 4 * NT:5 * NT]
                r2 = stats[:, NT:2 * NT]
                t1 = stats[:, 0:NT]
                ve.reciprocal(r2, t1)
                ve.tensor_tensor(t1, r2, r2, ALU.mult)
                ve.tensor_tensor(t1, t1, var2, ALU.mult)
                ve.tensor_scalar(t1, t1, -0.5, 1.5, ALU.mult, ALU.add)
                ve.tensor_tensor(r2, r2, t1, ALU.mult)
                ve.tensor_tensor(t1, r1, r2, ALU.mult)
                ve.tensor_tensor(var2, stats[:, 2 * NT:3 * NT], t1, ALU.mult)
                ve.tensor_tensor(mu2, mu2, r2, ALU.mult)
                cx.inc(ve.tensor_tensor(mu2, mu2, var2, ALU.add), "a5r")

            def AP_A(t):
                return stats[:, t:t + 1]

            def AP_B(t):
                return stats[:, 3 * NT + t:3 * NT + t + 1]

            for t in range(NT):
                @block.vector
                def _(ve, t=t):
                    cx.wait(ve, "a5r")
                    cx.wait(ve, "tpT", cx.c("tpT") - 8)
                    cx.inc(ve.scalar_tensor_tensor(
                        xn_f[:, t % 2, :], x_tm[:, t, :], AP_A(t),
                        AP_B(t).broadcast_to([128, D]), ALU.mult,
                        ALU.subtract), "xn")

                def mk_evac(t, c):
                    def f(ve, bank):
                        return ve.tensor_scalar(
                            A1[:, c, t * 128:(t + 1) * 128], bank,
                            gbet[:, c:c + 1], gbet[:, 8 + c:8 + c + 1],
                            ALU.mult, ALU.add)
                    return f

                xn_target = cx.c("xn")

                def prew2(te, tgt=xn_target):
                    cx.wait(te, "xn", tgt)

                emit_transposes(
                    [(xn_f[:, t % 2, c * 128:(c + 1) * 128], mk_evac(t, c))
                     for c in range(8)], te_pre_wait=prew2)

            cmP = [salloc(f"{tag}q", [128, 8, T], BF16),
                   salloc(f"{tag}k", [128, 8, T], BF16),
                   salloc(f"{tag}v", [128, NT, 1040], BF16),
                   salloc(f"{tag}w", [128, 8, 128], BF16),
                   salloc(f"{tag}st", [128, 512], F32),
                   salloc(f"{tag}sq", [128, 512], BF16)]
            A2q, A3k, A4v, wslab, astage, sqb = [t for _, t in cmP]

            @block.gpsimd
            def _(gp):
                last = None
                for h in range(HEADS):
                    last = gp.memset(A4v[:, :, 65 * h + 64:65 * h + 65], 1.0)
                cx.inc(last, "a5o")

            xfE_all = cx.c("tpE")
            a5w_base = cx.c("a5w")
            a5mm_base = cx.c("a5mm")
            a5ev_base = cx.c("a5ev")

            for m, wsrc in ((0, d_w1q), (1, d_w1k), (2, d_w1v)):
                for oc in range(8):
                    gi = (cx.c("a5w") - a5w_base) // 16

                    @block.gpsimd
                    def _(gp, m=m, oc=oc, wsrc=wsrc, gi=gi):
                        cx.wait(gp, "a5mm", a5mm_base + 4 * gi)
                        cx.inc(gp.dma_start(out=wslab[:], in_=wsrc[:, oc]),
                               "a5w", 16)

                    for t5 in range(4):
                        mm_i = cx.c("a5mm")
                        mm_loc = mm_i - a5mm_base

                        @block.tensor
                        def _(te, m=m, oc=oc, t5=t5, gi=gi, mm_i=mm_i,
                              mm_loc=mm_loc):
                            if t5 == 0:
                                cx.wait(te, "a5w")
                                if gi == 0:
                                    cx.wait(te, "tpE", xfE_all)
                            cx.wait(te, "a5ev", a5ev_base + mm_loc - 1)
                            last = None
                            for k in range(8):
                                last = te.matmul(
                                    pbank[mm_i % 2][:], wslab[:, k, :],
                                    A1[:, k, t5 * 512:(t5 + 1) * 512],
                                    start=(k == 0), stop=(k == 7))
                            cx.inc(last, "a5mm")

                        @block.vector
                        def _(ve, m=m, oc=oc, t5=t5, mm_i=mm_i):
                            cx.wait(ve, "a5mm", mm_i + 1)
                            bank = pbank[mm_i % 2][:]
                            bias = bqkv[:, 8 * m + oc:8 * m + oc + 1]
                            if m == 0:
                                dst = A2q[:, oc, t5 * 512:(t5 + 1) * 512]
                            elif m == 1:
                                dst = A3k[:, oc, t5 * 512:(t5 + 1) * 512]
                            else:
                                dst = astage[:]
                                cx.wait(ve, "tpT", cx.c("tpT"))
                            cx.inc(ve.nd.tensor_scalar(dst, bank, bias, None,
                                                       ALU.add), "a5ev")

                        if m == 2:
                            ev_target = cx.c("a5ev")

                            def prew3(te, tgt=ev_target):
                                cx.wait(te, "a5ev", tgt)

                            def mk_vev(oc, t5, tt):
                                def f(ve, bank):
                                    jt = 4 * t5 + tt
                                    ve.tensor_copy(
                                        A4v[:, jt, 65 * 2 * oc:65 * 2 * oc + 64],
                                        bank[:, 0:64])
                                    return ve.tensor_copy(
                                        A4v[:, jt,
                                            65 * (2 * oc + 1):65 * (2 * oc + 1) + 64],
                                        bank[:, 64:128])
                                return f
                            emit_transposes(
                                [(astage[:, tt * 128:(tt + 1) * 128],
                                  mk_vev(oc, t5, tt)) for tt in range(4)],
                                te_pre_wait=prew3)

            for m, srcb in ((0, A2q), (1, A3k)):
                for oc in range(8):
                    for half in range(4):
                        @block.vector
                        def _(ve, m=m, oc=oc, half=half):
                            cx.wait(ve, "a5ev")
                            cx.wait(ve, "a5nm", cx.c("a5nm"))
                            src_ap = srcb[:, oc, half * 512:(half + 1) * 512]
                            cx.inc(ve.scalar_tensor_tensor(
                                sqb[:], src_ap, 1.0, src_ap,
                                ALU.mult, ALU.mult), "a5sq")

                        @block.tensor
                        def _(te, m=m, oc=oc, half=half):
                            cx.wait(te, "a5sq")
                            if oc == 0:
                                cx.wait(te, "a5nr", cx.c("a5nr"))
                            cx.inc(te.matmul(pbank[2 + half][0:16, :],
                                             hsel[:, oc, :], sqb[:],
                                             start=(oc == 0), stop=(oc == 7)),
                                   "a5nm")

                @block.vector
                def _(ve, m=m):
                    cx.wait(ve, "a5nm")
                    for half in range(4):
                        ve.reduce_max(vsml[0:16, 4 * m + half:4 * m + half + 1],
                                      pbank[2 + half][0:16, :], AX.X)
                    cx.inc(ve.tensor_reduce(vsml[0:16, 8 + m:9 + m],
                                            vsml[0:16, 4 * m:4 * m + 4], AX.X,
                                            ALU.max), "a5nr")

            @block.scalar
            def _(se):
                cx.wait(se, "a5nr")
                cx.inc(se.activation(vsml[0:16, 10:12], vsml[0:16, 8:10],
                                     ACT.Sqrt), "a5ns")

            @block.vector
            def _(ve):
                cx.wait(ve, "a5ns")
                ve.tensor_tensor(vsml[0:16, 12:13], vsml[0:16, 10:11],
                                 vsml[0:16, 11:12], ALU.mult)
                cx.inc(ve.tensor_scalar(vsml[0:16, 12:13], vsml[0:16, 12:13],
                                        -0.125 * 1.03, None, ALU.mult), "a5nb")

            tb2 = cx.c("tpT")

            @block.tensor
            def _(te):
                cx.wait(te, "a5nb", cx.c("a5nb"))
                cx.wait(te, "tpE", cx.c("tpE") - 1)
                cx.inc(te.transpose(pbank[6 + tb2 % 2][0:1, 0:16],
                                    vsml[0:16, 12:13], ident[0:16, 0:16]), "tpT")

            @block.vector
            def _(ve):
                cx.wait(ve, "tpT")
                cx.inc(ve.tensor_copy(vsml[0:1, 0:16], pbank[6 + tb2 % 2][0:1, 0:16]),
                       "tpE")

            @block.tensor
            def _(te):
                cx.wait(te, "tpE")
                cx.inc(te.matmul(pbank[2][:, 0:16], onesrow[:], vsml[0:1, 0:16],
                                 start=True, stop=True), "a5bb")

            @block.vector
            def _(ve):
                cx.wait(ve, "a5bb")
                cx.inc(ve.tensor_copy(scal[:, 0:16], pbank[2][:, 0:16]), "a5bc")

            ETbs = [A1[:, 0:4, :].rearrange("p a b -> p (a b)").rearrange(
                        "p (j i) -> p j i", i=512),
                    A1[:, 4:8, :].rearrange("p a b -> p (a b)").rearrange(
                        "p (j i) -> p j i", i=512)]
            for isl in range(4):
                for hp in range(8):
                    e_base = cx.c("a5e")
                    pv_prev = cx.c("a5pv")
                    sc_base = cx.c("a5sc")
                    for j in range(NT):
                        @block.tensor
                        def _(te, isl=isl, hp=hp, j=j, e_base=e_base):
                            if j == 0 and hp == 0 and isl == 0:
                                cx.wait(te, "a5bc")
                            p = j // 2
                            for hsub in range(2):
                                cx.wait(te, "a5e",
                                        e_base + 2 * (p - 1) + hsub + 1)
                                cx.inc(te.matmul(
                                    pall[:, 2 * hsub + j % 2, :],
                                    A3k[64 * hsub:64 * hsub + 64, hp,
                                        j * 128:(j + 1) * 128],
                                    A2q[64 * hsub:64 * hsub + 64, hp,
                                        isl * 512:(isl + 1) * 512],
                                    start=True, stop=True), "a5sc")

                    @block.scalar
                    def _(se, isl=isl, hp=hp, sc_base=sc_base, pv_prev=pv_prev):
                        cx.wait(se, "a5pv", pv_prev)
                        for p in range(8):
                            j = 2 * p + 1
                            cx.wait(se, "a5sc", sc_base + 4 * p + 4)
                            for hsub in range(2):
                                h = 2 * hp + hsub
                                cx.inc(se.nd.activation(
                                    ETbs[hsub][:, j - 1:j + 1, :],
                                    pall[:, 2 * hsub:2 * hsub + 2, :],
                                    ACT.Exp, bias=scal[:, h:h + 1],
                                    scale=cbv(CB_I8R)), "a5e")

                    for hsub in range(2):
                        h = 2 * hp + hsub
                        ETb = ETbs[hsub]

                        @block.tensor
                        def _(te, isl=isl, h=h, hsub=hsub, ETb=ETb):
                            cx.wait(te, "a5e")
                            cx.wait(te, "a5ve", cx.c("a5ve") - 1)
                            if isl == 0 and h < 2:
                                cx.wait(te, "a5bc", cx.c("a5bc"))
                                cx.wait(te, "a5nr", cx.c("a5nr"))
                            last = None
                            for j in range(NT):
                                last = te.matmul(
                                    pall[0:65, 4 + hsub, :],
                                    A4v[:, j, 65 * h:65 * h + 65],
                                    ETb[:, j, :], start=(j == 0),
                                    stop=(j == NT - 1))
                            cx.inc(last, "a5pv")

                        @block.vector
                        def _(ve, isl=isl, h=h, hsub=hsub):
                            cx.wait(ve, "a5pv")
                            cx.wait(ve, "tpT", cx.c("tpT"))
                            cx.inc(ve.nd.tensor_copy(astage[0:65, :],
                                                     pall[0:65, 4 + hsub, :]),
                                   "a5ve")

                        ve_target = cx.c("a5ve")

                        def prew5(te, tgt=ve_target):
                            cx.wait(te, "a5ve", tgt)

                        def mk_pvev(isl, h, tt):
                            def f(ve, bank):
                                tti = isl * 4 + tt
                                ve.reciprocal(vsml[:, 15:16], bank[:, 64:65])
                                return ve.scalar_tensor_tensor(
                                    x_tm[:, tti, 64 * h:64 * h + 64],
                                    bank[:, 0:64], vsml[:, 15:16],
                                    x_tm[:, tti, 64 * h:64 * h + 64],
                                    ALU.mult, ALU.add)
                            return f
                        emit_transposes(
                            [(astage[0:65, tt * 128:(tt + 1) * 128],
                              mk_pvev(isl, h, tt)) for tt in range(4)],
                            te_pre_wait=prew5)

            @block.vector
            def _(ve):
                cx.inc(ve.memset(vsml[:, 15:16], 0.0), "vres")

            sfree(*[cm for cm, _ in cmP])

        def dump_x(tag):
            if tag not in dbg_outs:
                return

            @block.sync
            def _(sy):
                cx.wait(sy, "vres")
                cx.inc(sy.dma_start(
                    out=dbg_outs[tag].rearrange("(t p) d -> p t d", p=128),
                    in_=x_tm[:]), "dout", 16)

        with nc.sbuf_tensor("HIDb", [128, NH, D], BF16) as HIDb, \
             nc.sbuf_tensor("wgsb", [128, 2, 8, 128], BF16) as wg_sb, \
             nc.sbuf_tensor("wusb", [128, 2, 8, 128], BF16) as wu_sb, \
             nc.sbuf_tensor("wdsb", [128, 2, 4, 128], BF16) as wd_sb, \
             nc.sbuf_tensor("sil", [128, 4, 512], F32) as sil:
            l0_ffn(HIDb, wg_sb, wu_sb, wd_sb, sil, pump=True)
        pump_dq(len(dq_units))

        @block.vector
        def _(ve):
            cx.wait(ve, "dqo")

        @block.scalar
        def _(se):
            cx.wait(se, "dqo")

        @block.sync
        def _(sy):
            cx.wait(sy, "dqo")

        @block.gpsimd
        def _(gp):
            cx.wait(gp, "dqo")

        sfree(*[cm for cm, _ in cm_dq])
        dump_x("p1")

        with nc.sbuf_tensor("A2", [128, NT, D], BF16) as A2, \
             nc.sbuf_tensor("lrb", [128, 4, T], BF16) as lrb, \
             nc.sbuf_tensor("clb", [128, 8, K0], BF16) as clb, \
             nc.sbuf_tensor("crb", [128, 8, K0], BF16) as crb, \
             nc.sbuf_tensor("sql", [128, 2, 512], BF16) as sql, \
             nc.sbuf_tensor("lrow", [1, 2 * T], F32) as lrow, \
             nc.sbuf_tensor("rl0", [128, 24], F32) as rl0:
            l0_attn(A2, lrb, clb, crb, sql, lrow, rl0)
        dump_x("p2")

        with nc.sbuf_tensor("HIDb3", [128, NH, D], BF16) as HIDb, \
             nc.sbuf_tensor("wgsb3", [128, 2, 8, 128], BF16) as wg_sb, \
             nc.sbuf_tensor("wusb3", [128, 2, 8, 128], BF16) as wu_sb, \
             nc.sbuf_tensor("wdsb3", [128, 2, 4, 128], BF16) as wd_sb, \
             nc.sbuf_tensor("sil3", [128, 4, 512], F32) as sil:
            l0_ffn(HIDb, wg_sb, wu_sb, wd_sb, sil)
        dump_x("p3")

        l1_ffn("f4")
        dump_x("p4")

        l1_attn("a5")
        dump_x("p5")

        l1_ffn("f6")
        dump_x("p6")

        @block.sync
        def _(sy):
            cx.wait(sy, "vres")
            cx.wait(sy, "ve")
            cx.inc(sy.dma_start(
                out=d_out.rearrange("(t p) d -> p t d", p=128), in_=x_tm[:]),
                "dout", 16)
            cx.wait(sy, "dout")

    return nc


def _tile_lhsT(w_T):
    IN, OUT = w_T.shape
    return np.ascontiguousarray(
        w_T.reshape(IN // 128, 128, OUT // 128, 128).transpose(1, 2, 0, 3))


def _tile_sdmaj(w_T):
    H_, D_ = w_T.shape
    return np.ascontiguousarray(
        w_T.reshape(H_ // 128, 128, D_ // 128, 128).transpose(1, 2, 0, 3))


def kernel(L0_attn_x, L0_attn_compare_left, L0_attn_compare_right,
           L0_attn_threshold, L0_attn_gate_bias,
           L0_ffn_w_gate, L0_ffn_w_up, L0_ffn_w_down,
           L1_attn_ln_g, L1_attn_ln_b, L1_attn_Wq, L1_attn_bq,
           L1_attn_Wk, L1_attn_bk, L1_attn_Wv, L1_attn_bv,
           L1_ffn_w_gate, L1_ffn_w_up, L1_ffn_w_down,
           _debug_taps=(), _nc=None):
    x = np.asarray(L0_attn_x, dtype=np.float32)
    n = 8

    w0g = _tile_lhsT(np.ascontiguousarray(np.asarray(L0_ffn_w_gate).T))
    w0u = _tile_lhsT(np.ascontiguousarray(np.asarray(L0_ffn_w_up).T))
    w0d = _tile_sdmaj(np.ascontiguousarray(np.asarray(L0_ffn_w_down).T))
    cl = np.ascontiguousarray(np.asarray(L0_attn_compare_left, dtype=np.float32)
                              .reshape(8, 128, K0).transpose(1, 0, 2))
    cr = np.ascontiguousarray(np.asarray(L0_attn_compare_right, dtype=np.float32)
                              .reshape(8, 128, K0).transpose(1, 0, 2))
    w1q = _tile_lhsT(np.ascontiguousarray(np.asarray(L1_attn_Wq, dtype=np.float32)))
    w1k = _tile_lhsT(np.ascontiguousarray(np.asarray(L1_attn_Wk, dtype=np.float32)))
    w1v = _tile_lhsT(np.ascontiguousarray(np.asarray(L1_attn_Wv, dtype=np.float32)))

    def chunks(v):
        return np.ascontiguousarray(np.asarray(v, dtype=np.float32).reshape(8, 128).T)

    common = dict(
        w0g=w0g, w0u=w0u, w0d=w0d, cl=cl, cr=cr,
        thr=np.asarray(L0_attn_threshold, np.float32).reshape(1),
        gb=np.asarray(L0_attn_gate_bias, np.float32).reshape(1),
        w1q=w1q, w1k=w1k, w1v=w1v,
        bq=chunks(L1_attn_bq), bk=chunks(L1_attn_bk), bv=chunks(L1_attn_bv),
        l1g=chunks(L1_attn_ln_g), l1b=chunks(L1_attn_ln_b),
        w1fg=np.asarray(L1_ffn_w_gate, np.float32),
        w1fu=np.asarray(L1_ffn_w_up, np.float32),
        w1fd=np.asarray(L1_ffn_w_down, np.float32),
    )
    in_maps = [dict(common, x=np.ascontiguousarray(x[b])) for b in range(n)]

    nc = _nc if _nc is not None else build_nc(debug_taps=_debug_taps)
    res = run_bass_kernel_spmd(nc, in_maps, core_ids=list(range(n)))
    out = np.stack([res.results[b]["out"] for b in range(n)], axis=0)
    if _debug_taps:
        dbg = {t: np.stack([res.results[b][f"dbg_{t}"] for b in range(n)])
               for t in _debug_taps}
        return out, dbg
    return out


# version 08 (final)
# speedup vs baseline: 1.0778x; 1.0778x over previous
import numpy as np
import ml_dtypes
from contextlib import ExitStack

import concourse.bass as bass
import concourse.mybir as mybir
from concourse.bass_utils import run_bass_kernel_spmd

F32 = mybir.dt.float32
BF16 = mybir.dt.bfloat16
I16 = mybir.dt.int16
I32 = mybir.dt.int32
ALU = mybir.AluOpType
ACT = mybir.ActivationFunctionType
AX = mybir.AxisListType

MAGIC = 12582912.0
T = 2048
NT = 16
D = 1024
ND = 8
HF = 3072
NH = 24
HEADS = 16
DH = 64
K0 = 256
EPS = 1e-5


SEM_ALIAS = {
    "f1z2": "f1z", "f1z3": "f1z",
    "f1d2": "f1d", "f1d3": "f1d", "f1d4": "f1d",
    "f1sg": "f1s", "f1sh": "f1s", "f1p": "f1s",
    "f1q1": "f1q", "f1q2": "f1q",
    "ro0": "riu0", "ro1": "riu1",
    "a5sq": "lnsq", "a5nm": "l0nm", "a5nr": "l0nx", "a5ns": "l0bs",
    "a5nb": "l0bv", "a5bb": "l0bb", "a5bc": "l0bc", "a5sc": "l0s",
    "a5e": "l0e", "a5pv": "l0pv", "a5ve": "l0m", "a5mm": "lrmm",
    "a5ev": "l0r", "a5w": "dcl", "a5s": "xn", "a5v": "xn", "a5r": "xn",
    "a5q": "lnsr", "a5o": "gpi",
}


class Ctx:
    def __init__(self, nc, es):
        self.nc = nc
        self.es = es
        self._sems = {}
        self.cnt = {}
        self.block = es.enter_context(nc.Block())

    def sem(self, name):
        name = SEM_ALIAS.get(name, name)
        if name not in self._sems:
            self._sems[name] = self.es.enter_context(self.nc.semaphore(name))
            self.cnt[name] = 0
        return self._sems[name]

    def inc(self, instr, name, n=1):
        name = SEM_ALIAS.get(name, name)
        instr.then_inc(self.sem(name), n)
        self.cnt[name] += n
        return instr

    def c(self, name):
        name = SEM_ALIAS.get(name, name)
        self.sem(name)
        return self.cnt[name]

    def wait(self, eng, name, val=None):
        name = SEM_ALIAS.get(name, name)
        s = self.sem(name)
        v = self.cnt[name] if val is None else val
        if v > 0:
            eng.wait_ge(s, v)


_COMPUTE_OPS = (
    "tensor_copy", "tensor_tensor", "tensor_add", "tensor_sub", "tensor_mul",
    "tensor_max", "tensor_scalar", "tensor_single_scalar", "tensor_scalar_add",
    "tensor_scalar_sub", "tensor_scalar_mul", "tensor_scalar_max",
    "tensor_scalar_min", "tensor_relu", "tensor_reduce", "reduce_max",
    "reduce_sum", "scalar_tensor_tensor", "tensor_tensor_scan", "reciprocal",
    "select", "copy_predicated", "memset", "memzero", "activation", "add",
    "mul", "copy", "sqrt", "square", "sign", "affine_select", "iota",
    "bn_stats", "bn_aggr", "transpose", "stream_shuffle", "pool", "pool_avg",
    "pool_max", "tensor_tensor_reduce", "affine_mul_reduce",
)


class _NoDrain:

    def __init__(self, proxy):
        self._p = proxy

    def __getattr__(self, name):
        attr = getattr(self._p._eng, name)
        if name in _COMPUTE_OPS:
            def wrapped(*a, **kw):
                self._p._dirty = True
                return attr(*a, **kw)
            return wrapped
        return attr


class EngProxy:

    def __init__(self, eng):
        self._eng = eng
        self._dirty = False
        self.nd = _NoDrain(self)

    def __getattr__(self, name):
        attr = getattr(self._eng, name)
        if name in _COMPUTE_OPS:
            def wrapped(*a, **kw):
                if self._dirty:
                    self._eng.drain()
                self._dirty = True
                return attr(*a, **kw)
            return wrapped
        return attr


def build_nc(debug_taps=(), cc_world=8):
    nc = bass.Bass("TRN2", target_bir_lowering=False, debug=False)

    d_x = nc.dram_tensor("x", [T, D], F32, kind="ExternalInput").ap()
    d_w0g = nc.dram_tensor("w0g", [128, NH, 8, 128], I32, kind="ExternalInput").ap()
    d_w0u = nc.dram_tensor("w0u", [128, NH, 8, 128], I32, kind="ExternalInput").ap()
    d_w0d = nc.dram_tensor("w0d", [128, 8, NH, 128], I32, kind="ExternalInput").ap()
    d_cl = nc.dram_tensor("cl", [128, 8, K0], F32, kind="ExternalInput").ap()
    d_cr = nc.dram_tensor("cr", [128, 8, K0], F32, kind="ExternalInput").ap()
    d_thr = nc.dram_tensor("thr", [1], F32, kind="ExternalInput").ap()
    d_gb = nc.dram_tensor("gb", [1], F32, kind="ExternalInput").ap()
    d_w1q = nc.dram_tensor("w1q", [128, 8, 8, 128], F32, kind="ExternalInput").ap()
    d_w1k = nc.dram_tensor("w1k", [128, 8, 8, 128], F32, kind="ExternalInput").ap()
    d_w1v = nc.dram_tensor("w1v", [128, 8, 8, 128], F32, kind="ExternalInput").ap()
    d_bq = nc.dram_tensor("bq", [128, 8], F32, kind="ExternalInput").ap()
    d_bk = nc.dram_tensor("bk", [128, 8], F32, kind="ExternalInput").ap()
    d_bv = nc.dram_tensor("bv", [128, 8], F32, kind="ExternalInput").ap()
    d_l1g = nc.dram_tensor("l1g", [128, 8], F32, kind="ExternalInput").ap()
    d_l1b = nc.dram_tensor("l1b", [128, 8], F32, kind="ExternalInput").ap()
    d_w1fg = nc.dram_tensor("w1fg", [HF, D], F32, kind="ExternalInput").ap()
    d_w1fu = nc.dram_tensor("w1fu", [HF, D], F32, kind="ExternalInput").ap()
    d_w1fd = nc.dram_tensor("w1fd", [D, HF], F32, kind="ExternalInput").ap()
    d_out = nc.dram_tensor("out", [T, D], F32, kind="ExternalOutput").ap()
    dbg_outs = {t: nc.dram_tensor(f"dbg_{t}", [T, D], F32, kind="ExternalOutput").ap()
                for t in debug_taps}

    s_w0g = nc.dram_tensor("s_w0g", [128, NH, 8, 128], BF16).ap()
    s_w0u = nc.dram_tensor("s_w0u", [128, NH, 8, 128], BF16).ap()
    s_w0d = nc.dram_tensor("s_w0d", [128, 8, NH, 128], BF16).ap()
    s_w1g = nc.dram_tensor("s_w1g", [128, NH, 8, 128], BF16).ap()
    s_w1u = nc.dram_tensor("s_w1u", [128, NH, 8, 128], BF16).ap()
    s_w1d = nc.dram_tensor("s_w1d", [128, 8, NH, 128], BF16).ap()
    s_up = nc.dram_tensor("s_up", [NH, 128, T], F32).ap()
    s_gate = nc.dram_tensor("s_gate", [NH, 128, T], F32).ap()
    s_ui = nc.dram_tensor("s_ui", [NH, 128, T], I16).ap()
    s_gi = nc.dram_tensor("s_gi", [NH, 128, T], I16).ap()
    s_o32 = nc.dram_tensor("s_o32", [NH, 128, T], I32).ap()
    cc_in = nc.dram_tensor("cc_in", [8], F32).ap()
    cc_out = nc.dram_tensor("cc_out", [8], F32).ap()
    d_sc = nc.dram_tensor("d_sc", [16], F32).ap()

    with ExitStack() as es:
        cx = Ctx(nc, es)
        rawblock = cx.block
        _prox = {}

        class _BlockWrap:
            def _mk(self, name):
                dec = getattr(rawblock, name)
                def deco(fn):
                    def inner(eng):
                        key = name
                        if key not in _prox:
                            _prox[key] = EngProxy(eng) if name in ("vector", "scalar", "gpsimd") else eng
                        return fn(_prox[key])
                    return dec(inner)
                return deco

            @property
            def vector(self):
                return self._mk("vector")

            @property
            def scalar(self):
                return self._mk("scalar")

            @property
            def gpsimd(self):
                return self._mk("gpsimd")

            @property
            def tensor(self):
                return rawblock.tensor

            @property
            def sync(self):
                return rawblock.sync

        block = _BlockWrap()

        def sbuf(name, shape, dt=F32):
            return es.enter_context(nc.sbuf_tensor(name, shape, dt))

        def salloc(name, shape, dt=F32):
            cm = nc.sbuf_tensor(name, shape, dt)
            return cm, cm.__enter__()

        def sfree(*cms):
            for cm in reversed(cms):
                cm.__exit__(None, None, None)

        def psum(name, shape, dt=F32):
            return es.enter_context(nc.psum_tensor(name, shape, dt))

        x_tm = sbuf("x_tm", [128, NT, D])
        ident = sbuf("ident", [128, 128])
        ones_bf = sbuf("ones_bf", [128, 1], BF16)
        cb = sbuf("cb", [128, 16])
        onesrow = sbuf("onesrow", [1, 128])
        stats = sbuf("stats", [128, 6 * NT])
        scal = sbuf("scal", [128, 24])
        vsml = sbuf("vsml", [128, 16])
        hsel = sbuf("hsel", [128, 8, 16], BF16)
        bqkv = sbuf("bqkv", [128, 24])
        gbet = sbuf("gbet", [128, 16])
        thr_sb = sbuf("thr_sb", [128, 2])
        xn_f = sbuf("xn_f", [128, 2, D])
        A1 = sbuf("A1", [128, 8, T], BF16)

        CB_HALF, CB_MAGIC, CB_I127, CB_ID, CB_EPS, CB_I7, CB_E12, CB_65536, \
            CB_65535, CB_I65536, CB_QM, CB_M05127, CB_I32R, CB_I8R = range(14)
        CB_VALS = [0.5, MAGIC, 1.0 / 127.0, 1.0 / 1024.0, EPS, 1.0 / 7.0, 1e-12,
                   65536.0, 65535.0, 1.0 / 65536.0, 1.0 / 32767.0, 0.5 / 127.0,
                   1.0 / 32.0, 0.125]

        def cbv(i):
            return cb[:, i:i + 1]

        pall = psum("pall", [128, 8, 512])
        pbank = [pall[:, i, :] for i in range(8)]

        @block.gpsimd
        def _(gp):
            for i, v in enumerate(CB_VALS):
                gp.memset(cb[:, i:i + 1], v)
            cx.inc(gp.memset(ident[:], 0.0), "gpi")
            gp.memset(ones_bf[:], 1.0)
            gp.memset(onesrow[:], 1.0)
            cx.inc(gp.memset(hsel[:], 0.0), "gpi")
            cx.wait(gp, "gpi")
            gp.affine_select(out=ident[:], in_=ident[:], compare_op=ALU.not_equal,
                             fill=1.0, base=0, pattern=[[-1, 128]], channel_multiplier=1)
            for oc in range(8):
                gp.memset(hsel[0:64, oc, 2 * oc:2 * oc + 1], 1.0)
                last = gp.memset(hsel[64:128, oc, 2 * oc + 1:2 * oc + 2], 1.0)
            cx.inc(last, "gp")

        @block.sync
        def _(sy):
            cx.inc(sy.dma_start(
                out=x_tm[:], in_=d_x.rearrange("(t p) d -> p t d", p=128)), "dmx", 16)
            cx.inc(sy.dma_start(out=bqkv[:, 0:8], in_=d_bq), "dsm", 16)
            cx.inc(sy.dma_start(out=bqkv[:, 8:16], in_=d_bk), "dsm", 16)
            cx.inc(sy.dma_start(out=bqkv[:, 16:24], in_=d_bv), "dsm", 16)
            cx.inc(sy.dma_start(out=gbet[:, 0:8], in_=d_l1g), "dsm", 16)
            cx.inc(sy.dma_start(out=gbet[:, 8:16], in_=d_l1b), "dsm", 16)
            cx.inc(sy.dma_start(out=thr_sb[:, 0:1],
                                in_=d_thr[None, :].partition_broadcast(128)), "dsm", 16)
            cx.inc(sy.dma_start(out=thr_sb[:, 1:2],
                                in_=d_gb[None, :].partition_broadcast(128)), "dsm", 16)

        @block.vector
        def _(ve):
            cx.wait(ve, "dsm")
            cx.inc(ve.tensor_scalar(thr_sb[:, 0:1], thr_sb[:, 0:1], 32.0, None,
                                    ALU.mult), "ve")

        def emit_transposes(jobs, te_pre_wait=None, ve_pre_wait=None):
            tb = cx.c("tpT")
            shp = [(j[0].shape[0], j[0].shape[1]) for j in jobs]

            @block.tensor
            def _(te):
                if te_pre_wait is not None:
                    te_pre_wait(te)
                for i in range(len(jobs)):
                    pc, fc = shp[i]
                    cx.wait(te, "tpE", tb + i - 1)
                    cx.inc(te.transpose(
                        pbank[6 + (tb + i) % 2][0:fc, 0:pc],
                        jobs[i][0], ident[0:pc, 0:pc]), "tpT")

            @block.vector
            def _(ve):
                if ve_pre_wait is not None:
                    ve_pre_wait(ve)
                for i in range(len(jobs)):
                    pc, fc = shp[i]
                    cx.wait(ve, "tpT", tb + i + 1)
                    cx.inc(jobs[i][1](ve, pbank[6 + (tb + i) % 2][0:fc, 0:pc]),
                           "tpE")

        def fence_entry():
            snap = {k: v for k, v in cx.cnt.items() if v > 0}

            @block.sync
            def _(sy):
                for k, v in snap.items():
                    cx.wait(sy, k, v)

            @block.gpsimd
            def _(gp):
                for k, v in snap.items():
                    cx.wait(gp, k, v)

        with nc.sbuf_tensor("w0buf", [128, NH * 8 * 128], BF16) as w0buf:
            @block.gpsimd
            def _(gp):
                for (src, dst) in [(d_w0g, s_w0g), (d_w0u, s_w0u), (d_w0d, s_w0d)]:
                    cx.inc(gp.dma_start(out=w0buf[:],
                                        in_=src.rearrange("p a b c -> p (a b c)")),
                           "dwc", 16)
                    cx.wait(gp, "dwc")
                    cx.inc(gp.dma_start(out=dst.rearrange("p a b c -> p (a b c)"),
                                        in_=w0buf[:]), "dwc", 16)
                    cx.wait(gp, "dwc")

        if True:
            with nc.sbuf_tensor("wnat", [128, 2, HF], F32) as wnat, \
                 nc.sbuf_tensor("wdq", [128, 2, HF], F32) as wdq, \
                 nc.sbuf_tensor("wblk", [128, 192], F32) as wblk, nc.sbuf_tensor("wstg2", [128, 192], F32) as wstg2, \
                 nc.sbuf_tensor("wstg", [128, NH, 128], BF16) as wstg:

                dq_slab = [0]
                dq_done = {}
                dq_tpt = {}

                def dequant_weight(src, dst, nslab, icols):
                    nblk = icols // 32
                    nic = icols // 128
                    for sl in range(nslab):
                        g = dq_slab[0]
                        dq_slab[0] += 1
                        bg = g % 2
                        use_gp = (g % 4 == 3)

                        @block.sync
                        def _(sy, sl=sl, bg=bg, g=g):
                            cx.wait(sy, "dwc")
                            if g - 2 in dq_done:
                                cx.wait(sy, *dq_done[g - 2])
                            cx.inc(sy.dma_start(out=wnat[:, bg, 0:icols],
                                                in_=src[sl * 128:(sl + 1) * 128, :]),
                                   f"dqL{bg}", 16)

                        @block.vector
                        def _(ve, sl=sl, bg=bg, g=g, nblk=nblk, icols=icols):
                            cx.wait(ve, "gp", 1)
                            cx.wait(ve, f"dqL{bg}")
                            if g - 2 in dq_done:
                                cx.wait(ve, *dq_done[g - 2])
                            w3 = wnat[:, bg, 0:icols].rearrange(
                                "p (b k) -> p b k", k=32)
                            bm = wblk[:, 96 * bg:96 * bg + nblk]
                            ve.nd.tensor_reduce(bm[:, :, None], w3, AX.X, ALU.max,
                                                apply_absolute_value=True)
                            ve.tensor_scalar(bm, bm, cbv(CB_I7),
                                             cbv(CB_E12), ALU.mult, ALU.max)
                            cx.inc(ve.reciprocal(
                                wstg2[:, 96 * bg:96 * bg + nblk], bm), "dqm")

                        engsec = block.gpsimd if use_gp else block.vector
                        dqm_tgt = cx.c("dqm")
                        dq_sem = "dqw" if use_gp else "dqv"

                        @engsec
                        def _(eng, sl=sl, bg=bg, g=g, nblk=nblk, icols=icols,
                              dqm_tgt=dqm_tgt, dq_sem=dq_sem):
                            cx.wait(eng, "dqm", dqm_tgt)
                            if g - 2 in dq_tpt:
                                cx.wait(eng, "tpT", dq_tpt[g - 2])
                            w3 = wnat[:, bg, 0:icols].rearrange(
                                "p (b k) -> p b k", k=32)
                            bm = wblk[:, 96 * bg:96 * bg + nblk]
                            w3d = wdq[:, bg, 0:icols].rearrange(
                                "p (b k) -> p b k", k=32)
                            eng.tensor_tensor(
                                w3d, w3,
                                wstg2[:, 96 * bg:96 * bg + nblk, None]
                                .broadcast_to([128, nblk, 32]), ALU.mult)
                            eng.tensor_scalar(wdq[:, bg, 0:icols],
                                              wdq[:, bg, 0:icols],
                                              cbv(CB_MAGIC), cbv(CB_MAGIC),
                                              ALU.add, ALU.subtract)
                            cx.inc(eng.tensor_tensor(
                                w3d, w3d, bm[:, :, None].broadcast_to([128, nblk, 32]),
                                ALU.mult), dq_sem)
                        dq_done[g] = (dq_sem, cx.c(dq_sem))

                        def mk_evac(c):
                            def f(ve, bank):
                                return ve.nd.tensor_copy(wstg[:, c, :], bank)
                            return f

                        dqv_target = cx.c(dq_sem)

                        def prew(te, tgt=dqv_target, dq_sem=dq_sem):
                            cx.wait(te, dq_sem, tgt)

                        dqo_tgt = cx.c("dqo")

                        def vprew(ve, tgt=dqo_tgt):
                            cx.wait(ve, "dqo", tgt)

                        emit_transposes(
                            [(wdq[:, bg, c * 128:(c + 1) * 128], mk_evac(c))
                             for c in range(nic)], te_pre_wait=prew,
                            ve_pre_wait=vprew)
                        dq_tpt[g] = cx.c("tpT")

                        @block.sync
                        def _(sy, sl=sl, nic=nic):
                            cx.wait(sy, "tpE")
                            cx.inc(sy.dma_start(out=dst[:, sl, 0:nic, :],
                                                in_=wstg[:, 0:nic, :]), "dqo", 16)

                dequant_weight(d_w1fg, s_w1g, NH, D)
                dequant_weight(d_w1fu, s_w1u, NH, D)
                dequant_weight(d_w1fd, s_w1d, 8, HF)

        def layernorm_stats():
            sums = stats[:, 0:NT]
            sumsq = stats[:, NT:2 * NT]

            @block.vector
            def _(ve):
                cx.wait(ve, "gp", 1)
                cx.wait(ve, "vres")
                cx.wait(ve, "dmx")
                cx.wait(ve, "tpT")
                for t in range(NT):
                    ve.nd.reduce_sum(sums[:, t:t + 1], x_tm[:, t, :], AX.X)
                for t in range(NT):
                    cx.inc(ve.scalar_tensor_tensor(
                        xn_f[:, t % 2, :], x_tm[:, t, :], 1.0, x_tm[:, t, :],
                        ALU.mult, ALU.mult, accum_out=sumsq[:, t:t + 1]), "lnsq")
                mu = stats[:, 2 * NT:3 * NT]
                var = stats[:, 3 * NT:4 * NT]
                ve.tensor_scalar(mu, sums, cbv(CB_ID), None, ALU.mult)
                ve.tensor_scalar(var, sumsq, cbv(CB_ID), None, ALU.mult)
                ve.tensor_tensor(sumsq, mu, mu, ALU.mult)
                ve.tensor_tensor(var, var, sumsq, ALU.subtract)
                cx.inc(ve.tensor_scalar(var, var, cbv(CB_EPS), None, ALU.add), "lnv")

            @block.scalar
            def _(se):
                cx.wait(se, "lnv")
                cx.inc(se.activation(stats[:, 5 * NT:6 * NT],
                                     stats[:, 3 * NT:4 * NT], ACT.Sqrt), "lnsr")

            @block.vector
            def _(ve):
                cx.wait(ve, "lnsr")
                var = stats[:, 3 * NT:4 * NT]
                r = stats[:, 4 * NT:5 * NT]
                sq = stats[:, 5 * NT:6 * NT]
                t1 = stats[:, 0:NT]
                ve.reciprocal(r, sq)
                ve.tensor_tensor(t1, r, r, ALU.mult)
                ve.tensor_tensor(t1, t1, var, ALU.mult)
                ve.tensor_scalar(t1, t1, -0.5, 1.5, ALU.mult, ALU.add)
                cx.inc(ve.tensor_tensor(r, r, t1, ALU.mult), "lnr")

        def MU(t):
            return stats[:, 2 * NT + t:2 * NT + t + 1]

        def RS(t):
            return stats[:, 4 * NT + t:4 * NT + t + 1]

        def normalize_to_fm(dst, dst_tm=None):
            for t in range(NT):
                @block.vector
                def _(ve, t=t):
                    cx.wait(ve, "lnr")
                    cx.wait(ve, "tpT", cx.c("tpT") - 8)
                    cx.inc(ve.nd.scalar_tensor_tensor(
                        xn_f[:, t % 2, :], x_tm[:, t, :], MU(t),
                        RS(t).broadcast_to([128, D]), ALU.subtract, ALU.mult), "xn")
                    if dst_tm is not None:
                        cx.inc(ve.tensor_copy(dst_tm[:, t, :], xn_f[:, t % 2, :]),
                               "xntm")

                def mk_evac(t, c):
                    def f(ve, bank):
                        return ve.tensor_copy(dst[:, c, t * 128:(t + 1) * 128], bank)
                    return f

                xn_target = cx.c("xn")

                def prew(te, tgt=xn_target):
                    cx.wait(te, "xn", tgt)

                emit_transposes(
                    [(xn_f[:, t % 2, c * 128:(c + 1) * 128], mk_evac(t, c))
                     for c in range(8)], te_pre_wait=prew)

        def down_half(HIDb, wsrc, wd_sb, res_scale, th):
            for dh in range(2):
                vh_all = cx.c("vh")
                vres_prev = cx.c("vres")
                for c in range(NH):
                    di = cx.c("tdn")
                    par = di % 2

                    @block.sync
                    def _(sy, c=c, dh=dh, par=par, di=di):
                        cx.wait(sy, "dwc")
                        cx.wait(sy, "so0")
                        cx.wait(sy, "so1")
                        cx.wait(sy, "tdn", di - 1)
                        cx.inc(sy.dma_start(
                            out=wd_sb[:, par],
                            in_=wsrc[:, 4 * dh:4 * (dh + 1), c, :]),
                            f"wd{par}", 16)

                    @block.tensor
                    def _(te, c=c, dh=dh, par=par, vh_all=vh_all,
                          vres_prev=vres_prev):
                        cx.wait(te, f"wd{par}")
                        if c == 0:
                            cx.wait(te, "vh", vh_all)
                            cx.wait(te, "vres", vres_prev)
                        last = None
                        for t in range(8):
                            last = te.matmul(
                                pbank[t][:],
                                HIDb[:, c, t * 128:(t + 1) * 128],
                                wd_sb[:, par].rearrange("p a b -> p (a b)"),
                                start=(c == 0), stop=(c == NH - 1))
                        cx.inc(last, "tdn")

                @block.vector
                def _(ve, dh=dh, th=th, res_scale=res_scale):
                    cx.wait(ve, "tdn")
                    for t in range(8):
                        tt = th * 8 + t
                        cx.inc(ve.nd.scalar_tensor_tensor(
                            x_tm[:, tt, dh * 512:(dh + 1) * 512],
                            pbank[t][:], res_scale,
                            x_tm[:, tt, dh * 512:(dh + 1) * 512],
                            ALU.mult, ALU.add), "vres")

        def l0_ffn(HIDb, wg_sb, wu_sb, wd_sb, sil):
            fence_entry()
            layernorm_stats()
            normalize_to_fm(A1)
            xfE_all = cx.c("tpE")

            for th in range(2):
                tc0 = th * 1024
                gumm_base = cx.c("gumm")
                sil_base = cx.c("sil")
                vh_base = cx.c("vh")
                for ht in range(NH):
                    gi = cx.c("gumm")
                    par = gi % 2

                    @block.sync
                    def _(sy, ht=ht, par=par, gi=gi):
                        cx.wait(sy, "dwc")
                        cx.wait(sy, "gumm", gi - 1)
                        cx.inc(sy.dma_start(out=wg_sb[:, par], in_=s_w0g[:, ht]),
                               f"wg{par}", 16)
                        cx.inc(sy.dma_start(out=wu_sb[:, par], in_=s_w0u[:, ht]),
                               f"wu{par}", 16)

                    @block.tensor
                    def _(te, ht=ht, par=par, gi=gi, tc0=tc0, vh_base=vh_base,
                          lht=ht):
                        cx.wait(te, f"wg{par}")
                        cx.wait(te, f"wu{par}")
                        cx.wait(te, "tpE", xfE_all)
                        cx.wait(te, "vh", vh_base + 2 * (lht - 1))
                        cx.wait(te, "vres")
                        bs = 4 * (gi % 2)
                        last = None
                        for t5 in range(2):
                            pg, pu = pall[:, bs + 2 * t5, :], pall[:, bs + 2 * t5 + 1, :]
                            for k in range(8):
                                te.matmul(pg, wg_sb[:, par, k, :],
                                          A1[:, k, tc0 + t5 * 512:tc0 + (t5 + 1) * 512],
                                          start=(k == 0), stop=(k == 7))
                            for k in range(8):
                                last = te.matmul(
                                    pu, wu_sb[:, par, k, :],
                                    A1[:, k, tc0 + t5 * 512:tc0 + (t5 + 1) * 512],
                                    start=(k == 0), stop=(k == 7))
                        cx.inc(last, "gumm")

                @block.scalar
                def _(se, th=th, gumm_base=gumm_base, vh_base=vh_base):
                    for lht in range(NH):
                        gi = gumm_base + lht
                        cx.wait(se, "gumm", gi + 1)
                        cx.wait(se, "vh", vh_base + 2 * (lht - 1))
                        bs = 4 * (gi % 2)
                        for t5 in range(2):
                            cx.inc(se.nd.activation(
                                sil[:, 2 * (gi % 2) + t5, :],
                                pall[:, bs + 2 * t5, :], ACT.Silu,
                                scale=cbv(CB_I127)), "sil")

                @block.vector
                def _(ve, th=th, gumm_base=gumm_base, sil_base=sil_base):
                    for lht in range(NH):
                        gi = gumm_base + lht
                        cx.wait(ve, "sil", sil_base + 2 * lht + 2)
                        bs = 4 * (gi % 2)
                        for t5 in range(2):
                            cx.inc(ve.nd.scalar_tensor_tensor(
                                HIDb[:, lht, t5 * 512:(t5 + 1) * 512],
                                pall[:, bs + 2 * t5 + 1, :], cbv(CB_I127),
                                sil[:, 2 * (gi % 2) + t5, :],
                                ALU.mult, ALU.mult), "vh")

                down_half(HIDb, s_w0d, wd_sb, cbv(CB_M05127), th)

        def l0_attn(A2, lrb, clb, crb, sql, lrow, rl0):
            fence_entry()
            layernorm_stats()
            normalize_to_fm(A1, dst_tm=A2)

            @block.gpsimd
            def _(gp):
                cx.inc(gp.dma_start(out=clb[:], in_=d_cl), "dcl", 16)
                cx.inc(gp.dma_start(out=crb[:], in_=d_cr), "dcl", 16)

            xn_all = cx.c("tpE")
            lr_jobs = []
            for m, wsb in ((0, clb), (1, crb)):
                for kc in range(2):
                    for t5 in range(4):
                        lr_jobs.append((m, wsb, kc, t5))
            for idx, (m, wsb, kc, t5) in enumerate(lr_jobs):
                @block.tensor
                def _(te, idx=idx, m=m, wsb=wsb, kc=kc, t5=t5):
                    if idx == 0:
                        cx.wait(te, "dcl")
                        cx.wait(te, "tpE", xn_all)
                    cx.wait(te, "lre", idx - 1)
                    last = None
                    for c in range(8):
                        last = te.matmul(pbank[idx % 2][:],
                                         wsb[:, c, kc * 128:(kc + 1) * 128],
                                         A1[:, c, t5 * 512:(t5 + 1) * 512],
                                         start=(c == 0), stop=(c == 7))
                    cx.inc(last, "lrmm")

                @block.scalar
                def _(se, idx=idx, m=m, kc=kc, t5=t5):
                    cx.wait(se, "lrmm", idx + 1)
                    cx.inc(se.nd.activation(
                        lrb[:, 2 * m + kc, t5 * 512:(t5 + 1) * 512],
                        pbank[idx % 2][:], ACT.Copy), "lre")

            for m in range(2):
                for t5 in range(4):
                    for kc in range(2):
                        @block.vector
                        def _(ve, m=m, kc=kc, t5=t5):
                            cx.wait(ve, "lre")
                            cx.wait(ve, "l0nm", cx.c("l0nm"))
                            src_ap = lrb[:, 2 * m + kc, t5 * 512:(t5 + 1) * 512]
                            cx.inc(ve.scalar_tensor_tensor(
                                sql[:, kc, :], src_ap, 1.0, src_ap,
                                ALU.mult, ALU.mult), "l0sq")

                    @block.tensor
                    def _(te, m=m, t5=t5):
                        cx.wait(te, "l0sq")
                        cx.wait(te, "l0ne", cx.c("l0ne"))
                        last = None
                        for kc in range(2):
                            last = te.matmul(pbank[2][0:1, :], ones_bf[:],
                                             sql[:, kc, :], start=(kc == 0),
                                             stop=(kc == 1))
                        cx.inc(last, "l0nm")

                    @block.vector
                    def _(ve, m=m, t5=t5):
                        cx.wait(ve, "l0nm")
                        cx.inc(ve.tensor_copy(
                            lrow[0:1, (4 * m + t5) * 512:
                                 (4 * m + t5 + 1) * 512],
                            pbank[2][0:1, :]), "l0ne")

            @block.vector
            def _(ve):
                cx.wait(ve, "l0ne")
                ve.reduce_max(rl0[0:1, 0:1], lrow[0:1, 0:T], AX.X)
                cx.inc(ve.reduce_max(rl0[0:1, 1:2], lrow[0:1, T:2 * T], AX.X),
                       "l0nx")

            @block.scalar
            def _(se):
                cx.wait(se, "l0nx")
                cx.inc(se.activation(rl0[0:1, 2:4], rl0[0:1, 0:2], ACT.Sqrt),
                       "l0bs")

            @block.vector
            def _(ve):
                cx.wait(ve, "l0bs")
                ve.tensor_tensor(rl0[0:1, 4:5], rl0[0:1, 2:3], rl0[0:1, 3:4],
                                 ALU.mult)
                cx.inc(ve.tensor_scalar(rl0[0:1, 4:5], rl0[0:1, 4:5],
                                        -1.03 / 32.0, None, ALU.mult), "l0bv")

            @block.tensor
            def _(te):
                cx.wait(te, "l0bv")
                cx.inc(te.matmul(pbank[2][:, 0:1], onesrow[:], rl0[0:1, 4:5],
                                 start=True, stop=True), "l0bb")

            @block.vector
            def _(ve):
                cx.wait(ve, "l0bb")
                cx.inc(ve.tensor_copy(scal[:, 0:1], pbank[2][:, 0:1]), "l0bc")

            ETb = A1[:, 0:4, :].rearrange("p a b -> p (a b)").rearrange(
                "p (j i) -> p j i", i=512)
            METb = A1[:, 4:8, :].rearrange("p a b -> p (a b)").rearrange(
                "p (j i) -> p j i", i=512)
            for isl in range(4):
                e_base = cx.c("l0e")
                m_base = cx.c("l0m")
                l_prev = cx.c("l0l")
                pv_prev = cx.c("l0pv")
                s_base = cx.c("l0s")
                for j in range(NT):
                    @block.tensor
                    def _(te, isl=isl, j=j):
                        if j == 0 and isl == 0:
                            cx.wait(te, "l0bc")
                        cx.wait(te, "l0e", e_base + j // 2 - 1)
                        cx.wait(te, "l0m", m_base + j // 2 - 1)
                        last = None
                        for kc in range(2):
                            last = te.matmul(pall[:, j % 4, :],
                                             lrb[:, 2 + kc, j * 128:(j + 1) * 128],
                                             lrb[:, kc, isl * 512:(isl + 1) * 512],
                                             start=(kc == 0), stop=(kc == 1))
                        cx.inc(last, "l0s")

                @block.scalar
                def _(se, isl=isl, s_base=s_base, l_prev=l_prev, e_base=e_base):
                    cx.wait(se, "l0l", l_prev)
                    for p in range(8):
                        cx.wait(se, "l0s", s_base + 2 * p + 2)
                        cx.wait(se, "l0m", m_base + p - 1)
                        cx.inc(se.nd.activation(
                            ETb[:, 2 * p:2 * p + 2, :],
                            pall[:, (2 * p) % 4:(2 * p) % 4 + 2, :],
                            ACT.Exp, bias=scal[:, 0:1],
                            scale=cbv(CB_I32R)), "l0e")

                @block.vector
                def _(ve, isl=isl, e_base=e_base):
                    cx.wait(ve, "l0pv", pv_prev)
                    for p in range(8):
                        cx.wait(ve, "l0e", e_base + p + 1)
                        cx.inc(ve.nd.scalar_tensor_tensor(
                            METb[:, 2 * p:2 * p + 2, :],
                            pall[:, (2 * p) % 4:(2 * p) % 4 + 2, :],
                            thr_sb[:, 0:1],
                            ETb[:, 2 * p:2 * p + 2, :], ALU.is_gt, ALU.mult),
                            "l0m")

                @block.tensor
                def _(te, isl=isl):
                    cx.wait(te, "l0e")
                    cx.wait(te, "l0r", cx.c("l0r"))
                    cx.wait(te, "l0bc", cx.c("l0bc"))
                    for ic in range(4):
                        last = None
                        for j in range(NT):
                            last = te.matmul(pall[:, 4, ic:ic + 1],
                                             ETb[:, j, ic * 128:(ic + 1) * 128],
                                             ones_bf[:], start=(j == 0),
                                             stop=(j == NT - 1))
                        cx.inc(last, "l0l")

                @block.vector
                def _(ve, isl=isl):
                    cx.wait(ve, "l0l")
                    cx.inc(ve.reciprocal(rl0[:, 8 + isl * 4:8 + isl * 4 + 4],
                                         pall[:, 4, 0:4]), "l0r")

                for ic in range(4):
                    @block.tensor
                    def _(te, isl=isl, ic=ic):
                        cx.wait(te, "l0m")
                        cx.wait(te, "l0r", cx.c("l0r"))
                        cx.wait(te, "vres")
                        last = None
                        for dhh in range(2):
                            for j in range(NT):
                                last = te.matmul(
                                    pall[:, 5 + dhh, :],
                                    METb[:, j, ic * 128:(ic + 1) * 128],
                                    A2[:, j, dhh * 512:(dhh + 1) * 512],
                                    start=(j == 0), stop=(j == NT - 1))
                        cx.inc(last, "l0pv")

                    @block.vector
                    def _(ve, isl=isl, ic=ic):
                        cx.wait(ve, "l0pv")
                        tt = isl * 4 + ic
                        for dhh in range(2):
                            cx.inc(ve.nd.scalar_tensor_tensor(
                                x_tm[:, tt, dhh * 512:(dhh + 1) * 512],
                                pall[:, 5 + dhh, :],
                                rl0[:, 8 + tt:8 + tt + 1],
                                x_tm[:, tt, dhh * 512:(dhh + 1) * 512],
                                ALU.mult, ALU.add), "vres")


        ccn = [0]

        def barrier(brow_k, dst_slots, prep_sem):
            k = brow_k
            n = ccn[0]
            ccn[0] += 1
            cci = nc.dram_tensor(f"cci{n}", [k], F32).ap()
            cco = nc.dram_tensor(f"cco{n}", [k], F32).ap()

            @block.sync
            def _(sy):
                cx.wait(sy, prep_sem[0], prep_sem[1])
                cx.inc(sy.dma_start(out=cci, in_=vsml[0:1, 8:8 + k]), "cds", 16)

            @block.gpsimd
            def _(gp):
                cx.wait(gp, "cds")
                cx.inc(gp.collective_compute(
                    "AllReduce", ALU.max, replica_groups=[list(range(cc_world))],
                    ins=[cci], outs=[cco]), "ccl")

            @block.sync
            def _(sy):
                cx.wait(sy, "ccl")
                for i, sl in enumerate(dst_slots):
                    cx.inc(sy.dma_start(
                        out=scal[:, sl:sl + 1],
                        in_=cco[i:i + 1][None, :].partition_broadcast(128)),
                        "cbb", 16)

        def part_max(srcs, dst_cols, src_sem):
            def mk(i):
                def f(ve, bank):
                    return ve.reduce_max(vsml[0:1, dst_cols[i]:dst_cols[i] + 1],
                                         bank[0:1, 0:128], AX.X)
                return f

            def prew(te, ss=src_sem):
                cx.wait(te, ss[0], ss[1])
            emit_transposes([(s, mk(i)) for i, s in enumerate(srcs)],
                            te_pre_wait=prew)

        def l1_ffn(tag):
            fence_entry()
            layernorm_stats()

            @block.vector
            def _(ve):
                cx.wait(ve, "lnr")
                xmx = stats[:, 0:NT]
                xmn = stats[:, NT:2 * NT]
                for t in range(NT):
                    ve.reduce_max(xmx[:, t:t + 1], x_tm[:, t, :], AX.X)
                    ve.tensor_reduce(xmn[:, t:t + 1], x_tm[:, t, :], AX.X, ALU.min)
                mu = stats[:, 2 * NT:3 * NT]
                r = stats[:, 4 * NT:5 * NT]
                sq = stats[:, 5 * NT:6 * NT]
                ve.tensor_tensor(xmx, xmx, mu, ALU.subtract)
                ve.tensor_tensor(xmn, mu, xmn, ALU.subtract)
                ve.tensor_tensor(xmx, xmx, xmn, ALU.max)
                ve.tensor_tensor(xmx, xmx, r, ALU.mult)
                cx.inc(ve.reduce_max(vsml[:, 0:1], xmx, AX.X), "f1x")

            prew_tgt = cx.c("f1x")

            def prew(te, tgt=prew_tgt):
                cx.wait(te, "f1x", tgt)
            emit_transposes([(vsml[:, 0:1],
                              lambda ve, bank: ve.reduce_max(
                                  vsml[0:1, 8:9], bank[0:1, 0:128], AX.X))],
                            te_pre_wait=prew)
            barrier(1, [1], ("tpE", cx.c("tpE")))

            @block.vector
            def _(ve):
                cx.wait(ve, "cbb")
                ve.tensor_scalar(scal[:, 2:3], scal[:, 1:2], cbv(CB_QM),
                                 cbv(CB_E12), ALU.mult, ALU.max)
                ve.reciprocal(scal[:, 3:4], scal[:, 2:3])
                A = stats[:, 0:NT]
                B = stats[:, NT:2 * NT]
                ve.tensor_scalar(A, stats[:, 4 * NT:5 * NT], scal[:, 3:4], None,
                                 ALU.mult)
                ve.tensor_tensor(B, stats[:, 2 * NT:3 * NT], A, ALU.mult)
                cx.inc(ve.tensor_scalar(B, B, -1.0, None, ALU.mult), "f1d")

            for t in range(NT):
                @block.vector
                def _(ve, t=t):
                    cx.wait(ve, "f1d")
                    cx.wait(ve, "tpT", cx.c("tpT") - 8)
                    ve.tensor_scalar(xn_f[:, t % 2, :], x_tm[:, t, :],
                                     stats[:, t:t + 1],
                                     stats[:, NT + t:NT + t + 1],
                                     ALU.mult, ALU.add)
                    cx.inc(ve.tensor_scalar(xn_f[:, t % 2, :], xn_f[:, t % 2, :],
                                            cbv(CB_MAGIC), cbv(CB_MAGIC),
                                            ALU.add, ALU.subtract), "xn")

                def mk_evac(t, c):
                    def f(ve, bank):
                        return ve.tensor_copy(A1[:, c, t * 128:(t + 1) * 128], bank)
                    return f

                xn_target = cx.c("xn")

                def prew2(te, tgt=xn_target):
                    cx.wait(te, "xn", tgt)

                emit_transposes(
                    [(xn_f[:, t % 2, c * 128:(c + 1) * 128], mk_evac(t, c))
                     for c in range(8)], te_pre_wait=prew2)

            cmA = [salloc(f"{tag}stgu", [128, 2, T], F32),
                   salloc(f"{tag}stgg", [128, 2, T], F32),
                   salloc(f"{tag}wg", [128, 2, 8, 128], BF16),
                   salloc(f"{tag}wu", [128, 2, 8, 128], BF16)]
            stgu, stgg, wg_sb, wu_sb = [t for _, t in cmA]

            @block.gpsimd
            def _(gp):
                gp.memset(vsml[:, 3:4], 0.0)
                cx.inc(gp.memset(vsml[:, 5:6], 0.0), "f1z")

            xfE_all = cx.c("tpE")
            f1g5_base = cx.c("f1g5")
            sue_base = cx.c("sue")
            spu_base = {0: cx.c("spu0"), 1: cx.c("spu1")}
            spg_base = {0: cx.c("spg0"), 1: cx.c("spg1")}
            sum_base = cx.c("sum")
            for ht in range(NH):
                par = ht % 2
                g_base = cx.c("f1g5")

                @block.sync
                def _(sy, ht=ht, par=par, g_base=g_base):
                    cx.wait(sy, "dqo")
                    cx.wait(sy, "f1g5", g_base - 4)
                    cx.inc(sy.dma_start(out=wg_sb[:, par], in_=s_w1g[:, ht]),
                           f"wg{par}", 16)
                    cx.inc(sy.dma_start(out=wu_sb[:, par], in_=s_w1u[:, ht]),
                           f"wu{par}", 16)

                for t5 in range(4):
                    g5 = cx.c("f1g5")

                    @block.tensor
                    def _(te, ht=ht, par=par, t5=t5, g5=g5):
                        if t5 == 0:
                            cx.wait(te, f"wg{par}")
                            cx.wait(te, f"wu{par}")
                            cx.wait(te, "tpE", xfE_all)
                        cx.wait(te, "sue", 2 * (g5 - 3))
                        bk = 2 * (g5 % 4)
                        last = None
                        for k in range(8):
                            te.matmul(pall[:, bk, :], wg_sb[:, par, k, :],
                                      A1[:, k, t5 * 512:(t5 + 1) * 512],
                                      start=(k == 0), stop=(k == 7))
                        for k in range(8):
                            last = te.matmul(pall[:, bk + 1, :], wu_sb[:, par, k, :],
                                             A1[:, k, t5 * 512:(t5 + 1) * 512],
                                             start=(k == 0), stop=(k == 7))
                        cx.inc(last, "f1g5")

                @block.sync
                def _(sy, ht=ht, par=par, sue_base=sue_base):
                    cx.wait(sy, "sue", sue_base + 8 * (ht + 1))
                    cx.inc(sy.dma_start(out=s_up[ht], in_=stgu[:, par, :]),
                           f"spu{par}", 16)
                    cx.inc(sy.dma_start(out=s_gate[ht], in_=stgg[:, par, :]),
                           f"spg{par}", 16)

            @block.scalar
            def _(se, f1g5_base=f1g5_base, spu_base=spu_base, spg_base=spg_base,
                  sum_base=sum_base):
                for ht in range(NH):
                    par = ht % 2
                    for t5 in range(4):
                        g5 = f1g5_base + 4 * ht + t5
                        cx.wait(se, "f1g5", g5 + 1)
                        if t5 == 0:
                            cx.wait(se, f"spu{par}",
                                    spu_base[par] + 16 * (ht // 2))
                            cx.wait(se, f"spg{par}",
                                    spg_base[par] + 16 * (ht // 2))
                            cx.wait(se, "sum", sum_base + ht - 1)
                        bk = 2 * (g5 % 4)
                        cx.inc(se.nd.activation(
                            stgg[:, par, t5 * 512:(t5 + 1) * 512],
                            pall[:, bk, :], ACT.Copy), "sue")
                        cx.inc(se.nd.activation(
                            stgu[:, par, t5 * 512:(t5 + 1) * 512],
                            pall[:, bk + 1, :], ACT.Copy), "sue")

            @block.vector
            def _(ve, sue_base=sue_base):
                cx.wait(ve, "f1z", cx.c("f1z"))
                for ht in range(NH):
                    par = ht % 2
                    cx.wait(ve, "sue", sue_base + 8 * (ht + 1))
                    ve.nd.tensor_reduce(vsml[:, 2:3], stgu[:, par, :], AX.X,
                                        ALU.max, apply_absolute_value=True)
                    ve.tensor_tensor(vsml[:, 3:4], vsml[:, 3:4], vsml[:, 2:3],
                                     ALU.max)
                    ve.nd.tensor_reduce(vsml[:, 4:5], stgg[:, par, :], AX.X,
                                        ALU.max, apply_absolute_value=True)
                    cx.inc(ve.tensor_tensor(vsml[:, 5:6], vsml[:, 5:6],
                                            vsml[:, 4:5], ALU.max), "sum")


            part_max([vsml[:, 3:4], vsml[:, 5:6]], [8, 9], ("sum", cx.c("sum")))
            barrier(2, [4, 5], ("tpE", cx.c("tpE")))

            @block.vector
            def _(ve):
                cx.wait(ve, "cbb")
                ve.tensor_tensor(vsml[:, 6:7], scal[:, 4:5], scal[:, 2:3], ALU.mult)
                ve.tensor_scalar(scal[:, 6:7], vsml[:, 6:7], cbv(CB_QM),
                                 cbv(CB_E12), ALU.mult, ALU.max)
                ve.reciprocal(vsml[:, 7:8], scal[:, 6:7])
                ve.tensor_tensor(scal[:, 8:9], scal[:, 2:3], vsml[:, 7:8], ALU.mult)
                ve.tensor_tensor(vsml[:, 6:7], scal[:, 5:6], scal[:, 2:3], ALU.mult)
                ve.tensor_scalar(scal[:, 7:8], vsml[:, 6:7], cbv(CB_QM),
                                 cbv(CB_E12), ALU.mult, ALU.max)
                ve.reciprocal(vsml[:, 7:8], scal[:, 7:8])
                cx.inc(ve.tensor_tensor(scal[:, 9:10], scal[:, 2:3], vsml[:, 7:8],
                                        ALU.mult), "f1d2")

            sfree(*[cm for cm, _ in cmA])
            cmT = [salloc(f"{tag}ts", [128, 4, T], F32),
                   salloc(f"{tag}iu", [128, 2, T], I16),
                   salloc(f"{tag}ig", [128, 2, T], I16)]
            tsc, iub, igb = [t for _, t in cmT]
            cmB = [salloc(f"{tag}ru", [128, 2, T], F32),
                   salloc(f"{tag}rg", [128, 2, T], F32)]
            ru, rg = [t for _, t in cmB]

            @block.gpsimd
            def _(gp):
                cx.inc(gp.memset(vsml[:, 11:12], 0.0), "f1z2")

            f1a_base = cx.c("f1a")
            for ht in range(NH):
                par = ht % 2
                a_base = cx.c("f1a")

                @block.sync
                def _(sy, ht=ht, par=par, a_base=a_base):
                    cx.wait(sy, f"spu{par}")
                    cx.wait(sy, f"spg{par}")
                    cx.wait(sy, "f1a", a_base - 1)
                    cx.inc(sy.dma_start(out=ru[:, par, :], in_=s_up[ht]),
                           f"ru{par}", 16)
                    cx.inc(sy.dma_start(out=rg[:, par, :], in_=s_gate[ht]),
                           f"rg{par}", 16)

                @block.scalar
                def _(se, ht=ht, par=par, f1a_base=f1a_base):
                    cx.wait(se, f"rg{par}")
                    cx.wait(se, "f1d2", cx.c("f1d2"))
                    cx.wait(se, "f1a", f1a_base + ht - 1)
                    if ht < 2:
                        for s in ("spu0", "spu1", "spg0", "spg1"):
                            cx.wait(se, s)
                    cx.inc(se.activation(tsc[:, 2 * par, :], rg[:, par, :],
                                         ACT.Identity,
                                         bias=cbv(CB_MAGIC), scale=scal[:, 9:10]),
                           "f1p")

                @block.vector
                def _(ve, ht=ht, par=par):
                    cx.wait(ve, f"ru{par}")
                    cx.wait(ve, "f1p", cx.c("f1p"))
                    cx.wait(ve, "f1z2", cx.c("f1z2"))
                    if ht < 2:
                        for s in ("spu0", "spu1", "spg0", "spg1"):
                            cx.wait(ve, s)
                    if ht >= 2:
                        cx.wait(ve, f"siu{par}")
                        cx.wait(ve, f"sig{par}")
                    A = tsc[:, 2 * par, :]
                    B = tsc[:, 2 * par + 1, :]
                    ve.tensor_scalar(ru[:, par, :], ru[:, par, :], scal[:, 8:9],
                                     cbv(CB_MAGIC), ALU.mult, ALU.add)
                    ve.tensor_scalar(iub[:, par, :], ru[:, par, :], cbv(CB_MAGIC),
                                     None, ALU.subtract)
                    ve.tensor_scalar(B, A, cbv(CB_MAGIC), None, ALU.subtract)
                    cx.inc(ve.tensor_scalar(igb[:, par, :], A,
                                            cbv(CB_MAGIC), None, ALU.subtract),
                           "f1q1")

                @block.scalar
                def _(se, ht=ht, par=par):
                    cx.wait(se, "f1q1", cx.c("f1q1"))
                    cx.inc(se.activation(tsc[:, 2 * par, :], tsc[:, 2 * par + 1, :],
                                         ACT.Sigmoid,
                                         scale=scal[:, 7:8]), "f1sg")

                @block.vector
                def _(ve, ht=ht, par=par):
                    cx.wait(ve, "f1sg", cx.c("f1sg"))
                    A = tsc[:, 2 * par, :]
                    B = tsc[:, 2 * par + 1, :]
                    ve.tensor_scalar(A, A, cbv(CB_65536),
                                     cbv(CB_MAGIC), ALU.mult, ALU.add)
                    ve.tensor_scalar(A, A, cbv(CB_MAGIC),
                                     cbv(CB_65535), ALU.subtract, ALU.min)
                    ve.tensor_tensor(A, A, B, ALU.mult)
                    ve.tensor_reduce(vsml[:, 10:11], A, AX.X, ALU.max,
                                     apply_absolute_value=True)
                    cx.inc(ve.tensor_tensor(vsml[:, 11:12], vsml[:, 11:12],
                                            vsml[:, 10:11], ALU.max), "f1a")

                @block.sync
                def _(sy, ht=ht, par=par):
                    cx.wait(sy, "f1a")
                    cx.inc(sy.dma_start(out=s_ui[ht], in_=iub[:, par, :]),
                           f"siu{par}", 16)
                    cx.inc(sy.dma_start(out=s_gi[ht], in_=igb[:, par, :]),
                           f"sig{par}", 16)

            b2_base = cx.c("f1b")
            for ht in range(2):
                par = ht % 2

                @block.gpsimd
                def _(gp, ht=ht, par=par):
                    cx.wait(gp, f"siu{par}")
                    cx.wait(gp, f"sig{par}")
                    cx.inc(gp.dma_start(out=tsc[:, 2 * par, :], in_=s_ui[ht]),
                           f"riu{par}", 16)
                    cx.inc(gp.dma_start(out=tsc[:, 2 * par + 1, :], in_=s_gi[ht]),
                           f"rig{par}", 16)

            part_max([vsml[:, 11:12]], [8], ("f1a", cx.c("f1a")))
            barrier(1, [11], ("tpE", cx.c("tpE")))

            @block.vector
            def _(ve):
                cx.wait(ve, "cbb")
                ve.tensor_tensor(vsml[:, 6:7], scal[:, 11:12], scal[:, 7:8],
                                 ALU.mult)
                ve.tensor_scalar(vsml[:, 6:7], vsml[:, 6:7], cbv(CB_I65536), None,
                                 ALU.mult)
                ve.tensor_scalar(scal[:, 12:13], vsml[:, 6:7], cbv(CB_QM),
                                 cbv(CB_E12), ALU.mult, ALU.max)
                ve.reciprocal(vsml[:, 7:8], scal[:, 12:13])
                ve.tensor_tensor(vsml[:, 7:8], vsml[:, 7:8], scal[:, 7:8], ALU.mult)
                cx.inc(ve.tensor_scalar(scal[:, 13:14], vsml[:, 7:8],
                                        cbv(CB_I65536), None, ALU.mult), "f1d3")

            sfree(*[cm for cm, _ in cmB])
            cmC = [salloc(f"{tag}o32", [128, 2, T], I32),
                   salloc(f"{tag}sgm", [128, 2, T], F32)]
            o32b = cmC[0][1]
            sgm = cmC[1][1]

            @block.gpsimd
            def _(gp):
                cx.inc(gp.memset(vsml[:, 13:14], 0.0), "f1z3")

            for ht in range(NH):
                par = ht % 2

                if ht >= 2:
                    @block.gpsimd
                    def _(gp, ht=ht, par=par, b2_base=b2_base):
                        cx.wait(gp, f"siu{par}")
                        cx.wait(gp, f"sig{par}")
                        cx.wait(gp, "f1b", b2_base + ht - 1)
                        cx.inc(gp.dma_start(out=tsc[:, 2 * par, :], in_=s_ui[ht]),
                               f"riu{par}", 16)
                        cx.inc(gp.dma_start(out=tsc[:, 2 * par + 1, :],
                                            in_=s_gi[ht]),
                               f"rig{par}", 16)

                @block.vector
                def _(ve, ht=ht, par=par):
                    cx.wait(ve, f"riu{par}")
                    cx.wait(ve, f"rig{par}")
                    cx.wait(ve, "f1d3", cx.c("f1d3"))
                    cx.wait(ve, "f1z3", cx.c("f1z3"))
                    if ht >= 2:
                        cx.wait(ve, f"so{par}")
                    cx.inc(ve.nd.memset(vsml[:, 9:10], 0.0), "f1q2")

                @block.scalar
                def _(se, ht=ht, par=par, b2_base=b2_base):
                    cx.wait(se, "f1q2", cx.c("f1q2"))
                    cx.wait(se, "f1b", b2_base + ht - 1)
                    cx.inc(se.activation(sgm[:, par, :], tsc[:, 2 * par + 1, :],
                                         ACT.Sigmoid,
                                         scale=scal[:, 7:8]), "f1sh")

                @block.vector
                def _(ve, ht=ht, par=par):
                    cx.wait(ve, "f1sh", cx.c("f1sh"))
                    X = sgm[:, par, :]
                    ve.tensor_scalar(X, X, cbv(CB_65536),
                                     cbv(CB_MAGIC), ALU.mult, ALU.add)
                    ve.tensor_scalar(X, X, cbv(CB_MAGIC),
                                     cbv(CB_65535), ALU.subtract, ALU.min)
                    ve.tensor_tensor(X, X, tsc[:, 2 * par + 1, :], ALU.mult)
                    ve.tensor_scalar(X, X, scal[:, 13:14],
                                     cbv(CB_MAGIC), ALU.mult, ALU.add)
                    ve.tensor_scalar(X, X, cbv(CB_MAGIC), None, ALU.subtract)
                    ve.tensor_tensor(X, X, tsc[:, 2 * par, :], ALU.mult)
                    ve.tensor_reduce(vsml[:, 10:11], X, AX.X, ALU.max,
                                     apply_absolute_value=True)
                    ve.tensor_tensor(vsml[:, 13:14], vsml[:, 13:14],
                                     vsml[:, 10:11], ALU.max)
                    cx.inc(ve.tensor_copy(o32b[:, par, :], X), "f1b")

                @block.sync
                def _(sy, ht=ht, par=par):
                    cx.wait(sy, "f1b")
                    cx.inc(sy.dma_start(out=s_o32[ht], in_=o32b[:, par, :]),
                           f"so{par}", 16)

            part_max([vsml[:, 13:14]], [8], ("f1b", cx.c("f1b")))
            barrier(1, [14], ("tpE", cx.c("tpE")))

            @block.vector
            def _(ve):
                cx.wait(ve, "cbb")
                ve.tensor_tensor(vsml[:, 6:7], scal[:, 14:15], scal[:, 12:13],
                                 ALU.mult)
                ve.tensor_tensor(vsml[:, 6:7], vsml[:, 6:7], scal[:, 6:7], ALU.mult)
                ve.tensor_scalar(scal[:, 15:16], vsml[:, 6:7], cbv(CB_QM),
                                 cbv(CB_E12), ALU.mult, ALU.max)
                ve.reciprocal(vsml[:, 7:8], scal[:, 15:16])
                ve.tensor_tensor(vsml[:, 7:8], vsml[:, 7:8], scal[:, 12:13],
                                 ALU.mult)
                ve.tensor_tensor(scal[:, 16:17], vsml[:, 7:8], scal[:, 6:7],
                                 ALU.mult)
                cx.inc(ve.tensor_scalar(scal[:, 17:18], scal[:, 15:16],
                                        cbv(CB_HALF), None, ALU.mult), "f1d4")

            sfree(*[cm for cm, _ in cmT], *[cm for cm, _ in cmC])
            cmD = [salloc(f"{tag}t3", [128, 2, 1024], F32),
                   salloc(f"{tag}hid", [128, NH, D], BF16),
                   salloc(f"{tag}wd", [128, 2, 4, 128], BF16)]
            t3b, HIDb, wd_sb = [t for _, t in cmD]
            f1h_base = cx.c("f1h")
            idx = 0
            for th in range(2):
                for ht in range(NH):
                    par = ht % 2

                    @block.gpsimd
                    def _(gp, ht=ht, th=th, par=par, idx=idx,
                          f1h_base=f1h_base):
                        cx.wait(gp, f"so{par}")
                        cx.wait(gp, "f1h", f1h_base + idx - 1)
                        cx.inc(gp.dma_start(
                            out=t3b[:, par, :],
                            in_=s_o32[ht, :, th * 1024:(th + 1) * 1024]),
                            f"ro{par}", 16)

                    @block.vector
                    def _(ve, ht=ht, th=th, par=par):
                        cx.wait(ve, f"ro{par}")
                        if ht < 2 and th == 0:
                            cx.wait(ve, "so0")
                            cx.wait(ve, "so1")
                        cx.wait(ve, "f1d4", cx.c("f1d4"))
                        cx.wait(ve, "tdn")
                        ve.nd.tensor_scalar(t3b[:, par, :], t3b[:, par, :],
                                            scal[:, 16:17], cbv(CB_MAGIC),
                                            ALU.mult, ALU.add)
                        cx.inc(ve.tensor_scalar(HIDb[:, ht, :], t3b[:, par, :],
                                                cbv(CB_MAGIC), None, ALU.subtract),
                               "f1h")
                        cx.inc(ve.nd.memset(vsml[:, 15:16], 0.0), "vh")
                    idx += 1

                down_half(HIDb, s_w1d, wd_sb, scal[:, 17:18], th)
            sfree(*[cm for cm, _ in cmD])


        def l1_attn(tag):
            fence_entry()
            layernorm_stats()

            sums2 = stats[:, 0:NT]
            sumsq2 = stats[:, NT:2 * NT]
            for t in range(NT):
                @block.vector
                def _(ve, t=t):
                    cx.wait(ve, "lnr")
                    ve.scalar_tensor_tensor(
                        xn_f[:, t % 2, :], x_tm[:, t, :], MU(t),
                        RS(t).broadcast_to([128, D]), ALU.subtract, ALU.mult)
                    ve.reduce_sum(sums2[:, t:t + 1], xn_f[:, t % 2, :], AX.X)
                    cx.inc(ve.scalar_tensor_tensor(
                        xn_f[:, t % 2, :], xn_f[:, t % 2, :], 1.0,
                        xn_f[:, t % 2, :], ALU.mult, ALU.mult,
                        accum_out=sumsq2[:, t:t + 1]), "a5s")

            @block.vector
            def _(ve):
                cx.wait(ve, "a5s")
                mu2 = stats[:, 3 * NT:4 * NT]
                var2 = stats[:, 5 * NT:6 * NT]
                ve.tensor_scalar(mu2, sums2, cbv(CB_ID), None, ALU.mult)
                ve.tensor_scalar(var2, sumsq2, cbv(CB_ID), None, ALU.mult)
                ve.tensor_tensor(sums2, mu2, mu2, ALU.mult)
                ve.tensor_tensor(var2, var2, sums2, ALU.subtract)
                cx.inc(ve.tensor_scalar(var2, var2, cbv(CB_EPS), None, ALU.add),
                       "a5v")

            @block.scalar
            def _(se):
                cx.wait(se, "a5v")
                cx.inc(se.activation(stats[:, 0:NT], stats[:, 5 * NT:6 * NT],
                                     ACT.Sqrt), "a5q")

            @block.vector
            def _(ve):
                cx.wait(ve, "a5q")
                mu2 = stats[:, 3 * NT:4 * NT]
                var2 = stats[:, 5 * NT:6 * NT]
                r1 = stats[:, 4 * NT:5 * NT]
                r2 = stats[:, NT:2 * NT]
                t1 = stats[:, 0:NT]
                ve.reciprocal(r2, t1)
                ve.tensor_tensor(t1, r2, r2, ALU.mult)
                ve.tensor_tensor(t1, t1, var2, ALU.mult)
                ve.tensor_scalar(t1, t1, -0.5, 1.5, ALU.mult, ALU.add)
                ve.tensor_tensor(r2, r2, t1, ALU.mult)
                ve.tensor_tensor(t1, r1, r2, ALU.mult)
                ve.tensor_tensor(var2, stats[:, 2 * NT:3 * NT], t1, ALU.mult)
                ve.tensor_tensor(mu2, mu2, r2, ALU.mult)
                cx.inc(ve.tensor_tensor(mu2, mu2, var2, ALU.add), "a5r")

            def AP_A(t):
                return stats[:, t:t + 1]

            def AP_B(t):
                return stats[:, 3 * NT + t:3 * NT + t + 1]

            for t in range(NT):
                @block.vector
                def _(ve, t=t):
                    cx.wait(ve, "a5r")
                    cx.wait(ve, "tpT", cx.c("tpT") - 8)
                    cx.inc(ve.scalar_tensor_tensor(
                        xn_f[:, t % 2, :], x_tm[:, t, :], AP_A(t),
                        AP_B(t).broadcast_to([128, D]), ALU.mult,
                        ALU.subtract), "xn")

                def mk_evac(t, c):
                    def f(ve, bank):
                        return ve.tensor_scalar(
                            A1[:, c, t * 128:(t + 1) * 128], bank,
                            gbet[:, c:c + 1], gbet[:, 8 + c:8 + c + 1],
                            ALU.mult, ALU.add)
                    return f

                xn_target = cx.c("xn")

                def prew2(te, tgt=xn_target):
                    cx.wait(te, "xn", tgt)

                emit_transposes(
                    [(xn_f[:, t % 2, c * 128:(c + 1) * 128], mk_evac(t, c))
                     for c in range(8)], te_pre_wait=prew2)

            cmP = [salloc(f"{tag}q", [128, 8, T], BF16),
                   salloc(f"{tag}k", [128, 8, T], BF16),
                   salloc(f"{tag}v", [128, NT, 1040], BF16),
                   salloc(f"{tag}w", [128, 8, 128], BF16),
                   salloc(f"{tag}st", [128, 512], F32),
                   salloc(f"{tag}sq", [128, 512], BF16)]
            A2q, A3k, A4v, wslab, astage, sqb = [t for _, t in cmP]

            @block.gpsimd
            def _(gp):
                last = None
                for h in range(HEADS):
                    last = gp.memset(A4v[:, :, 65 * h + 64:65 * h + 65], 1.0)
                cx.inc(last, "a5o")

            xfE_all = cx.c("tpE")
            a5w_base = cx.c("a5w")
            a5mm_base = cx.c("a5mm")
            a5ev_base = cx.c("a5ev")

            for m, wsrc in ((0, d_w1q), (1, d_w1k), (2, d_w1v)):
                for oc in range(8):
                    gi = (cx.c("a5w") - a5w_base) // 16

                    @block.gpsimd
                    def _(gp, m=m, oc=oc, wsrc=wsrc, gi=gi):
                        cx.wait(gp, "a5mm", a5mm_base + 4 * gi)
                        cx.inc(gp.dma_start(out=wslab[:], in_=wsrc[:, oc]),
                               "a5w", 16)

                    for t5 in range(4):
                        mm_i = cx.c("a5mm")
                        mm_loc = mm_i - a5mm_base

                        @block.tensor
                        def _(te, m=m, oc=oc, t5=t5, gi=gi, mm_i=mm_i,
                              mm_loc=mm_loc):
                            if t5 == 0:
                                cx.wait(te, "a5w")
                                if gi == 0:
                                    cx.wait(te, "tpE", xfE_all)
                            cx.wait(te, "a5ev", a5ev_base + mm_loc - 1)
                            last = None
                            for k in range(8):
                                last = te.matmul(
                                    pbank[mm_i % 2][:], wslab[:, k, :],
                                    A1[:, k, t5 * 512:(t5 + 1) * 512],
                                    start=(k == 0), stop=(k == 7))
                            cx.inc(last, "a5mm")

                        @block.vector
                        def _(ve, m=m, oc=oc, t5=t5, mm_i=mm_i):
                            cx.wait(ve, "a5mm", mm_i + 1)
                            bank = pbank[mm_i % 2][:]
                            bias = bqkv[:, 8 * m + oc:8 * m + oc + 1]
                            if m == 0:
                                dst = A2q[:, oc, t5 * 512:(t5 + 1) * 512]
                            elif m == 1:
                                dst = A3k[:, oc, t5 * 512:(t5 + 1) * 512]
                            else:
                                dst = astage[:]
                                cx.wait(ve, "tpT", cx.c("tpT"))
                            cx.inc(ve.nd.tensor_scalar(dst, bank, bias, None,
                                                       ALU.add), "a5ev")

                        if m == 2:
                            ev_target = cx.c("a5ev")

                            def prew3(te, tgt=ev_target):
                                cx.wait(te, "a5ev", tgt)

                            def mk_vev(oc, t5, tt):
                                def f(ve, bank):
                                    jt = 4 * t5 + tt
                                    ve.tensor_copy(
                                        A4v[:, jt, 65 * 2 * oc:65 * 2 * oc + 64],
                                        bank[:, 0:64])
                                    return ve.tensor_copy(
                                        A4v[:, jt,
                                            65 * (2 * oc + 1):65 * (2 * oc + 1) + 64],
                                        bank[:, 64:128])
                                return f
                            emit_transposes(
                                [(astage[:, tt * 128:(tt + 1) * 128],
                                  mk_vev(oc, t5, tt)) for tt in range(4)],
                                te_pre_wait=prew3)

            for m, srcb in ((0, A2q), (1, A3k)):
                for oc in range(8):
                    for half in range(4):
                        @block.vector
                        def _(ve, m=m, oc=oc, half=half):
                            cx.wait(ve, "a5ev")
                            cx.wait(ve, "a5nm", cx.c("a5nm"))
                            src_ap = srcb[:, oc, half * 512:(half + 1) * 512]
                            cx.inc(ve.scalar_tensor_tensor(
                                sqb[:], src_ap, 1.0, src_ap,
                                ALU.mult, ALU.mult), "a5sq")

                        @block.tensor
                        def _(te, m=m, oc=oc, half=half):
                            cx.wait(te, "a5sq")
                            if oc == 0:
                                cx.wait(te, "a5nr", cx.c("a5nr"))
                            cx.inc(te.matmul(pbank[2 + half][0:16, :],
                                             hsel[:, oc, :], sqb[:],
                                             start=(oc == 0), stop=(oc == 7)),
                                   "a5nm")

                @block.vector
                def _(ve, m=m):
                    cx.wait(ve, "a5nm")
                    for half in range(4):
                        ve.reduce_max(vsml[0:16, 4 * m + half:4 * m + half + 1],
                                      pbank[2 + half][0:16, :], AX.X)
                    cx.inc(ve.tensor_reduce(vsml[0:16, 8 + m:9 + m],
                                            vsml[0:16, 4 * m:4 * m + 4], AX.X,
                                            ALU.max), "a5nr")

            @block.scalar
            def _(se):
                cx.wait(se, "a5nr")
                cx.inc(se.activation(vsml[0:16, 10:12], vsml[0:16, 8:10],
                                     ACT.Sqrt), "a5ns")

            @block.vector
            def _(ve):
                cx.wait(ve, "a5ns")
                ve.tensor_tensor(vsml[0:16, 12:13], vsml[0:16, 10:11],
                                 vsml[0:16, 11:12], ALU.mult)
                cx.inc(ve.tensor_scalar(vsml[0:16, 12:13], vsml[0:16, 12:13],
                                        -0.125 * 1.03, None, ALU.mult), "a5nb")

            tb2 = cx.c("tpT")

            @block.tensor
            def _(te):
                cx.wait(te, "a5nb", cx.c("a5nb"))
                cx.wait(te, "tpE", cx.c("tpE") - 1)
                cx.inc(te.transpose(pbank[6 + tb2 % 2][0:1, 0:16],
                                    vsml[0:16, 12:13], ident[0:16, 0:16]), "tpT")

            @block.vector
            def _(ve):
                cx.wait(ve, "tpT")
                cx.inc(ve.tensor_copy(vsml[0:1, 0:16], pbank[6 + tb2 % 2][0:1, 0:16]),
                       "tpE")

            @block.tensor
            def _(te):
                cx.wait(te, "tpE")
                cx.inc(te.matmul(pbank[2][:, 0:16], onesrow[:], vsml[0:1, 0:16],
                                 start=True, stop=True), "a5bb")

            @block.vector
            def _(ve):
                cx.wait(ve, "a5bb")
                cx.inc(ve.tensor_copy(scal[:, 0:16], pbank[2][:, 0:16]), "a5bc")

            ETbs = [A1[:, 0:4, :].rearrange("p a b -> p (a b)").rearrange(
                        "p (j i) -> p j i", i=512),
                    A1[:, 4:8, :].rearrange("p a b -> p (a b)").rearrange(
                        "p (j i) -> p j i", i=512)]
            for isl in range(4):
                for hp in range(8):
                    e_base = cx.c("a5e")
                    pv_prev = cx.c("a5pv")
                    sc_base = cx.c("a5sc")
                    for j in range(NT):
                        @block.tensor
                        def _(te, isl=isl, hp=hp, j=j, e_base=e_base):
                            if j == 0 and hp == 0 and isl == 0:
                                cx.wait(te, "a5bc")
                            p = j // 2
                            for hsub in range(2):
                                cx.wait(te, "a5e",
                                        e_base + 2 * (p - 1) + hsub + 1)
                                cx.inc(te.matmul(
                                    pall[:, 2 * hsub + j % 2, :],
                                    A3k[64 * hsub:64 * hsub + 64, hp,
                                        j * 128:(j + 1) * 128],
                                    A2q[64 * hsub:64 * hsub + 64, hp,
                                        isl * 512:(isl + 1) * 512],
                                    start=True, stop=True), "a5sc")

                    @block.scalar
                    def _(se, isl=isl, hp=hp, sc_base=sc_base, pv_prev=pv_prev):
                        cx.wait(se, "a5pv", pv_prev)
                        for p in range(8):
                            j = 2 * p + 1
                            cx.wait(se, "a5sc", sc_base + 4 * p + 4)
                            for hsub in range(2):
                                h = 2 * hp + hsub
                                cx.inc(se.nd.activation(
                                    ETbs[hsub][:, j - 1:j + 1, :],
                                    pall[:, 2 * hsub:2 * hsub + 2, :],
                                    ACT.Exp, bias=scal[:, h:h + 1],
                                    scale=cbv(CB_I8R)), "a5e")

                    for hsub in range(2):
                        h = 2 * hp + hsub
                        ETb = ETbs[hsub]

                        @block.tensor
                        def _(te, isl=isl, h=h, hsub=hsub, ETb=ETb):
                            cx.wait(te, "a5e")
                            cx.wait(te, "a5ve", cx.c("a5ve") - 1)
                            if isl == 0 and h < 2:
                                cx.wait(te, "a5bc", cx.c("a5bc"))
                                cx.wait(te, "a5nr", cx.c("a5nr"))
                            last = None
                            for j in range(NT):
                                last = te.matmul(
                                    pall[0:65, 4 + hsub, :],
                                    A4v[:, j, 65 * h:65 * h + 65],
                                    ETb[:, j, :], start=(j == 0),
                                    stop=(j == NT - 1))
                            cx.inc(last, "a5pv")

                        @block.vector
                        def _(ve, isl=isl, h=h, hsub=hsub):
                            cx.wait(ve, "a5pv")
                            cx.wait(ve, "tpT", cx.c("tpT"))
                            cx.inc(ve.nd.tensor_copy(astage[0:65, :],
                                                     pall[0:65, 4 + hsub, :]),
                                   "a5ve")

                        ve_target = cx.c("a5ve")

                        def prew5(te, tgt=ve_target):
                            cx.wait(te, "a5ve", tgt)

                        def mk_pvev(isl, h, tt):
                            def f(ve, bank):
                                tti = isl * 4 + tt
                                ve.reciprocal(vsml[:, 15:16], bank[:, 64:65])
                                return ve.scalar_tensor_tensor(
                                    x_tm[:, tti, 64 * h:64 * h + 64],
                                    bank[:, 0:64], vsml[:, 15:16],
                                    x_tm[:, tti, 64 * h:64 * h + 64],
                                    ALU.mult, ALU.add)
                            return f
                        emit_transposes(
                            [(astage[0:65, tt * 128:(tt + 1) * 128],
                              mk_pvev(isl, h, tt)) for tt in range(4)],
                            te_pre_wait=prew5)

            @block.vector
            def _(ve):
                cx.inc(ve.memset(vsml[:, 15:16], 0.0), "vres")

            sfree(*[cm for cm, _ in cmP])

        def dump_x(tag):
            if tag not in dbg_outs:
                return

            @block.sync
            def _(sy):
                cx.wait(sy, "vres")
                cx.inc(sy.dma_start(
                    out=dbg_outs[tag].rearrange("(t p) d -> p t d", p=128),
                    in_=x_tm[:]), "dout", 16)

        with nc.sbuf_tensor("HIDb", [128, NH, D], BF16) as HIDb, \
             nc.sbuf_tensor("wgsb", [128, 2, 8, 128], BF16) as wg_sb, \
             nc.sbuf_tensor("wusb", [128, 2, 8, 128], BF16) as wu_sb, \
             nc.sbuf_tensor("wdsb", [128, 2, 4, 128], BF16) as wd_sb, \
             nc.sbuf_tensor("sil", [128, 4, 512], F32) as sil:
            l0_ffn(HIDb, wg_sb, wu_sb, wd_sb, sil)
        dump_x("p1")

        with nc.sbuf_tensor("A2", [128, NT, D], BF16) as A2, \
             nc.sbuf_tensor("lrb", [128, 4, T], BF16) as lrb, \
             nc.sbuf_tensor("clb", [128, 8, K0], BF16) as clb, \
             nc.sbuf_tensor("crb", [128, 8, K0], BF16) as crb, \
             nc.sbuf_tensor("sql", [128, 2, 512], BF16) as sql, \
             nc.sbuf_tensor("lrow", [1, 2 * T], F32) as lrow, \
             nc.sbuf_tensor("rl0", [128, 24], F32) as rl0:
            l0_attn(A2, lrb, clb, crb, sql, lrow, rl0)
        dump_x("p2")

        with nc.sbuf_tensor("HIDb3", [128, NH, D], BF16) as HIDb, \
             nc.sbuf_tensor("wgsb3", [128, 2, 8, 128], BF16) as wg_sb, \
             nc.sbuf_tensor("wusb3", [128, 2, 8, 128], BF16) as wu_sb, \
             nc.sbuf_tensor("wdsb3", [128, 2, 4, 128], BF16) as wd_sb, \
             nc.sbuf_tensor("sil3", [128, 4, 512], F32) as sil:
            l0_ffn(HIDb, wg_sb, wu_sb, wd_sb, sil)
        dump_x("p3")

        l1_ffn("f4")
        dump_x("p4")

        l1_attn("a5")
        dump_x("p5")

        l1_ffn("f6")
        dump_x("p6")

        @block.sync
        def _(sy):
            cx.wait(sy, "vres")
            cx.wait(sy, "ve")
            cx.inc(sy.dma_start(
                out=d_out.rearrange("(t p) d -> p t d", p=128), in_=x_tm[:]),
                "dout", 16)
            cx.wait(sy, "dout")

    return nc


def _tile_lhsT(w_T):
    IN, OUT = w_T.shape
    return np.ascontiguousarray(
        w_T.reshape(IN // 128, 128, OUT // 128, 128).transpose(1, 2, 0, 3))


def _tile_sdmaj(w_T):
    H_, D_ = w_T.shape
    return np.ascontiguousarray(
        w_T.reshape(H_ // 128, 128, D_ // 128, 128).transpose(1, 2, 0, 3))


def kernel(L0_attn_x, L0_attn_compare_left, L0_attn_compare_right,
           L0_attn_threshold, L0_attn_gate_bias,
           L0_ffn_w_gate, L0_ffn_w_up, L0_ffn_w_down,
           L1_attn_ln_g, L1_attn_ln_b, L1_attn_Wq, L1_attn_bq,
           L1_attn_Wk, L1_attn_bk, L1_attn_Wv, L1_attn_bv,
           L1_ffn_w_gate, L1_ffn_w_up, L1_ffn_w_down,
           _debug_taps=(), _nc=None):
    x = np.asarray(L0_attn_x, dtype=np.float32)
    n = 8

    w0g = _tile_lhsT(np.ascontiguousarray(np.asarray(L0_ffn_w_gate).T))
    w0u = _tile_lhsT(np.ascontiguousarray(np.asarray(L0_ffn_w_up).T))
    w0d = _tile_sdmaj(np.ascontiguousarray(np.asarray(L0_ffn_w_down).T))
    cl = np.ascontiguousarray(np.asarray(L0_attn_compare_left, dtype=np.float32)
                              .reshape(8, 128, K0).transpose(1, 0, 2))
    cr = np.ascontiguousarray(np.asarray(L0_attn_compare_right, dtype=np.float32)
                              .reshape(8, 128, K0).transpose(1, 0, 2))
    w1q = _tile_lhsT(np.ascontiguousarray(np.asarray(L1_attn_Wq, dtype=np.float32)))
    w1k = _tile_lhsT(np.ascontiguousarray(np.asarray(L1_attn_Wk, dtype=np.float32)))
    w1v = _tile_lhsT(np.ascontiguousarray(np.asarray(L1_attn_Wv, dtype=np.float32)))

    def chunks(v):
        return np.ascontiguousarray(np.asarray(v, dtype=np.float32).reshape(8, 128).T)

    common = dict(
        w0g=w0g, w0u=w0u, w0d=w0d, cl=cl, cr=cr,
        thr=np.asarray(L0_attn_threshold, np.float32).reshape(1),
        gb=np.asarray(L0_attn_gate_bias, np.float32).reshape(1),
        w1q=w1q, w1k=w1k, w1v=w1v,
        bq=chunks(L1_attn_bq), bk=chunks(L1_attn_bk), bv=chunks(L1_attn_bv),
        l1g=chunks(L1_attn_ln_g), l1b=chunks(L1_attn_ln_b),
        w1fg=np.asarray(L1_ffn_w_gate, np.float32),
        w1fu=np.asarray(L1_ffn_w_up, np.float32),
        w1fd=np.asarray(L1_ffn_w_down, np.float32),
    )
    in_maps = [dict(common, x=np.ascontiguousarray(x[b])) for b in range(n)]

    nc = _nc if _nc is not None else build_nc(debug_taps=_debug_taps)
    res = run_bass_kernel_spmd(nc, in_maps, core_ids=list(range(n)))
    out = np.stack([res.results[b]["out"] for b in range(n)], axis=0)
    if _debug_taps:
        dbg = {t: np.stack([res.results[b][f"dbg_{t}"] for b in range(n)])
               for t in _debug_taps}
        return out, dbg
    return out


# version 09
# speedup vs baseline: 1.0845x; 1.0062x over previous
import numpy as np
import ml_dtypes
from contextlib import ExitStack

import concourse.bass as bass
import concourse.mybir as mybir
from concourse.bass_utils import run_bass_kernel_spmd

F32 = mybir.dt.float32
BF16 = mybir.dt.bfloat16
I16 = mybir.dt.int16
I32 = mybir.dt.int32
ALU = mybir.AluOpType
ACT = mybir.ActivationFunctionType
AX = mybir.AxisListType

MAGIC = 12582912.0
T = 2048
NT = 16
D = 1024
ND = 8
HF = 3072
NH = 24
HEADS = 16
DH = 64
K0 = 256
EPS = 1e-5


SEM_ALIAS = {
    "f1z2": "f1z", "f1z3": "f1z",
    "f1d2": "f1d", "f1d3": "f1d", "f1d4": "f1d",
    "f1sg": "f1s", "f1sh": "f1s", "f1p": "f1s",
    "f1q1": "f1q", "f1q2": "f1q",
    "ro0": "riu0", "ro1": "riu1",
    "a5sq": "lnsq", "a5nm": "l0nm", "a5nr": "l0nx", "a5ns": "l0bs",
    "a5nb": "l0bv", "a5bb": "l0bb", "a5bc": "l0bc", "a5sc": "l0s",
    "a5e": "l0e", "a5pv": "l0pv", "a5ve": "l0m", "a5mm": "lrmm",
    "a5ev": "l0r", "a5w": "dcl", "a5s": "xn", "a5v": "xn", "a5r": "xn",
    "a5q": "lnsr", "a5o": "gpi",
}


class Ctx:
    def __init__(self, nc, es):
        self.nc = nc
        self.es = es
        self._sems = {}
        self.cnt = {}
        self.block = es.enter_context(nc.Block())

    def sem(self, name):
        name = SEM_ALIAS.get(name, name)
        if name not in self._sems:
            self._sems[name] = self.es.enter_context(self.nc.semaphore(name))
            self.cnt[name] = 0
        return self._sems[name]

    def inc(self, instr, name, n=1):
        name = SEM_ALIAS.get(name, name)
        instr.then_inc(self.sem(name), n)
        self.cnt[name] += n
        return instr

    def c(self, name):
        name = SEM_ALIAS.get(name, name)
        self.sem(name)
        return self.cnt[name]

    def wait(self, eng, name, val=None):
        name = SEM_ALIAS.get(name, name)
        s = self.sem(name)
        v = self.cnt[name] if val is None else val
        if v > 0:
            eng.wait_ge(s, v)


_COMPUTE_OPS = (
    "tensor_copy", "tensor_tensor", "tensor_add", "tensor_sub", "tensor_mul",
    "tensor_max", "tensor_scalar", "tensor_single_scalar", "tensor_scalar_add",
    "tensor_scalar_sub", "tensor_scalar_mul", "tensor_scalar_max",
    "tensor_scalar_min", "tensor_relu", "tensor_reduce", "reduce_max",
    "reduce_sum", "scalar_tensor_tensor", "tensor_tensor_scan", "reciprocal",
    "select", "copy_predicated", "memset", "memzero", "activation", "add",
    "mul", "copy", "sqrt", "square", "sign", "affine_select", "iota",
    "bn_stats", "bn_aggr", "transpose", "stream_shuffle", "pool", "pool_avg",
    "pool_max", "tensor_tensor_reduce", "affine_mul_reduce",
)


class _NoDrain:

    def __init__(self, proxy):
        self._p = proxy

    def __getattr__(self, name):
        attr = getattr(self._p._eng, name)
        if name in _COMPUTE_OPS:
            def wrapped(*a, **kw):
                self._p._dirty = True
                return attr(*a, **kw)
            return wrapped
        return attr


class EngProxy:

    def __init__(self, eng):
        self._eng = eng
        self._dirty = False
        self.nd = _NoDrain(self)

    def __getattr__(self, name):
        attr = getattr(self._eng, name)
        if name in _COMPUTE_OPS:
            def wrapped(*a, **kw):
                if self._dirty:
                    self._eng.drain()
                self._dirty = True
                return attr(*a, **kw)
            return wrapped
        return attr


def build_nc(debug_taps=(), cc_world=8):
    nc = bass.Bass("TRN2", target_bir_lowering=False, debug=False)

    d_x = nc.dram_tensor("x", [T, D], F32, kind="ExternalInput").ap()
    d_w0g = nc.dram_tensor("w0g", [128, NH, 8, 128], I32, kind="ExternalInput").ap()
    d_w0u = nc.dram_tensor("w0u", [128, NH, 8, 128], I32, kind="ExternalInput").ap()
    d_w0d = nc.dram_tensor("w0d", [128, 8, NH, 128], I32, kind="ExternalInput").ap()
    d_cl = nc.dram_tensor("cl", [128, 8, K0], F32, kind="ExternalInput").ap()
    d_cr = nc.dram_tensor("cr", [128, 8, K0], F32, kind="ExternalInput").ap()
    d_thr = nc.dram_tensor("thr", [1], F32, kind="ExternalInput").ap()
    d_gb = nc.dram_tensor("gb", [1], F32, kind="ExternalInput").ap()
    d_w1q = nc.dram_tensor("w1q", [128, 8, 8, 128], F32, kind="ExternalInput").ap()
    d_w1k = nc.dram_tensor("w1k", [128, 8, 8, 128], F32, kind="ExternalInput").ap()
    d_w1v = nc.dram_tensor("w1v", [128, 8, 8, 128], F32, kind="ExternalInput").ap()
    d_bq = nc.dram_tensor("bq", [128, 8], F32, kind="ExternalInput").ap()
    d_bk = nc.dram_tensor("bk", [128, 8], F32, kind="ExternalInput").ap()
    d_bv = nc.dram_tensor("bv", [128, 8], F32, kind="ExternalInput").ap()
    d_l1g = nc.dram_tensor("l1g", [128, 8], F32, kind="ExternalInput").ap()
    d_l1b = nc.dram_tensor("l1b", [128, 8], F32, kind="ExternalInput").ap()
    d_w1fg = nc.dram_tensor("w1fg", [HF, D], F32, kind="ExternalInput").ap()
    d_w1fu = nc.dram_tensor("w1fu", [HF, D], F32, kind="ExternalInput").ap()
    d_w1fd = nc.dram_tensor("w1fd", [D, HF], F32, kind="ExternalInput").ap()
    d_out = nc.dram_tensor("out", [T, D], F32, kind="ExternalOutput").ap()
    dbg_outs = {t: nc.dram_tensor(f"dbg_{t}", [T, D], F32, kind="ExternalOutput").ap()
                for t in debug_taps}

    s_w0g = nc.dram_tensor("s_w0g", [128, NH, 8, 128], BF16).ap()
    s_w0u = nc.dram_tensor("s_w0u", [128, NH, 8, 128], BF16).ap()
    s_w0d = nc.dram_tensor("s_w0d", [128, 8, NH, 128], BF16).ap()
    s_w1g = nc.dram_tensor("s_w1g", [128, NH, 8, 128], BF16).ap()
    s_w1u = nc.dram_tensor("s_w1u", [128, NH, 8, 128], BF16).ap()
    s_w1d = nc.dram_tensor("s_w1d", [128, 8, NH, 128], BF16).ap()
    s_up = nc.dram_tensor("s_up", [NH, 128, T], F32).ap()
    s_gate = nc.dram_tensor("s_gate", [NH, 128, T], F32).ap()
    s_ui = nc.dram_tensor("s_ui", [NH, 128, T], I16).ap()
    s_gi = nc.dram_tensor("s_gi", [NH, 128, T], I16).ap()
    s_o32 = nc.dram_tensor("s_o32", [NH, 128, T], I32).ap()
    cc_in = nc.dram_tensor("cc_in", [8], F32).ap()
    cc_out = nc.dram_tensor("cc_out", [8], F32).ap()
    d_sc = nc.dram_tensor("d_sc", [16], F32).ap()

    with ExitStack() as es:
        cx = Ctx(nc, es)
        rawblock = cx.block
        _prox = {}

        class _BlockWrap:
            def _mk(self, name):
                dec = getattr(rawblock, name)
                def deco(fn):
                    def inner(eng):
                        key = name
                        if key not in _prox:
                            _prox[key] = EngProxy(eng) if name in ("vector", "scalar", "gpsimd") else eng
                        return fn(_prox[key])
                    return dec(inner)
                return deco

            @property
            def vector(self):
                return self._mk("vector")

            @property
            def scalar(self):
                return self._mk("scalar")

            @property
            def gpsimd(self):
                return self._mk("gpsimd")

            @property
            def tensor(self):
                return rawblock.tensor

            @property
            def sync(self):
                return rawblock.sync

        block = _BlockWrap()

        def sbuf(name, shape, dt=F32):
            return es.enter_context(nc.sbuf_tensor(name, shape, dt))

        def salloc(name, shape, dt=F32):
            cm = nc.sbuf_tensor(name, shape, dt)
            return cm, cm.__enter__()

        def sfree(*cms):
            for cm in reversed(cms):
                cm.__exit__(None, None, None)

        def psum(name, shape, dt=F32):
            return es.enter_context(nc.psum_tensor(name, shape, dt))

        x_tm = sbuf("x_tm", [128, NT, D])
        ident = sbuf("ident", [128, 128])
        ones_bf = sbuf("ones_bf", [128, 1], BF16)
        cb = sbuf("cb", [128, 16])
        onesrow = sbuf("onesrow", [1, 128])
        stats = sbuf("stats", [128, 6 * NT])
        scal = sbuf("scal", [128, 24])
        vsml = sbuf("vsml", [128, 16])
        hsel = sbuf("hsel", [128, 8, 16], BF16)
        bqkv = sbuf("bqkv", [128, 24])
        gbet = sbuf("gbet", [128, 16])
        thr_sb = sbuf("thr_sb", [128, 2])
        xn_f = sbuf("xn_f", [128, 2, D])
        A1 = sbuf("A1", [128, 8, T], BF16)

        CB_HALF, CB_MAGIC, CB_I127, CB_ID, CB_EPS, CB_I7, CB_E12, CB_65536, \
            CB_65535, CB_I65536, CB_QM, CB_M05127, CB_I32R, CB_I8R = range(14)
        CB_VALS = [0.5, MAGIC, 1.0 / 127.0, 1.0 / 1024.0, EPS, 1.0 / 7.0, 1e-12,
                   65536.0, 65535.0, 1.0 / 65536.0, 1.0 / 32767.0, 0.5 / 127.0,
                   1.0 / 32.0, 0.125]

        def cbv(i):
            return cb[:, i:i + 1]

        pall = psum("pall", [128, 8, 512])
        pbank = [pall[:, i, :] for i in range(8)]

        @block.gpsimd
        def _(gp):
            for i, v in enumerate(CB_VALS):
                gp.memset(cb[:, i:i + 1], v)
            cx.inc(gp.memset(ident[:], 0.0), "gpi")
            gp.memset(ones_bf[:], 1.0)
            gp.memset(onesrow[:], 1.0)
            cx.inc(gp.memset(hsel[:], 0.0), "gpi")
            cx.wait(gp, "gpi")
            gp.affine_select(out=ident[:], in_=ident[:], compare_op=ALU.not_equal,
                             fill=1.0, base=0, pattern=[[-1, 128]], channel_multiplier=1)
            for oc in range(8):
                gp.memset(hsel[0:64, oc, 2 * oc:2 * oc + 1], 1.0)
                last = gp.memset(hsel[64:128, oc, 2 * oc + 1:2 * oc + 2], 1.0)
            cx.inc(last, "gp")

        @block.sync
        def _(sy):
            cx.inc(sy.dma_start(
                out=x_tm[:], in_=d_x.rearrange("(t p) d -> p t d", p=128)), "dmx", 16)
            cx.inc(sy.dma_start(out=bqkv[:, 0:8], in_=d_bq), "dsm", 16)
            cx.inc(sy.dma_start(out=bqkv[:, 8:16], in_=d_bk), "dsm", 16)
            cx.inc(sy.dma_start(out=bqkv[:, 16:24], in_=d_bv), "dsm", 16)
            cx.inc(sy.dma_start(out=gbet[:, 0:8], in_=d_l1g), "dsm", 16)
            cx.inc(sy.dma_start(out=gbet[:, 8:16], in_=d_l1b), "dsm", 16)
            cx.inc(sy.dma_start(out=thr_sb[:, 0:1],
                                in_=d_thr[None, :].partition_broadcast(128)), "dsm", 16)
            cx.inc(sy.dma_start(out=thr_sb[:, 1:2],
                                in_=d_gb[None, :].partition_broadcast(128)), "dsm", 16)

        @block.vector
        def _(ve):
            cx.wait(ve, "dsm")
            cx.inc(ve.tensor_scalar(thr_sb[:, 0:1], thr_sb[:, 0:1], 32.0, None,
                                    ALU.mult), "ve")

        def emit_transposes(jobs, te_pre_wait=None, ve_pre_wait=None):
            tb = cx.c("tpT")
            shp = [(j[0].shape[0], j[0].shape[1]) for j in jobs]

            @block.tensor
            def _(te):
                if te_pre_wait is not None:
                    te_pre_wait(te)
                for i in range(len(jobs)):
                    pc, fc = shp[i]
                    cx.wait(te, "tpE", tb + i - 1)
                    cx.inc(te.transpose(
                        pbank[6 + (tb + i) % 2][0:fc, 0:pc],
                        jobs[i][0], ident[0:pc, 0:pc]), "tpT")

            @block.vector
            def _(ve):
                if ve_pre_wait is not None:
                    ve_pre_wait(ve)
                for i in range(len(jobs)):
                    pc, fc = shp[i]
                    cx.wait(ve, "tpT", tb + i + 1)
                    cx.inc(jobs[i][1](ve, pbank[6 + (tb + i) % 2][0:fc, 0:pc]),
                           "tpE")

        def fence_entry():
            snap = {k: v for k, v in cx.cnt.items() if v > 0}

            @block.sync
            def _(sy):
                for k, v in snap.items():
                    cx.wait(sy, k, v)

            @block.gpsimd
            def _(gp):
                for k, v in snap.items():
                    cx.wait(gp, k, v)

        with nc.sbuf_tensor("w0buf", [128, NH * 8 * 128], BF16) as w0buf:
            @block.gpsimd
            def _(gp):
                for (src, dst) in [(d_w0g, s_w0g), (d_w0u, s_w0u), (d_w0d, s_w0d)]:
                    cx.inc(gp.dma_start(out=w0buf[:],
                                        in_=src.rearrange("p a b c -> p (a b c)")),
                           "dwc", 16)
                    cx.wait(gp, "dwc")
                    cx.inc(gp.dma_start(out=dst.rearrange("p a b c -> p (a b c)"),
                                        in_=w0buf[:]), "dwc", 16)
                    cx.wait(gp, "dwc")

        if True:
            with nc.sbuf_tensor("wnat", [128, 2, HF], F32) as wnat, \
                 nc.sbuf_tensor("wdq", [128, 2, HF], F32) as wdq, \
                 nc.sbuf_tensor("wblk", [128, 192], F32) as wblk, nc.sbuf_tensor("wstg2", [128, 192], F32) as wstg2, \
                 nc.sbuf_tensor("wstg", [128, NH, 128], BF16) as wstg:

                dq_slab = [0]
                dq_done = {}
                dq_tpt = {}

                def dequant_weight(src, dst, nslab, icols):
                    nblk = icols // 32
                    nic = icols // 128
                    for sl in range(nslab):
                        g = dq_slab[0]
                        dq_slab[0] += 1
                        bg = g % 2
                        use_gp = (g % 4 == 3)

                        @block.sync
                        def _(sy, sl=sl, bg=bg, g=g):
                            cx.wait(sy, "dwc")
                            if g - 2 in dq_done:
                                cx.wait(sy, *dq_done[g - 2])
                            cx.inc(sy.dma_start(out=wnat[:, bg, 0:icols],
                                                in_=src[sl * 128:(sl + 1) * 128, :]),
                                   f"dqL{bg}", 16)

                        @block.vector
                        def _(ve, sl=sl, bg=bg, g=g, nblk=nblk, icols=icols):
                            cx.wait(ve, "gp", 1)
                            cx.wait(ve, f"dqL{bg}")
                            if g - 2 in dq_done:
                                cx.wait(ve, *dq_done[g - 2])
                            w3 = wnat[:, bg, 0:icols].rearrange(
                                "p (b k) -> p b k", k=32)
                            bm = wblk[:, 96 * bg:96 * bg + nblk]
                            ve.nd.tensor_reduce(bm[:, :, None], w3, AX.X, ALU.max,
                                                apply_absolute_value=True)
                            ve.tensor_scalar(bm, bm, cbv(CB_I7),
                                             cbv(CB_E12), ALU.mult, ALU.max)
                            cx.inc(ve.reciprocal(
                                wstg2[:, 96 * bg:96 * bg + nblk], bm), "dqm")

                        engsec = block.gpsimd if use_gp else block.vector
                        dqm_tgt = cx.c("dqm")
                        dq_sem = "dqw" if use_gp else "dqv"

                        @engsec
                        def _(eng, sl=sl, bg=bg, g=g, nblk=nblk, icols=icols,
                              dqm_tgt=dqm_tgt, dq_sem=dq_sem):
                            cx.wait(eng, "dqm", dqm_tgt)
                            if g - 2 in dq_tpt:
                                cx.wait(eng, "tpT", dq_tpt[g - 2])
                            w3 = wnat[:, bg, 0:icols].rearrange(
                                "p (b k) -> p b k", k=32)
                            bm = wblk[:, 96 * bg:96 * bg + nblk]
                            w3d = wdq[:, bg, 0:icols].rearrange(
                                "p (b k) -> p b k", k=32)
                            eng.tensor_tensor(
                                w3d, w3,
                                wstg2[:, 96 * bg:96 * bg + nblk, None]
                                .broadcast_to([128, nblk, 32]), ALU.mult)
                            eng.tensor_scalar(wdq[:, bg, 0:icols],
                                              wdq[:, bg, 0:icols],
                                              cbv(CB_MAGIC), cbv(CB_MAGIC),
                                              ALU.add, ALU.subtract)
                            cx.inc(eng.tensor_tensor(
                                w3d, w3d, bm[:, :, None].broadcast_to([128, nblk, 32]),
                                ALU.mult), dq_sem)
                        dq_done[g] = (dq_sem, cx.c(dq_sem))

                        def mk_evac(c):
                            def f(ve, bank):
                                return ve.nd.tensor_copy(wstg[:, c, :], bank)
                            return f

                        dqv_target = cx.c(dq_sem)

                        def prew(te, tgt=dqv_target, dq_sem=dq_sem):
                            cx.wait(te, dq_sem, tgt)

                        dqo_tgt = cx.c("dqo")

                        def vprew(ve, tgt=dqo_tgt):
                            cx.wait(ve, "dqo", tgt)

                        emit_transposes(
                            [(wdq[:, bg, c * 128:(c + 1) * 128], mk_evac(c))
                             for c in range(nic)], te_pre_wait=prew,
                            ve_pre_wait=vprew)
                        dq_tpt[g] = cx.c("tpT")

                        @block.sync
                        def _(sy, sl=sl, nic=nic):
                            cx.wait(sy, "tpE")
                            cx.inc(sy.dma_start(out=dst[:, sl, 0:nic, :],
                                                in_=wstg[:, 0:nic, :]), "dqo", 16)

                dequant_weight(d_w1fg, s_w1g, NH, D)
                dequant_weight(d_w1fu, s_w1u, NH, D)
                dequant_weight(d_w1fd, s_w1d, 8, HF)

        def layernorm_stats():
            sums = stats[:, 0:NT]
            sumsq = stats[:, NT:2 * NT]

            @block.vector
            def _(ve):
                cx.wait(ve, "gp", 1)
                cx.wait(ve, "vres")
                cx.wait(ve, "dmx")
                cx.wait(ve, "tpT")
                for t in range(NT):
                    ve.nd.reduce_sum(sums[:, t:t + 1], x_tm[:, t, :], AX.X)
                for t in range(NT):
                    cx.inc(ve.scalar_tensor_tensor(
                        xn_f[:, t % 2, :], x_tm[:, t, :], 1.0, x_tm[:, t, :],
                        ALU.mult, ALU.mult, accum_out=sumsq[:, t:t + 1]), "lnsq")
                mu = stats[:, 2 * NT:3 * NT]
                var = stats[:, 3 * NT:4 * NT]
                ve.tensor_scalar(mu, sums, cbv(CB_ID), None, ALU.mult)
                ve.tensor_scalar(var, sumsq, cbv(CB_ID), None, ALU.mult)
                ve.tensor_tensor(sumsq, mu, mu, ALU.mult)
                ve.tensor_tensor(var, var, sumsq, ALU.subtract)
                cx.inc(ve.tensor_scalar(var, var, cbv(CB_EPS), None, ALU.add), "lnv")

            @block.scalar
            def _(se):
                cx.wait(se, "lnv")
                cx.inc(se.activation(stats[:, 5 * NT:6 * NT],
                                     stats[:, 3 * NT:4 * NT], ACT.Sqrt), "lnsr")

            @block.vector
            def _(ve):
                cx.wait(ve, "lnsr")
                var = stats[:, 3 * NT:4 * NT]
                r = stats[:, 4 * NT:5 * NT]
                sq = stats[:, 5 * NT:6 * NT]
                t1 = stats[:, 0:NT]
                ve.reciprocal(r, sq)
                ve.tensor_tensor(t1, r, r, ALU.mult)
                ve.tensor_tensor(t1, t1, var, ALU.mult)
                ve.tensor_scalar(t1, t1, -0.5, 1.5, ALU.mult, ALU.add)
                cx.inc(ve.tensor_tensor(r, r, t1, ALU.mult), "lnr")

        def MU(t):
            return stats[:, 2 * NT + t:2 * NT + t + 1]

        def RS(t):
            return stats[:, 4 * NT + t:4 * NT + t + 1]

        def normalize_to_fm(dst, dst_tm=None):
            for t in range(NT):
                @block.vector
                def _(ve, t=t):
                    cx.wait(ve, "lnr")
                    cx.wait(ve, "tpT", cx.c("tpT") - 8)
                    cx.inc(ve.nd.scalar_tensor_tensor(
                        xn_f[:, t % 2, :], x_tm[:, t, :], MU(t),
                        RS(t).broadcast_to([128, D]), ALU.subtract, ALU.mult), "xn")
                    if dst_tm is not None:
                        cx.inc(ve.tensor_copy(dst_tm[:, t, :], xn_f[:, t % 2, :]),
                               "xntm")

                def mk_evac(t, c):
                    def f(ve, bank):
                        return ve.tensor_copy(dst[:, c, t * 128:(t + 1) * 128], bank)
                    return f

                xn_target = cx.c("xn")

                def prew(te, tgt=xn_target):
                    cx.wait(te, "xn", tgt)

                emit_transposes(
                    [(xn_f[:, t % 2, c * 128:(c + 1) * 128], mk_evac(t, c))
                     for c in range(8)], te_pre_wait=prew)

        def down_half(HIDb, wsrc, wd_sb, res_scale, th):
            for dh in range(2):
                vh_all = cx.c("vh")
                vres_prev = cx.c("vres")
                for c in range(NH):
                    di = cx.c("tdn")
                    par = di % 2

                    @block.sync
                    def _(sy, c=c, dh=dh, par=par, di=di):
                        cx.wait(sy, "dwc")
                        cx.wait(sy, "so0")
                        cx.wait(sy, "so1")
                        cx.wait(sy, "tdn", di - 1)
                        cx.inc(sy.dma_start(
                            out=wd_sb[:, par],
                            in_=wsrc[:, 4 * dh:4 * (dh + 1), c, :]),
                            f"wd{par}", 16)

                    @block.tensor
                    def _(te, c=c, dh=dh, par=par, vh_all=vh_all,
                          vres_prev=vres_prev):
                        cx.wait(te, f"wd{par}")
                        if c == 0:
                            cx.wait(te, "vh", vh_all)
                            cx.wait(te, "vres", vres_prev)
                        last = None
                        for t in range(8):
                            last = te.matmul(
                                pbank[t][:],
                                HIDb[:, c, t * 128:(t + 1) * 128],
                                wd_sb[:, par].rearrange("p a b -> p (a b)"),
                                start=(c == 0), stop=(c == NH - 1))
                        cx.inc(last, "tdn")

                @block.vector
                def _(ve, dh=dh, th=th, res_scale=res_scale):
                    cx.wait(ve, "tdn")
                    for t in range(8):
                        tt = th * 8 + t
                        cx.inc(ve.nd.scalar_tensor_tensor(
                            x_tm[:, tt, dh * 512:(dh + 1) * 512],
                            pbank[t][:], res_scale,
                            x_tm[:, tt, dh * 512:(dh + 1) * 512],
                            ALU.mult, ALU.add), "vres")

        def l0_ffn(HIDb, wg_sb, wu_sb, wd_sb, sil):
            fence_entry()
            layernorm_stats()
            normalize_to_fm(A1)
            xfE_all = cx.c("tpE")

            for th in range(2):
                tc0 = th * 1024
                gumm_base = cx.c("gumm")
                sil_base = cx.c("sil")
                vh_base = cx.c("vh")
                for ht in range(NH):
                    gi = cx.c("gumm")
                    par = gi % 2

                    @block.sync
                    def _(sy, ht=ht, par=par, gi=gi):
                        cx.wait(sy, "dwc")
                        cx.wait(sy, "gumm", gi - 1)
                        cx.inc(sy.dma_start(out=wg_sb[:, par], in_=s_w0g[:, ht]),
                               f"wg{par}", 16)
                        cx.inc(sy.dma_start(out=wu_sb[:, par], in_=s_w0u[:, ht]),
                               f"wu{par}", 16)

                    @block.tensor
                    def _(te, ht=ht, par=par, gi=gi, tc0=tc0, vh_base=vh_base,
                          lht=ht):
                        cx.wait(te, f"wg{par}")
                        cx.wait(te, f"wu{par}")
                        cx.wait(te, "tpE", xfE_all)
                        cx.wait(te, "vh", vh_base + 2 * (lht - 1))
                        cx.wait(te, "vres")
                        bs = 4 * (gi % 2)
                        last = None
                        for t5 in range(2):
                            pg, pu = pall[:, bs + 2 * t5, :], pall[:, bs + 2 * t5 + 1, :]
                            for k in range(8):
                                te.matmul(pg, wg_sb[:, par, k, :],
                                          A1[:, k, tc0 + t5 * 512:tc0 + (t5 + 1) * 512],
                                          start=(k == 0), stop=(k == 7))
                            for k in range(8):
                                last = te.matmul(
                                    pu, wu_sb[:, par, k, :],
                                    A1[:, k, tc0 + t5 * 512:tc0 + (t5 + 1) * 512],
                                    start=(k == 0), stop=(k == 7))
                        cx.inc(last, "gumm")

                @block.scalar
                def _(se, th=th, gumm_base=gumm_base, vh_base=vh_base):
                    for lht in range(NH):
                        gi = gumm_base + lht
                        cx.wait(se, "gumm", gi + 1)
                        cx.wait(se, "vh", vh_base + 2 * (lht - 1))
                        bs = 4 * (gi % 2)
                        for t5 in range(2):
                            cx.inc(se.nd.activation(
                                sil[:, 2 * (gi % 2) + t5, :],
                                pall[:, bs + 2 * t5, :], ACT.Silu,
                                scale=cbv(CB_I127)), "sil")

                @block.vector
                def _(ve, th=th, gumm_base=gumm_base, sil_base=sil_base):
                    for lht in range(NH):
                        gi = gumm_base + lht
                        cx.wait(ve, "sil", sil_base + 2 * lht + 2)
                        bs = 4 * (gi % 2)
                        for t5 in range(2):
                            cx.inc(ve.nd.scalar_tensor_tensor(
                                HIDb[:, lht, t5 * 512:(t5 + 1) * 512],
                                pall[:, bs + 2 * t5 + 1, :], cbv(CB_I127),
                                sil[:, 2 * (gi % 2) + t5, :],
                                ALU.mult, ALU.mult), "vh")

                down_half(HIDb, s_w0d, wd_sb, cbv(CB_M05127), th)

        def l0_attn(A2, lrb, clb, crb, sql, lrow, rl0):
            fence_entry()
            layernorm_stats()
            normalize_to_fm(A1, dst_tm=A2)

            @block.gpsimd
            def _(gp):
                cx.inc(gp.dma_start(out=clb[:], in_=d_cl), "dcl", 16)
                cx.inc(gp.dma_start(out=crb[:], in_=d_cr), "dcl", 16)

            xn_all = cx.c("tpE")
            lr_jobs = []
            for m, wsb in ((0, clb), (1, crb)):
                for kc in range(2):
                    for t5 in range(4):
                        lr_jobs.append((m, wsb, kc, t5))
            for idx, (m, wsb, kc, t5) in enumerate(lr_jobs):
                @block.tensor
                def _(te, idx=idx, m=m, wsb=wsb, kc=kc, t5=t5):
                    if idx == 0:
                        cx.wait(te, "dcl")
                        cx.wait(te, "tpE", xn_all)
                    cx.wait(te, "lre", idx - 1)
                    last = None
                    for c in range(8):
                        last = te.matmul(pbank[idx % 2][:],
                                         wsb[:, c, kc * 128:(kc + 1) * 128],
                                         A1[:, c, t5 * 512:(t5 + 1) * 512],
                                         start=(c == 0), stop=(c == 7))
                    cx.inc(last, "lrmm")

                @block.scalar
                def _(se, idx=idx, m=m, kc=kc, t5=t5):
                    cx.wait(se, "lrmm", idx + 1)
                    cx.inc(se.nd.activation(
                        lrb[:, 2 * m + kc, t5 * 512:(t5 + 1) * 512],
                        pbank[idx % 2][:], ACT.Copy), "lre")

            for m in range(2):
                for t5 in range(4):
                    for kc in range(2):
                        @block.vector
                        def _(ve, m=m, kc=kc, t5=t5):
                            cx.wait(ve, "lre")
                            cx.wait(ve, "l0nm", cx.c("l0nm"))
                            src_ap = lrb[:, 2 * m + kc, t5 * 512:(t5 + 1) * 512]
                            cx.inc(ve.scalar_tensor_tensor(
                                sql[:, kc, :], src_ap, 1.0, src_ap,
                                ALU.mult, ALU.mult), "l0sq")

                    @block.tensor
                    def _(te, m=m, t5=t5):
                        cx.wait(te, "l0sq")
                        cx.wait(te, "l0ne", cx.c("l0ne"))
                        last = None
                        for kc in range(2):
                            last = te.matmul(pbank[2][0:1, :], ones_bf[:],
                                             sql[:, kc, :], start=(kc == 0),
                                             stop=(kc == 1))
                        cx.inc(last, "l0nm")

                    @block.vector
                    def _(ve, m=m, t5=t5):
                        cx.wait(ve, "l0nm")
                        cx.inc(ve.tensor_copy(
                            lrow[0:1, (4 * m + t5) * 512:
                                 (4 * m + t5 + 1) * 512],
                            pbank[2][0:1, :]), "l0ne")

            @block.vector
            def _(ve):
                cx.wait(ve, "l0ne")
                ve.reduce_max(rl0[0:1, 0:1], lrow[0:1, 0:T], AX.X)
                cx.inc(ve.reduce_max(rl0[0:1, 1:2], lrow[0:1, T:2 * T], AX.X),
                       "l0nx")

            @block.scalar
            def _(se):
                cx.wait(se, "l0nx")
                cx.inc(se.activation(rl0[0:1, 2:4], rl0[0:1, 0:2], ACT.Sqrt),
                       "l0bs")

            @block.vector
            def _(ve):
                cx.wait(ve, "l0bs")
                ve.tensor_tensor(rl0[0:1, 4:5], rl0[0:1, 2:3], rl0[0:1, 3:4],
                                 ALU.mult)
                cx.inc(ve.tensor_scalar(rl0[0:1, 4:5], rl0[0:1, 4:5],
                                        -1.03 / 32.0, None, ALU.mult), "l0bv")

            @block.tensor
            def _(te):
                cx.wait(te, "l0bv")
                cx.inc(te.matmul(pbank[2][:, 0:1], onesrow[:], rl0[0:1, 4:5],
                                 start=True, stop=True), "l0bb")

            @block.vector
            def _(ve):
                cx.wait(ve, "l0bb")
                cx.inc(ve.tensor_copy(scal[:, 0:1], pbank[2][:, 0:1]), "l0bc")

            ETb = A1[:, 0:4, :].rearrange("p a b -> p (a b)").rearrange(
                "p (j i) -> p j i", i=512)
            METb = A1[:, 4:8, :].rearrange("p a b -> p (a b)").rearrange(
                "p (j i) -> p j i", i=512)
            for isl in range(4):
                e_base = cx.c("l0e")
                m_base = cx.c("l0m")
                l_prev = cx.c("l0l")
                pv_prev = cx.c("l0pv")
                s_base = cx.c("l0s")
                for j in range(NT):
                    @block.tensor
                    def _(te, isl=isl, j=j):
                        if j == 0 and isl == 0:
                            cx.wait(te, "l0bc")
                        cx.wait(te, "l0e", e_base + j // 2 - 1)
                        cx.wait(te, "l0m", m_base + j // 2 - 1)
                        last = None
                        for kc in range(2):
                            last = te.matmul(pall[:, j % 4, :],
                                             lrb[:, 2 + kc, j * 128:(j + 1) * 128],
                                             lrb[:, kc, isl * 512:(isl + 1) * 512],
                                             start=(kc == 0), stop=(kc == 1))
                        cx.inc(last, "l0s")

                @block.scalar
                def _(se, isl=isl, s_base=s_base, l_prev=l_prev, e_base=e_base):
                    cx.wait(se, "l0l", l_prev)
                    for p in range(8):
                        cx.wait(se, "l0s", s_base + 2 * p + 2)
                        cx.wait(se, "l0m", m_base + p - 1)
                        cx.inc(se.nd.activation(
                            ETb[:, 2 * p:2 * p + 2, :],
                            pall[:, (2 * p) % 4:(2 * p) % 4 + 2, :],
                            ACT.Exp, bias=scal[:, 0:1],
                            scale=cbv(CB_I32R)), "l0e")

                @block.vector
                def _(ve, isl=isl, e_base=e_base):
                    cx.wait(ve, "l0pv", pv_prev)
                    for p in range(8):
                        cx.wait(ve, "l0e", e_base + p + 1)
                        cx.inc(ve.nd.scalar_tensor_tensor(
                            METb[:, 2 * p:2 * p + 2, :],
                            pall[:, (2 * p) % 4:(2 * p) % 4 + 2, :],
                            thr_sb[:, 0:1],
                            ETb[:, 2 * p:2 * p + 2, :], ALU.is_gt, ALU.mult),
                            "l0m")

                @block.tensor
                def _(te, isl=isl):
                    cx.wait(te, "l0e")
                    cx.wait(te, "l0r", cx.c("l0r"))
                    cx.wait(te, "l0bc", cx.c("l0bc"))
                    for ic in range(4):
                        last = None
                        for j in range(NT):
                            last = te.matmul(pall[:, 4, ic:ic + 1],
                                             ETb[:, j, ic * 128:(ic + 1) * 128],
                                             ones_bf[:], start=(j == 0),
                                             stop=(j == NT - 1))
                        cx.inc(last, "l0l")

                @block.vector
                def _(ve, isl=isl):
                    cx.wait(ve, "l0l")
                    cx.inc(ve.reciprocal(rl0[:, 8 + isl * 4:8 + isl * 4 + 4],
                                         pall[:, 4, 0:4]), "l0r")

                for ic in range(4):
                    @block.tensor
                    def _(te, isl=isl, ic=ic):
                        cx.wait(te, "l0m")
                        cx.wait(te, "l0r", cx.c("l0r"))
                        cx.wait(te, "vres")
                        last = None
                        for dhh in range(2):
                            for j in range(NT):
                                last = te.matmul(
                                    pall[:, 5 + dhh, :],
                                    METb[:, j, ic * 128:(ic + 1) * 128],
                                    A2[:, j, dhh * 512:(dhh + 1) * 512],
                                    start=(j == 0), stop=(j == NT - 1))
                        cx.inc(last, "l0pv")

                    @block.vector
                    def _(ve, isl=isl, ic=ic):
                        cx.wait(ve, "l0pv")
                        tt = isl * 4 + ic
                        for dhh in range(2):
                            cx.inc(ve.nd.scalar_tensor_tensor(
                                x_tm[:, tt, dhh * 512:(dhh + 1) * 512],
                                pall[:, 5 + dhh, :],
                                rl0[:, 8 + tt:8 + tt + 1],
                                x_tm[:, tt, dhh * 512:(dhh + 1) * 512],
                                ALU.mult, ALU.add), "vres")


        ccn = [0]

        def barrier(brow_k, dst_slots, prep_sem):
            k = brow_k
            n = ccn[0]
            ccn[0] += 1
            cci = nc.dram_tensor(f"cci{n}", [k], F32).ap()
            cco = nc.dram_tensor(f"cco{n}", [k], F32).ap()

            @block.sync
            def _(sy):
                cx.wait(sy, prep_sem[0], prep_sem[1])
                cx.inc(sy.dma_start(out=cci, in_=vsml[0:1, 8:8 + k]), "cds", 16)

            @block.gpsimd
            def _(gp):
                cx.wait(gp, "cds")
                cx.inc(gp.collective_compute(
                    "AllReduce", ALU.max, replica_groups=[list(range(cc_world))],
                    ins=[cci], outs=[cco]), "ccl")

            @block.sync
            def _(sy):
                cx.wait(sy, "ccl")
                for i, sl in enumerate(dst_slots):
                    cx.inc(sy.dma_start(
                        out=scal[:, sl:sl + 1],
                        in_=cco[i:i + 1][None, :].partition_broadcast(128)),
                        "cbb", 16)

        def part_max(srcs, dst_cols, src_sem):
            def mk(i):
                def f(ve, bank):
                    return ve.reduce_max(vsml[0:1, dst_cols[i]:dst_cols[i] + 1],
                                         bank[0:1, 0:128], AX.X)
                return f

            def prew(te, ss=src_sem):
                cx.wait(te, ss[0], ss[1])
            emit_transposes([(s, mk(i)) for i, s in enumerate(srcs)],
                            te_pre_wait=prew)

        def l1_ffn(tag):
            fence_entry()
            layernorm_stats()

            @block.vector
            def _(ve):
                cx.wait(ve, "lnr")
                xmx = stats[:, 0:NT]
                xmn = stats[:, NT:2 * NT]
                for t in range(NT):
                    ve.reduce_max(xmx[:, t:t + 1], x_tm[:, t, :], AX.X)
                    ve.tensor_reduce(xmn[:, t:t + 1], x_tm[:, t, :], AX.X, ALU.min)
                mu = stats[:, 2 * NT:3 * NT]
                r = stats[:, 4 * NT:5 * NT]
                sq = stats[:, 5 * NT:6 * NT]
                ve.tensor_tensor(xmx, xmx, mu, ALU.subtract)
                ve.tensor_tensor(xmn, mu, xmn, ALU.subtract)
                ve.tensor_tensor(xmx, xmx, xmn, ALU.max)
                ve.tensor_tensor(xmx, xmx, r, ALU.mult)
                cx.inc(ve.reduce_max(vsml[:, 0:1], xmx, AX.X), "f1x")

            prew_tgt = cx.c("f1x")

            def prew(te, tgt=prew_tgt):
                cx.wait(te, "f1x", tgt)
            emit_transposes([(vsml[:, 0:1],
                              lambda ve, bank: ve.reduce_max(
                                  vsml[0:1, 8:9], bank[0:1, 0:128], AX.X))],
                            te_pre_wait=prew)
            barrier(1, [1], ("tpE", cx.c("tpE")))

            @block.vector
            def _(ve):
                cx.wait(ve, "cbb")
                ve.tensor_scalar(scal[:, 2:3], scal[:, 1:2], cbv(CB_QM),
                                 cbv(CB_E12), ALU.mult, ALU.max)
                ve.reciprocal(scal[:, 3:4], scal[:, 2:3])
                A = stats[:, 0:NT]
                B = stats[:, NT:2 * NT]
                ve.tensor_scalar(A, stats[:, 4 * NT:5 * NT], scal[:, 3:4], None,
                                 ALU.mult)
                ve.tensor_tensor(B, stats[:, 2 * NT:3 * NT], A, ALU.mult)
                cx.inc(ve.tensor_scalar(B, B, -1.0, None, ALU.mult), "f1d")

            for t in range(NT):
                @block.vector
                def _(ve, t=t):
                    cx.wait(ve, "f1d")
                    cx.wait(ve, "tpT", cx.c("tpT") - 8)
                    ve.tensor_scalar(xn_f[:, t % 2, :], x_tm[:, t, :],
                                     stats[:, t:t + 1],
                                     stats[:, NT + t:NT + t + 1],
                                     ALU.mult, ALU.add)
                    cx.inc(ve.tensor_scalar(xn_f[:, t % 2, :], xn_f[:, t % 2, :],
                                            cbv(CB_MAGIC), cbv(CB_MAGIC),
                                            ALU.add, ALU.subtract), "xn")

                def mk_evac(t, c):
                    def f(ve, bank):
                        return ve.tensor_copy(A1[:, c, t * 128:(t + 1) * 128], bank)
                    return f

                xn_target = cx.c("xn")

                def prew2(te, tgt=xn_target):
                    cx.wait(te, "xn", tgt)

                emit_transposes(
                    [(xn_f[:, t % 2, c * 128:(c + 1) * 128], mk_evac(t, c))
                     for c in range(8)], te_pre_wait=prew2)

            cmA = [salloc(f"{tag}stgu", [128, 2, T], F32),
                   salloc(f"{tag}stgg", [128, 2, T], F32),
                   salloc(f"{tag}wg", [128, 2, 8, 128], BF16),
                   salloc(f"{tag}wu", [128, 2, 8, 128], BF16)]
            stgu, stgg, wg_sb, wu_sb = [t for _, t in cmA]

            @block.gpsimd
            def _(gp):
                gp.memset(vsml[:, 3:4], 0.0)
                cx.inc(gp.memset(vsml[:, 5:6], 0.0), "f1z")

            xfE_all = cx.c("tpE")
            f1g5_base = cx.c("f1g5")
            sue_base = cx.c("sue")
            spu_base = {0: cx.c("spu0"), 1: cx.c("spu1")}
            spg_base = {0: cx.c("spg0"), 1: cx.c("spg1")}
            sum_base = cx.c("sum")
            for ht in range(NH):
                par = ht % 2
                g_base = cx.c("f1g5")

                @block.sync
                def _(sy, ht=ht, par=par, g_base=g_base):
                    cx.wait(sy, "dqo")
                    cx.wait(sy, "f1g5", g_base - 4)
                    cx.inc(sy.dma_start(out=wg_sb[:, par], in_=s_w1g[:, ht]),
                           f"wg{par}", 16)
                    cx.inc(sy.dma_start(out=wu_sb[:, par], in_=s_w1u[:, ht]),
                           f"wu{par}", 16)

                for t5 in range(4):
                    g5 = cx.c("f1g5")

                    @block.tensor
                    def _(te, ht=ht, par=par, t5=t5, g5=g5):
                        if t5 == 0:
                            cx.wait(te, f"wg{par}")
                            cx.wait(te, f"wu{par}")
                            cx.wait(te, "tpE", xfE_all)
                        cx.wait(te, "sue", 2 * (g5 - 3))
                        bk = 2 * (g5 % 4)
                        last = None
                        for k in range(8):
                            te.matmul(pall[:, bk, :], wg_sb[:, par, k, :],
                                      A1[:, k, t5 * 512:(t5 + 1) * 512],
                                      start=(k == 0), stop=(k == 7))
                        for k in range(8):
                            last = te.matmul(pall[:, bk + 1, :], wu_sb[:, par, k, :],
                                             A1[:, k, t5 * 512:(t5 + 1) * 512],
                                             start=(k == 0), stop=(k == 7))
                        cx.inc(last, "f1g5")

                @block.sync
                def _(sy, ht=ht, par=par, sue_base=sue_base):
                    cx.wait(sy, "sue", sue_base + 8 * (ht + 1))
                    cx.inc(sy.dma_start(out=s_up[ht], in_=stgu[:, par, :]),
                           f"spu{par}", 16)
                    cx.inc(sy.dma_start(out=s_gate[ht], in_=stgg[:, par, :]),
                           f"spg{par}", 16)

            @block.scalar
            def _(se, f1g5_base=f1g5_base, spu_base=spu_base, spg_base=spg_base,
                  sum_base=sum_base):
                for ht in range(NH):
                    par = ht % 2
                    for t5 in range(4):
                        g5 = f1g5_base + 4 * ht + t5
                        cx.wait(se, "f1g5", g5 + 1)
                        if t5 == 0:
                            cx.wait(se, f"spu{par}",
                                    spu_base[par] + 16 * (ht // 2))
                            cx.wait(se, f"spg{par}",
                                    spg_base[par] + 16 * (ht // 2))
                            cx.wait(se, "sum", sum_base + ht - 1)
                        bk = 2 * (g5 % 4)
                        cx.inc(se.nd.activation(
                            stgg[:, par, t5 * 512:(t5 + 1) * 512],
                            pall[:, bk, :], ACT.Copy), "sue")
                        cx.inc(se.nd.activation(
                            stgu[:, par, t5 * 512:(t5 + 1) * 512],
                            pall[:, bk + 1, :], ACT.Copy), "sue")

            @block.vector
            def _(ve, sue_base=sue_base):
                cx.wait(ve, "f1z", cx.c("f1z"))
                for ht in range(NH):
                    par = ht % 2
                    cx.wait(ve, "sue", sue_base + 8 * (ht + 1))
                    ve.nd.tensor_reduce(vsml[:, 2:3], stgu[:, par, :], AX.X,
                                        ALU.max, apply_absolute_value=True)
                    ve.tensor_tensor(vsml[:, 3:4], vsml[:, 3:4], vsml[:, 2:3],
                                     ALU.max)
                    ve.nd.tensor_reduce(vsml[:, 4:5], stgg[:, par, :], AX.X,
                                        ALU.max, apply_absolute_value=True)
                    cx.inc(ve.tensor_tensor(vsml[:, 5:6], vsml[:, 5:6],
                                            vsml[:, 4:5], ALU.max), "sum")


            sfree(*[cm for cm, _ in cmA])
            cmT = [salloc(f"{tag}ts", [128, 4, T], F32),
                   salloc(f"{tag}iu", [128, 2, T], I16),
                   salloc(f"{tag}ig", [128, 2, T], I16)]
            tsc, iub, igb = [t for _, t in cmT]
            cmB = [salloc(f"{tag}ru", [128, 2, T], F32),
                   salloc(f"{tag}rg", [128, 2, T], F32)]
            ru, rg = [t for _, t in cmB]
            for ht in range(2):
                par = ht % 2

                @block.sync
                def _(sy, ht=ht, par=par):
                    cx.wait(sy, f"spu{par}")
                    cx.wait(sy, f"spg{par}")
                    cx.inc(sy.dma_start(out=ru[:, par, :], in_=s_up[ht]),
                           f"ru{par}", 16)
                    cx.inc(sy.dma_start(out=rg[:, par, :], in_=s_gate[ht]),
                           f"rg{par}", 16)

            part_max([vsml[:, 3:4], vsml[:, 5:6]], [8, 9], ("sum", cx.c("sum")))
            barrier(2, [4, 5], ("tpE", cx.c("tpE")))

            @block.vector
            def _(ve):
                cx.wait(ve, "cbb")
                ve.tensor_tensor(vsml[:, 6:7], scal[:, 4:5], scal[:, 2:3], ALU.mult)
                ve.tensor_scalar(scal[:, 6:7], vsml[:, 6:7], cbv(CB_QM),
                                 cbv(CB_E12), ALU.mult, ALU.max)
                ve.reciprocal(vsml[:, 7:8], scal[:, 6:7])
                ve.tensor_tensor(scal[:, 8:9], scal[:, 2:3], vsml[:, 7:8], ALU.mult)
                ve.tensor_tensor(vsml[:, 6:7], scal[:, 5:6], scal[:, 2:3], ALU.mult)
                ve.tensor_scalar(scal[:, 7:8], vsml[:, 6:7], cbv(CB_QM),
                                 cbv(CB_E12), ALU.mult, ALU.max)
                ve.reciprocal(vsml[:, 7:8], scal[:, 7:8])
                cx.inc(ve.tensor_tensor(scal[:, 9:10], scal[:, 2:3], vsml[:, 7:8],
                                        ALU.mult), "f1d2")

            @block.gpsimd
            def _(gp):
                cx.inc(gp.memset(vsml[:, 11:12], 0.0), "f1z2")

            f1a_base = cx.c("f1a")
            for ht in range(NH):
                par = ht % 2
                a_base = cx.c("f1a")

                if ht >= 2:
                    @block.sync
                    def _(sy, ht=ht, par=par, a_base=a_base):
                        cx.wait(sy, f"spu{par}")
                        cx.wait(sy, f"spg{par}")
                        cx.wait(sy, "f1a", a_base - 1)
                        cx.inc(sy.dma_start(out=ru[:, par, :], in_=s_up[ht]),
                               f"ru{par}", 16)
                        cx.inc(sy.dma_start(out=rg[:, par, :], in_=s_gate[ht]),
                               f"rg{par}", 16)

                @block.scalar
                def _(se, ht=ht, par=par, f1a_base=f1a_base):
                    cx.wait(se, f"rg{par}")
                    cx.wait(se, "f1d2", cx.c("f1d2"))
                    cx.wait(se, "f1a", f1a_base + ht - 1)
                    if ht < 2:
                        for s in ("spu0", "spu1", "spg0", "spg1"):
                            cx.wait(se, s)
                    cx.inc(se.activation(tsc[:, 2 * par, :], rg[:, par, :],
                                         ACT.Identity,
                                         bias=cbv(CB_MAGIC), scale=scal[:, 9:10]),
                           "f1p")

                @block.vector
                def _(ve, ht=ht, par=par):
                    cx.wait(ve, f"ru{par}")
                    cx.wait(ve, "f1p", cx.c("f1p"))
                    cx.wait(ve, "f1z2", cx.c("f1z2"))
                    if ht < 2:
                        for s in ("spu0", "spu1", "spg0", "spg1"):
                            cx.wait(ve, s)
                    if ht >= 2:
                        cx.wait(ve, f"siu{par}")
                        cx.wait(ve, f"sig{par}")
                    A = tsc[:, 2 * par, :]
                    B = tsc[:, 2 * par + 1, :]
                    ve.tensor_scalar(ru[:, par, :], ru[:, par, :], scal[:, 8:9],
                                     cbv(CB_MAGIC), ALU.mult, ALU.add)
                    ve.tensor_scalar(iub[:, par, :], ru[:, par, :], cbv(CB_MAGIC),
                                     None, ALU.subtract)
                    ve.tensor_scalar(B, A, cbv(CB_MAGIC), None, ALU.subtract)
                    cx.inc(ve.tensor_scalar(igb[:, par, :], A,
                                            cbv(CB_MAGIC), None, ALU.subtract),
                           "f1q1")

                @block.scalar
                def _(se, ht=ht, par=par):
                    cx.wait(se, "f1q1", cx.c("f1q1"))
                    cx.inc(se.activation(tsc[:, 2 * par, :], tsc[:, 2 * par + 1, :],
                                         ACT.Sigmoid,
                                         scale=scal[:, 7:8]), "f1sg")

                @block.vector
                def _(ve, ht=ht, par=par):
                    cx.wait(ve, "f1sg", cx.c("f1sg"))
                    A = tsc[:, 2 * par, :]
                    B = tsc[:, 2 * par + 1, :]
                    ve.tensor_scalar(A, A, cbv(CB_65536),
                                     cbv(CB_MAGIC), ALU.mult, ALU.add)
                    ve.tensor_scalar(A, A, cbv(CB_MAGIC),
                                     cbv(CB_65535), ALU.subtract, ALU.min)
                    ve.tensor_tensor(A, A, B, ALU.mult)
                    ve.tensor_reduce(vsml[:, 10:11], A, AX.X, ALU.max,
                                     apply_absolute_value=True)
                    cx.inc(ve.tensor_tensor(vsml[:, 11:12], vsml[:, 11:12],
                                            vsml[:, 10:11], ALU.max), "f1a")

                @block.sync
                def _(sy, ht=ht, par=par):
                    cx.wait(sy, "f1a")
                    cx.inc(sy.dma_start(out=s_ui[ht], in_=iub[:, par, :]),
                           f"siu{par}", 16)
                    cx.inc(sy.dma_start(out=s_gi[ht], in_=igb[:, par, :]),
                           f"sig{par}", 16)

            b2_base = cx.c("f1b")
            for ht in range(2):
                par = ht % 2

                @block.gpsimd
                def _(gp, ht=ht, par=par):
                    cx.wait(gp, f"siu{par}")
                    cx.wait(gp, f"sig{par}")
                    cx.inc(gp.dma_start(out=tsc[:, 2 * par, :], in_=s_ui[ht]),
                           f"riu{par}", 16)
                    cx.inc(gp.dma_start(out=tsc[:, 2 * par + 1, :], in_=s_gi[ht]),
                           f"rig{par}", 16)

            part_max([vsml[:, 11:12]], [8], ("f1a", cx.c("f1a")))
            barrier(1, [11], ("tpE", cx.c("tpE")))

            @block.vector
            def _(ve):
                cx.wait(ve, "cbb")
                ve.tensor_tensor(vsml[:, 6:7], scal[:, 11:12], scal[:, 7:8],
                                 ALU.mult)
                ve.tensor_scalar(vsml[:, 6:7], vsml[:, 6:7], cbv(CB_I65536), None,
                                 ALU.mult)
                ve.tensor_scalar(scal[:, 12:13], vsml[:, 6:7], cbv(CB_QM),
                                 cbv(CB_E12), ALU.mult, ALU.max)
                ve.reciprocal(vsml[:, 7:8], scal[:, 12:13])
                ve.tensor_tensor(vsml[:, 7:8], vsml[:, 7:8], scal[:, 7:8], ALU.mult)
                cx.inc(ve.tensor_scalar(scal[:, 13:14], vsml[:, 7:8],
                                        cbv(CB_I65536), None, ALU.mult), "f1d3")

            sfree(*[cm for cm, _ in cmB])
            cmC = [salloc(f"{tag}o32", [128, 2, T], I32),
                   salloc(f"{tag}sgm", [128, 2, T], F32)]
            o32b = cmC[0][1]
            sgm = cmC[1][1]

            @block.gpsimd
            def _(gp):
                cx.inc(gp.memset(vsml[:, 13:14], 0.0), "f1z3")

            for ht in range(NH):
                par = ht % 2

                if ht >= 2:
                    @block.gpsimd
                    def _(gp, ht=ht, par=par, b2_base=b2_base):
                        cx.wait(gp, f"siu{par}")
                        cx.wait(gp, f"sig{par}")
                        cx.wait(gp, "f1b", b2_base + ht - 1)
                        cx.inc(gp.dma_start(out=tsc[:, 2 * par, :], in_=s_ui[ht]),
                               f"riu{par}", 16)
                        cx.inc(gp.dma_start(out=tsc[:, 2 * par + 1, :],
                                            in_=s_gi[ht]),
                               f"rig{par}", 16)

                @block.vector
                def _(ve, ht=ht, par=par):
                    cx.wait(ve, f"riu{par}")
                    cx.wait(ve, f"rig{par}")
                    cx.wait(ve, "f1d3", cx.c("f1d3"))
                    cx.wait(ve, "f1z3", cx.c("f1z3"))
                    if ht >= 2:
                        cx.wait(ve, f"so{par}")
                    cx.inc(ve.nd.memset(vsml[:, 9:10], 0.0), "f1q2")

                @block.scalar
                def _(se, ht=ht, par=par, b2_base=b2_base):
                    cx.wait(se, "f1q2", cx.c("f1q2"))
                    cx.wait(se, "f1b", b2_base + ht - 1)
                    cx.inc(se.activation(sgm[:, par, :], tsc[:, 2 * par + 1, :],
                                         ACT.Sigmoid,
                                         scale=scal[:, 7:8]), "f1sh")

                @block.vector
                def _(ve, ht=ht, par=par):
                    cx.wait(ve, "f1sh", cx.c("f1sh"))
                    X = sgm[:, par, :]
                    ve.tensor_scalar(X, X, cbv(CB_65536),
                                     cbv(CB_MAGIC), ALU.mult, ALU.add)
                    ve.tensor_scalar(X, X, cbv(CB_MAGIC),
                                     cbv(CB_65535), ALU.subtract, ALU.min)
                    ve.tensor_tensor(X, X, tsc[:, 2 * par + 1, :], ALU.mult)
                    ve.tensor_scalar(X, X, scal[:, 13:14],
                                     cbv(CB_MAGIC), ALU.mult, ALU.add)
                    ve.tensor_scalar(X, X, cbv(CB_MAGIC), None, ALU.subtract)
                    ve.tensor_tensor(X, X, tsc[:, 2 * par, :], ALU.mult)
                    ve.tensor_reduce(vsml[:, 10:11], X, AX.X, ALU.max,
                                     apply_absolute_value=True)
                    ve.tensor_tensor(vsml[:, 13:14], vsml[:, 13:14],
                                     vsml[:, 10:11], ALU.max)
                    cx.inc(ve.tensor_copy(o32b[:, par, :], X), "f1b")

                @block.sync
                def _(sy, ht=ht, par=par):
                    cx.wait(sy, "f1b")
                    cx.inc(sy.dma_start(out=s_o32[ht], in_=o32b[:, par, :]),
                           f"so{par}", 16)

            sfree(*[cm for cm, _ in cmT], *[cm for cm, _ in cmC])
            cmD = [salloc(f"{tag}t3", [128, 2, 1024], F32),
                   salloc(f"{tag}hid", [128, NH, D], BF16),
                   salloc(f"{tag}wd", [128, 2, 4, 128], BF16)]
            t3b, HIDb, wd_sb = [t for _, t in cmD]
            f1h_base = cx.c("f1h")
            for ht in range(2):
                par = ht % 2

                @block.gpsimd
                def _(gp, ht=ht, par=par):
                    cx.wait(gp, f"so{par}")
                    cx.inc(gp.dma_start(
                        out=t3b[:, par, :],
                        in_=s_o32[ht, :, 0:1024]),
                        f"ro{par}", 16)

            part_max([vsml[:, 13:14]], [8], ("f1b", cx.c("f1b")))
            barrier(1, [14], ("tpE", cx.c("tpE")))

            @block.vector
            def _(ve):
                cx.wait(ve, "cbb")
                ve.tensor_tensor(vsml[:, 6:7], scal[:, 14:15], scal[:, 12:13],
                                 ALU.mult)
                ve.tensor_tensor(vsml[:, 6:7], vsml[:, 6:7], scal[:, 6:7], ALU.mult)
                ve.tensor_scalar(scal[:, 15:16], vsml[:, 6:7], cbv(CB_QM),
                                 cbv(CB_E12), ALU.mult, ALU.max)
                ve.reciprocal(vsml[:, 7:8], scal[:, 15:16])
                ve.tensor_tensor(vsml[:, 7:8], vsml[:, 7:8], scal[:, 12:13],
                                 ALU.mult)
                ve.tensor_tensor(scal[:, 16:17], vsml[:, 7:8], scal[:, 6:7],
                                 ALU.mult)
                cx.inc(ve.tensor_scalar(scal[:, 17:18], scal[:, 15:16],
                                        cbv(CB_HALF), None, ALU.mult), "f1d4")

            idx = 0
            for th in range(2):
                for ht in range(NH):
                    par = ht % 2

                    if idx >= 2:
                        @block.gpsimd
                        def _(gp, ht=ht, th=th, par=par, idx=idx,
                              f1h_base=f1h_base):
                            cx.wait(gp, f"so{par}")
                            cx.wait(gp, "f1h", f1h_base + idx - 1)
                            cx.inc(gp.dma_start(
                                out=t3b[:, par, :],
                                in_=s_o32[ht, :, th * 1024:(th + 1) * 1024]),
                                f"ro{par}", 16)

                    @block.vector
                    def _(ve, ht=ht, th=th, par=par):
                        cx.wait(ve, f"ro{par}")
                        if ht < 2 and th == 0:
                            cx.wait(ve, "so0")
                            cx.wait(ve, "so1")
                        cx.wait(ve, "f1d4", cx.c("f1d4"))
                        cx.wait(ve, "tdn")
                        ve.nd.tensor_scalar(t3b[:, par, :], t3b[:, par, :],
                                            scal[:, 16:17], cbv(CB_MAGIC),
                                            ALU.mult, ALU.add)
                        cx.inc(ve.tensor_scalar(HIDb[:, ht, :], t3b[:, par, :],
                                                cbv(CB_MAGIC), None, ALU.subtract),
                               "f1h")
                        cx.inc(ve.nd.memset(vsml[:, 15:16], 0.0), "vh")
                    idx += 1

                down_half(HIDb, s_w1d, wd_sb, scal[:, 17:18], th)
            sfree(*[cm for cm, _ in cmD])


        def l1_attn(tag):
            fence_entry()
            layernorm_stats()

            sums2 = stats[:, 0:NT]
            sumsq2 = stats[:, NT:2 * NT]
            for t in range(NT):
                @block.vector
                def _(ve, t=t):
                    cx.wait(ve, "lnr")
                    ve.scalar_tensor_tensor(
                        xn_f[:, t % 2, :], x_tm[:, t, :], MU(t),
                        RS(t).broadcast_to([128, D]), ALU.subtract, ALU.mult)
                    ve.reduce_sum(sums2[:, t:t + 1], xn_f[:, t % 2, :], AX.X)
                    cx.inc(ve.scalar_tensor_tensor(
                        xn_f[:, t % 2, :], xn_f[:, t % 2, :], 1.0,
                        xn_f[:, t % 2, :], ALU.mult, ALU.mult,
                        accum_out=sumsq2[:, t:t + 1]), "a5s")

            @block.vector
            def _(ve):
                cx.wait(ve, "a5s")
                mu2 = stats[:, 3 * NT:4 * NT]
                var2 = stats[:, 5 * NT:6 * NT]
                ve.tensor_scalar(mu2, sums2, cbv(CB_ID), None, ALU.mult)
                ve.tensor_scalar(var2, sumsq2, cbv(CB_ID), None, ALU.mult)
                ve.tensor_tensor(sums2, mu2, mu2, ALU.mult)
                ve.tensor_tensor(var2, var2, sums2, ALU.subtract)
                cx.inc(ve.tensor_scalar(var2, var2, cbv(CB_EPS), None, ALU.add),
                       "a5v")

            @block.scalar
            def _(se):
                cx.wait(se, "a5v")
                cx.inc(se.activation(stats[:, 0:NT], stats[:, 5 * NT:6 * NT],
                                     ACT.Sqrt), "a5q")

            @block.vector
            def _(ve):
                cx.wait(ve, "a5q")
                mu2 = stats[:, 3 * NT:4 * NT]
                var2 = stats[:, 5 * NT:6 * NT]
                r1 = stats[:, 4 * NT:5 * NT]
                r2 = stats[:, NT:2 * NT]
                t1 = stats[:, 0:NT]
                ve.reciprocal(r2, t1)
                ve.tensor_tensor(t1, r2, r2, ALU.mult)
                ve.tensor_tensor(t1, t1, var2, ALU.mult)
                ve.tensor_scalar(t1, t1, -0.5, 1.5, ALU.mult, ALU.add)
                ve.tensor_tensor(r2, r2, t1, ALU.mult)
                ve.tensor_tensor(t1, r1, r2, ALU.mult)
                ve.tensor_tensor(var2, stats[:, 2 * NT:3 * NT], t1, ALU.mult)
                ve.tensor_tensor(mu2, mu2, r2, ALU.mult)
                cx.inc(ve.tensor_tensor(mu2, mu2, var2, ALU.add), "a5r")

            def AP_A(t):
                return stats[:, t:t + 1]

            def AP_B(t):
                return stats[:, 3 * NT + t:3 * NT + t + 1]

            for t in range(NT):
                @block.vector
                def _(ve, t=t):
                    cx.wait(ve, "a5r")
                    cx.wait(ve, "tpT", cx.c("tpT") - 8)
                    cx.inc(ve.scalar_tensor_tensor(
                        xn_f[:, t % 2, :], x_tm[:, t, :], AP_A(t),
                        AP_B(t).broadcast_to([128, D]), ALU.mult,
                        ALU.subtract), "xn")

                def mk_evac(t, c):
                    def f(ve, bank):
                        return ve.tensor_scalar(
                            A1[:, c, t * 128:(t + 1) * 128], bank,
                            gbet[:, c:c + 1], gbet[:, 8 + c:8 + c + 1],
                            ALU.mult, ALU.add)
                    return f

                xn_target = cx.c("xn")

                def prew2(te, tgt=xn_target):
                    cx.wait(te, "xn", tgt)

                emit_transposes(
                    [(xn_f[:, t % 2, c * 128:(c + 1) * 128], mk_evac(t, c))
                     for c in range(8)], te_pre_wait=prew2)

            cmP = [salloc(f"{tag}q", [128, 8, T], BF16),
                   salloc(f"{tag}k", [128, 8, T], BF16),
                   salloc(f"{tag}v", [128, NT, 1040], BF16),
                   salloc(f"{tag}w", [128, 8, 128], BF16),
                   salloc(f"{tag}st", [128, 512], F32),
                   salloc(f"{tag}sq", [128, 512], BF16)]
            A2q, A3k, A4v, wslab, astage, sqb = [t for _, t in cmP]

            @block.gpsimd
            def _(gp):
                last = None
                for h in range(HEADS):
                    last = gp.memset(A4v[:, :, 65 * h + 64:65 * h + 65], 1.0)
                cx.inc(last, "a5o")

            xfE_all = cx.c("tpE")
            a5w_base = cx.c("a5w")
            a5mm_base = cx.c("a5mm")
            a5ev_base = cx.c("a5ev")

            for m, wsrc in ((0, d_w1q), (1, d_w1k), (2, d_w1v)):
                for oc in range(8):
                    gi = (cx.c("a5w") - a5w_base) // 16

                    @block.gpsimd
                    def _(gp, m=m, oc=oc, wsrc=wsrc, gi=gi):
                        cx.wait(gp, "a5mm", a5mm_base + 4 * gi)
                        cx.inc(gp.dma_start(out=wslab[:], in_=wsrc[:, oc]),
                               "a5w", 16)

                    for t5 in range(4):
                        mm_i = cx.c("a5mm")
                        mm_loc = mm_i - a5mm_base

                        @block.tensor
                        def _(te, m=m, oc=oc, t5=t5, gi=gi, mm_i=mm_i,
                              mm_loc=mm_loc):
                            if t5 == 0:
                                cx.wait(te, "a5w")
                                if gi == 0:
                                    cx.wait(te, "tpE", xfE_all)
                            cx.wait(te, "a5ev", a5ev_base + mm_loc - 1)
                            last = None
                            for k in range(8):
                                last = te.matmul(
                                    pbank[mm_i % 2][:], wslab[:, k, :],
                                    A1[:, k, t5 * 512:(t5 + 1) * 512],
                                    start=(k == 0), stop=(k == 7))
                            cx.inc(last, "a5mm")

                        @block.vector
                        def _(ve, m=m, oc=oc, t5=t5, mm_i=mm_i):
                            cx.wait(ve, "a5mm", mm_i + 1)
                            bank = pbank[mm_i % 2][:]
                            bias = bqkv[:, 8 * m + oc:8 * m + oc + 1]
                            if m == 0:
                                dst = A2q[:, oc, t5 * 512:(t5 + 1) * 512]
                            elif m == 1:
                                dst = A3k[:, oc, t5 * 512:(t5 + 1) * 512]
                            else:
                                dst = astage[:]
                                cx.wait(ve, "tpT", cx.c("tpT"))
                            cx.inc(ve.nd.tensor_scalar(dst, bank, bias, None,
                                                       ALU.add), "a5ev")

                        if m == 2:
                            ev_target = cx.c("a5ev")

                            def prew3(te, tgt=ev_target):
                                cx.wait(te, "a5ev", tgt)

                            def mk_vev(oc, t5, tt):
                                def f(ve, bank):
                                    jt = 4 * t5 + tt
                                    ve.tensor_copy(
                                        A4v[:, jt, 65 * 2 * oc:65 * 2 * oc + 64],
                                        bank[:, 0:64])
                                    return ve.tensor_copy(
                                        A4v[:, jt,
                                            65 * (2 * oc + 1):65 * (2 * oc + 1) + 64],
                                        bank[:, 64:128])
                                return f
                            emit_transposes(
                                [(astage[:, tt * 128:(tt + 1) * 128],
                                  mk_vev(oc, t5, tt)) for tt in range(4)],
                                te_pre_wait=prew3)

            for m, srcb in ((0, A2q), (1, A3k)):
                for oc in range(8):
                    for half in range(4):
                        @block.vector
                        def _(ve, m=m, oc=oc, half=half):
                            cx.wait(ve, "a5ev")
                            cx.wait(ve, "a5nm", cx.c("a5nm"))
                            src_ap = srcb[:, oc, half * 512:(half + 1) * 512]
                            cx.inc(ve.scalar_tensor_tensor(
                                sqb[:], src_ap, 1.0, src_ap,
                                ALU.mult, ALU.mult), "a5sq")

                        @block.tensor
                        def _(te, m=m, oc=oc, half=half):
                            cx.wait(te, "a5sq")
                            if oc == 0:
                                cx.wait(te, "a5nr", cx.c("a5nr"))
                            cx.inc(te.matmul(pbank[2 + half][0:16, :],
                                             hsel[:, oc, :], sqb[:],
                                             start=(oc == 0), stop=(oc == 7)),
                                   "a5nm")

                @block.vector
                def _(ve, m=m):
                    cx.wait(ve, "a5nm")
                    for half in range(4):
                        ve.reduce_max(vsml[0:16, 4 * m + half:4 * m + half + 1],
                                      pbank[2 + half][0:16, :], AX.X)
                    cx.inc(ve.tensor_reduce(vsml[0:16, 8 + m:9 + m],
                                            vsml[0:16, 4 * m:4 * m + 4], AX.X,
                                            ALU.max), "a5nr")

            @block.scalar
            def _(se):
                cx.wait(se, "a5nr")
                cx.inc(se.activation(vsml[0:16, 10:12], vsml[0:16, 8:10],
                                     ACT.Sqrt), "a5ns")

            @block.vector
            def _(ve):
                cx.wait(ve, "a5ns")
                ve.tensor_tensor(vsml[0:16, 12:13], vsml[0:16, 10:11],
                                 vsml[0:16, 11:12], ALU.mult)
                cx.inc(ve.tensor_scalar(vsml[0:16, 12:13], vsml[0:16, 12:13],
                                        -0.125 * 1.03, None, ALU.mult), "a5nb")

            tb2 = cx.c("tpT")

            @block.tensor
            def _(te):
                cx.wait(te, "a5nb", cx.c("a5nb"))
                cx.wait(te, "tpE", cx.c("tpE") - 1)
                cx.inc(te.transpose(pbank[6 + tb2 % 2][0:1, 0:16],
                                    vsml[0:16, 12:13], ident[0:16, 0:16]), "tpT")

            @block.vector
            def _(ve):
                cx.wait(ve, "tpT")
                cx.inc(ve.tensor_copy(vsml[0:1, 0:16], pbank[6 + tb2 % 2][0:1, 0:16]),
                       "tpE")

            @block.tensor
            def _(te):
                cx.wait(te, "tpE")
                cx.inc(te.matmul(pbank[2][:, 0:16], onesrow[:], vsml[0:1, 0:16],
                                 start=True, stop=True), "a5bb")

            @block.vector
            def _(ve):
                cx.wait(ve, "a5bb")
                cx.inc(ve.tensor_copy(scal[:, 0:16], pbank[2][:, 0:16]), "a5bc")

            ETbs = [A1[:, 0:4, :].rearrange("p a b -> p (a b)").rearrange(
                        "p (j i) -> p j i", i=512),
                    A1[:, 4:8, :].rearrange("p a b -> p (a b)").rearrange(
                        "p (j i) -> p j i", i=512)]
            for isl in range(4):
                for hp in range(8):
                    e_base = cx.c("a5e")
                    pv_prev = cx.c("a5pv")
                    sc_base = cx.c("a5sc")
                    for j in range(NT):
                        @block.tensor
                        def _(te, isl=isl, hp=hp, j=j, e_base=e_base):
                            if j == 0 and hp == 0 and isl == 0:
                                cx.wait(te, "a5bc")
                            p = j // 2
                            for hsub in range(2):
                                cx.wait(te, "a5e",
                                        e_base + 2 * (p - 1) + hsub + 1)
                                cx.inc(te.matmul(
                                    pall[:, 2 * hsub + j % 2, :],
                                    A3k[64 * hsub:64 * hsub + 64, hp,
                                        j * 128:(j + 1) * 128],
                                    A2q[64 * hsub:64 * hsub + 64, hp,
                                        isl * 512:(isl + 1) * 512],
                                    start=True, stop=True), "a5sc")

                    @block.scalar
                    def _(se, isl=isl, hp=hp, sc_base=sc_base, pv_prev=pv_prev):
                        cx.wait(se, "a5pv", pv_prev)
                        for p in range(8):
                            j = 2 * p + 1
                            cx.wait(se, "a5sc", sc_base + 4 * p + 4)
                            for hsub in range(2):
                                h = 2 * hp + hsub
                                cx.inc(se.nd.activation(
                                    ETbs[hsub][:, j - 1:j + 1, :],
                                    pall[:, 2 * hsub:2 * hsub + 2, :],
                                    ACT.Exp, bias=scal[:, h:h + 1],
                                    scale=cbv(CB_I8R)), "a5e")

                    for hsub in range(2):
                        h = 2 * hp + hsub
                        ETb = ETbs[hsub]

                        @block.tensor
                        def _(te, isl=isl, h=h, hsub=hsub, ETb=ETb):
                            cx.wait(te, "a5e")
                            cx.wait(te, "a5ve", cx.c("a5ve") - 1)
                            if isl == 0 and h < 2:
                                cx.wait(te, "a5bc", cx.c("a5bc"))
                                cx.wait(te, "a5nr", cx.c("a5nr"))
                            last = None
                            for j in range(NT):
                                last = te.matmul(
                                    pall[0:65, 4 + hsub, :],
                                    A4v[:, j, 65 * h:65 * h + 65],
                                    ETb[:, j, :], start=(j == 0),
                                    stop=(j == NT - 1))
                            cx.inc(last, "a5pv")

                        @block.vector
                        def _(ve, isl=isl, h=h, hsub=hsub):
                            cx.wait(ve, "a5pv")
                            cx.wait(ve, "tpT", cx.c("tpT"))
                            cx.inc(ve.nd.tensor_copy(astage[0:65, :],
                                                     pall[0:65, 4 + hsub, :]),
                                   "a5ve")

                        ve_target = cx.c("a5ve")

                        def prew5(te, tgt=ve_target):
                            cx.wait(te, "a5ve", tgt)

                        def mk_pvev(isl, h, tt):
                            def f(ve, bank):
                                tti = isl * 4 + tt
                                ve.reciprocal(vsml[:, 15:16], bank[:, 64:65])
                                return ve.scalar_tensor_tensor(
                                    x_tm[:, tti, 64 * h:64 * h + 64],
                                    bank[:, 0:64], vsml[:, 15:16],
                                    x_tm[:, tti, 64 * h:64 * h + 64],
                                    ALU.mult, ALU.add)
                            return f
                        emit_transposes(
                            [(astage[0:65, tt * 128:(tt + 1) * 128],
                              mk_pvev(isl, h, tt)) for tt in range(4)],
                            te_pre_wait=prew5)

            @block.vector
            def _(ve):
                cx.inc(ve.memset(vsml[:, 15:16], 0.0), "vres")

            sfree(*[cm for cm, _ in cmP])

        def dump_x(tag):
            if tag not in dbg_outs:
                return

            @block.sync
            def _(sy):
                cx.wait(sy, "vres")
                cx.inc(sy.dma_start(
                    out=dbg_outs[tag].rearrange("(t p) d -> p t d", p=128),
                    in_=x_tm[:]), "dout", 16)

        with nc.sbuf_tensor("HIDb", [128, NH, D], BF16) as HIDb, \
             nc.sbuf_tensor("wgsb", [128, 2, 8, 128], BF16) as wg_sb, \
             nc.sbuf_tensor("wusb", [128, 2, 8, 128], BF16) as wu_sb, \
             nc.sbuf_tensor("wdsb", [128, 2, 4, 128], BF16) as wd_sb, \
             nc.sbuf_tensor("sil", [128, 4, 512], F32) as sil:
            l0_ffn(HIDb, wg_sb, wu_sb, wd_sb, sil)
        dump_x("p1")

        with nc.sbuf_tensor("A2", [128, NT, D], BF16) as A2, \
             nc.sbuf_tensor("lrb", [128, 4, T], BF16) as lrb, \
             nc.sbuf_tensor("clb", [128, 8, K0], BF16) as clb, \
             nc.sbuf_tensor("crb", [128, 8, K0], BF16) as crb, \
             nc.sbuf_tensor("sql", [128, 2, 512], BF16) as sql, \
             nc.sbuf_tensor("lrow", [1, 2 * T], F32) as lrow, \
             nc.sbuf_tensor("rl0", [128, 24], F32) as rl0:
            l0_attn(A2, lrb, clb, crb, sql, lrow, rl0)
        dump_x("p2")

        with nc.sbuf_tensor("HIDb3", [128, NH, D], BF16) as HIDb, \
             nc.sbuf_tensor("wgsb3", [128, 2, 8, 128], BF16) as wg_sb, \
             nc.sbuf_tensor("wusb3", [128, 2, 8, 128], BF16) as wu_sb, \
             nc.sbuf_tensor("wdsb3", [128, 2, 4, 128], BF16) as wd_sb, \
             nc.sbuf_tensor("sil3", [128, 4, 512], F32) as sil:
            l0_ffn(HIDb, wg_sb, wu_sb, wd_sb, sil)
        dump_x("p3")

        l1_ffn("f4")
        dump_x("p4")

        l1_attn("a5")
        dump_x("p5")

        l1_ffn("f6")
        dump_x("p6")

        @block.sync
        def _(sy):
            cx.wait(sy, "vres")
            cx.wait(sy, "ve")
            cx.inc(sy.dma_start(
                out=d_out.rearrange("(t p) d -> p t d", p=128), in_=x_tm[:]),
                "dout", 16)
            cx.wait(sy, "dout")

    return nc


def _tile_lhsT(w_T):
    IN, OUT = w_T.shape
    return np.ascontiguousarray(
        w_T.reshape(IN // 128, 128, OUT // 128, 128).transpose(1, 2, 0, 3))


def _tile_sdmaj(w_T):
    H_, D_ = w_T.shape
    return np.ascontiguousarray(
        w_T.reshape(H_ // 128, 128, D_ // 128, 128).transpose(1, 2, 0, 3))


def kernel(L0_attn_x, L0_attn_compare_left, L0_attn_compare_right,
           L0_attn_threshold, L0_attn_gate_bias,
           L0_ffn_w_gate, L0_ffn_w_up, L0_ffn_w_down,
           L1_attn_ln_g, L1_attn_ln_b, L1_attn_Wq, L1_attn_bq,
           L1_attn_Wk, L1_attn_bk, L1_attn_Wv, L1_attn_bv,
           L1_ffn_w_gate, L1_ffn_w_up, L1_ffn_w_down,
           _debug_taps=(), _nc=None):
    x = np.asarray(L0_attn_x, dtype=np.float32)
    n = 8

    w0g = _tile_lhsT(np.ascontiguousarray(np.asarray(L0_ffn_w_gate).T))
    w0u = _tile_lhsT(np.ascontiguousarray(np.asarray(L0_ffn_w_up).T))
    w0d = _tile_sdmaj(np.ascontiguousarray(np.asarray(L0_ffn_w_down).T))
    cl = np.ascontiguousarray(np.asarray(L0_attn_compare_left, dtype=np.float32)
                              .reshape(8, 128, K0).transpose(1, 0, 2))
    cr = np.ascontiguousarray(np.asarray(L0_attn_compare_right, dtype=np.float32)
                              .reshape(8, 128, K0).transpose(1, 0, 2))
    w1q = _tile_lhsT(np.ascontiguousarray(np.asarray(L1_attn_Wq, dtype=np.float32)))
    w1k = _tile_lhsT(np.ascontiguousarray(np.asarray(L1_attn_Wk, dtype=np.float32)))
    w1v = _tile_lhsT(np.ascontiguousarray(np.asarray(L1_attn_Wv, dtype=np.float32)))

    def chunks(v):
        return np.ascontiguousarray(np.asarray(v, dtype=np.float32).reshape(8, 128).T)

    common = dict(
        w0g=w0g, w0u=w0u, w0d=w0d, cl=cl, cr=cr,
        thr=np.asarray(L0_attn_threshold, np.float32).reshape(1),
        gb=np.asarray(L0_attn_gate_bias, np.float32).reshape(1),
        w1q=w1q, w1k=w1k, w1v=w1v,
        bq=chunks(L1_attn_bq), bk=chunks(L1_attn_bk), bv=chunks(L1_attn_bv),
        l1g=chunks(L1_attn_ln_g), l1b=chunks(L1_attn_ln_b),
        w1fg=np.asarray(L1_ffn_w_gate, np.float32),
        w1fu=np.asarray(L1_ffn_w_up, np.float32),
        w1fd=np.asarray(L1_ffn_w_down, np.float32),
    )
    in_maps = [dict(common, x=np.ascontiguousarray(x[b])) for b in range(n)]

    nc = _nc if _nc is not None else build_nc(debug_taps=_debug_taps)
    res = run_bass_kernel_spmd(nc, in_maps, core_ids=list(range(n)))
    out = np.stack([res.results[b]["out"] for b in range(n)], axis=0)
    if _debug_taps:
        dbg = {t: np.stack([res.results[b][f"dbg_{t}"] for b in range(n)])
               for t in _debug_taps}
        return out, dbg
    return out
